# version 13
# baseline (speedup 1.0000x reference)
import contextlib
import numpy as np
import concourse.bass as bass
import concourse.mybir as mybir
from concourse.bass_utils import run_bass_kernel_spmd

F32 = mybir.dt.float32
BF16 = mybir.dt.bfloat16
AF = mybir.ActivationFunctionType
ALU = mybir.AluOpType
AX = mybir.AxisListType

D = 1024
DK = 8
NST = 256
KH = 128
H = 2048
HC = 256
NHC = H // HC
HS = HC // 128
NE = 8
LN_EPS = 1e-5


class Sem:
    def __init__(self, h):
        self.h = h
        self.n = 0


class Res:
    def __init__(self):
        self.cons = []

    def ready(self, sem, val=None):
        self.cons = [(sem, sem.n if val is None else val)]

    def also(self, sem, val=None):
        self.cons.append((sem, sem.n if val is None else val))


class Emit:
    def __init__(self, nc):
        self.nc = nc
        self.ops = {k: [] for k in 'tvsgy'}

    def op(self, eng, method, *args, inc=None, k=1, **kw):
        semh = inc.h if inc is not None else None
        if inc is not None:
            inc.n += k

        def f(e, m=method, a=args, kw=kw, s=semh, k=k):
            r = getattr(e, m)(*a, **kw)
            if s is not None:
                r.then_inc(s, k)
        self.ops[eng].append(f)
        return inc.n if inc is not None else None

    def wait(self, eng, sem, val=None):
        v = sem.n if val is None else val
        self.ops[eng].append(lambda e, s=sem.h, v=v: e.wait_ge(s, v))

    def wait_res(self, eng, res):
        for sem, val in res.cons:
            self.wait(eng, sem, val)

    def drain(self, eng):
        self.ops[eng].append(lambda e: e.drain())

    def run_block(self):
        nc = self.nc
        ops = self.ops
        with nc.Block() as blk:
            @blk.tensor
            def _(t):
                for f in ops['t']:
                    f(t)

            @blk.vector
            def _(v):
                for f in ops['v']:
                    f(v)

            @blk.scalar
            def _(s):
                for f in ops['s']:
                    f(s)

            @blk.gpsimd
            def _(g):
                for f in ops['g']:
                    f(g)

            @blk.sync
            def _(y):
                for f in ops['y']:
                    f(y)
        self.ops = {k: [] for k in 'tvsgy'}


def build_program(TLOC=4096, phases=3, world=8):
    TGRP = min(2048, TLOC)
    NGRP = TLOC // TGRP
    NTT = TGRP // 512
    NTM = TGRP // 128
    NTILE = TLOC // 128
    CH = min(512, TLOC)
    NCHK = TLOC // CH
    PAD = CH // 2
    NLEV = CH.bit_length() - 1

    nc = bass.Bass("TRN2", target_bir_lowering=False)

    x_in = nc.dram_tensor("x", [TLOC, D], F32, kind="ExternalInput").ap()
    y_out = nc.dram_tensor("y", [TLOC, D], F32, kind="ExternalOutput").ap()
    d_sel = nc.dram_tensor("sel", [world, 1], F32, kind="ExternalInput").ap()

    d_rtW = nc.dram_tensor("rtW", [D, NE], F32, kind="ExternalInput").ap()
    d_sgT = nc.dram_tensor("sgT", [D, H], F32, kind="ExternalInput").ap()
    d_suT = nc.dram_tensor("suT", [D, H], F32, kind="ExternalInput").ap()
    d_sdT = nc.dram_tensor("sdT", [H, D], F32, kind="ExternalInput").ap()
    d_egT = nc.dram_tensor("egT", [NE, D, H], F32, kind="ExternalInput").ap()
    d_euT = nc.dram_tensor("euT", [NE, D, H], F32, kind="ExternalInput").ap()
    d_edT = nc.dram_tensor("edT", [NE, H, D], F32, kind="ExternalInput").ap()

    d_WaT = nc.dram_tensor("WaT", [D, NST], F32, kind="ExternalInput").ap()
    d_WBT = nc.dram_tensor("WBT", [D, NST], F32, kind="ExternalInput").ap()
    d_WCT = nc.dram_tensor("WCT", [NST, D], F32, kind="ExternalInput").ap()
    d_Wab = nc.dram_tensor("Wab", [128, 2], F32, kind="ExternalInput").ap()
    d_WBb = nc.dram_tensor("WBb", [128, 2], F32, kind="ExternalInput").ap()
    d_lgam = nc.dram_tensor("lgam", [1, 1], F32, kind="ExternalInput").ap()
    d_Ctab = nc.dram_tensor("Ctab", [NST, KH], F32, kind="ExternalInput").ap()
    d_Stab = nc.dram_tensor("Stab", [NST, KH], F32, kind="ExternalInput").ap()
    d_c128 = nc.dram_tensor("c128", [128, 2], F32, kind="ExternalInput").ap()
    d_ident = nc.dram_tensor("ident", [128, 128], F32, kind="ExternalInput").ap()
    d_lnw = nc.dram_tensor("lnw_rep", [128, D], F32, kind="ExternalInput").ap()
    d_lnb = nc.dram_tensor("lnb_rep", [128, D], F32, kind="ExternalInput").ap()
    d_WCb = nc.dram_tensor("WCb_rep", [128, D], F32, kind="ExternalInput").ap()
    d_Dsk = nc.dram_tensor("Dsk_rep", [128, D], F32, kind="ExternalInput").ap()

    x1s = nc.dram_tensor("x1s", [TLOC, D], F32).ap()
    mrow_scr = nc.dram_tensor("mrow_scr", [1, D], BF16).ap()
    x2s = nc.dram_tensor("x2s", [TLOC, D], F32).ap()
    cc_in = nc.dram_tensor("cc_in", [1, 512], F32)
    cc_out = nc.dram_tensor("cc_out", [world, 512], F32)

    em = Emit(nc)
    stk = contextlib.ExitStack()

    def sb(name, shape, dt):
        return stk.enter_context(nc.sbuf_tensor(name, shape, dt))

    def sem(name):
        return Sem(stk.enter_context(nc.semaphore(name)))

    with stk:
        s_mm = sem("s_mm")
        s_act = sem("s_act")
        s_dve = sem("s_dve")
        s_gp = sem("s_gp")
        s_cc = sem("s_cc")
        s_ld0 = sem("s_ld0")
        s_wl0 = sem("s_wl0")
        lx = [sem(f"lx{j}") for j in range(2)]
        so = [sem(f"so{j}") for j in range(2)]
        wl = [sem(f"wl{j}") for j in range(3)]
        wl2 = [sem(f"wl2{j}") for j in range(2)]
        l1 = [sem(f"l1{j}") for j in range(2)]

        c_identb = sb("c_identb", [128, 128], BF16)
        c_identf = sb("c_identf", [128, 128], F32)
        c_rtW = sb("c_rtW", [128, DK, NE], BF16)
        c_WCT = sb("c_WCT", [128, 2, D], BF16)
        c_C = sb("c_C", [128, 2, KH], BF16)
        c_S = sb("c_S", [128, 2, KH], BF16)
        c_c128 = sb("c_c128", [128, 2], BF16)
        c_Wab = sb("c_Wab", [128, 2], F32)
        c_WBb = sb("c_WBb", [128, 2], BF16)
        c_lnw = sb("c_lnw", [128, D], F32)
        c_lnb = sb("c_lnb", [128, D], F32)
        c_WCb = sb("c_WCb", [128, D], F32)
        c_Dsk = sb("c_Dsk", [128, D], F32)
        c_gamma = sb("c_gamma", [128, 1], F32)
        c_ones = sb("c_ones", [1, 128], F32)
        c_lgam = sb("c_lgam", [1, 1], F32)
        c_sel = sb("c_sel", [world, 1], F32)
        M2reT = sb("M2reT", [128, D], BF16)
        M2imT = sb("M2imT", [128, D], BF16)
        fac = sb("fac", [128, 1], F32)
        fbre = sb("fbre", [128, 1], F32)
        fbim = sb("fbim", [128, 1], F32)
        fb128 = sb("fb128", [1, 1], F32)
        c_eps = sb("c_eps", [128, 1], F32)
        c_two = sb("c_two", [128, 1], F32)

        op, wait, drain = em.op, em.wait, em.drain

        for dst, src in ((c_identf, d_ident), (c_lnw, d_lnw), (c_lnb, d_lnb),
                         (c_WCb, d_WCb), (c_Dsk, d_Dsk)):
            op('y', 'dma_start', out=dst[:, :], in_=src[:, :], inc=s_ld0, k=16)
        op('y', 'dma_start', out=c_lgam[:, :], in_=d_lgam[:, :], inc=s_ld0, k=16)
        op('y', 'dma_start', out=c_sel[:, :], in_=d_sel[:, :], inc=s_ld0, k=16)
        op('y', 'dma_start', out=c_Wab[:, :], in_=d_Wab[:, :], inc=s_ld0, k=16)
        op('g', 'dma_start', out=c_identb[:, :], in_=d_ident[:, :], inc=s_wl0, k=16)
        op('g', 'dma_start', out=c_rtW[:, :, :],
           in_=d_rtW.rearrange("(c p) e -> p c e", p=128), inc=s_wl0, k=16)
        op('g', 'dma_start', out=c_WCT[:, :, :],
           in_=d_WCT.rearrange("(g p) d -> p g d", p=128), inc=s_wl0, k=16)
        op('g', 'dma_start', out=c_C[:, :, :],
           in_=d_Ctab.rearrange("(g p) k -> p g k", p=128), inc=s_wl0, k=16)
        op('g', 'dma_start', out=c_S[:, :, :],
           in_=d_Stab.rearrange("(g p) k -> p g k", p=128), inc=s_wl0, k=16)
        op('g', 'dma_start', out=c_c128[:, :], in_=d_c128[:, :], inc=s_wl0, k=16)
        op('g', 'dma_start', out=c_WBb[:, :], in_=d_WBb[:, :], inc=s_wl0, k=16)
        ld_all = s_ld0.n
        wl_all = s_wl0.n

        wait('s', s_ld0, ld_all)
        op('s', 'activation', c_lgam[:, :], c_lgam[:, :], AF.Sigmoid, inc=s_act)
        op('v', 'memset', c_ones[:, :], 1.0)
        op('v', 'memset', fac[:, :], 2.0)
        op('v', 'memset', c_eps[:, :], LN_EPS)
        op('v', 'memset', c_two[:, :], 2.0)
        drain('v')
        op('v', 'memset', fac[0:1, :], 1.0, inc=s_dve)
        drain('v')

        with nc.psum_tensor("ip0", [128, 512], F32) as ip0, \
             nc.psum_tensor("ip1", [1, 512], F32) as ip1, \
             nc.sbuf_tensor("MrowT", [1, D], BF16) as Mrow:
            wait('t', s_act, s_act.n)
            wait('t', s_dve, s_dve.n)
            wait('t', s_wl0, wl_all)
            wait('t', s_ld0, ld_all)
            op('t', 'matmul', ip0[:, 0:1], c_ones[:, :], c_lgam[:, :], start=True, stop=True,
               inc=s_mm)
            wait('v', s_mm, s_mm.n)
            op('v', 'tensor_copy', c_gamma[:, :], ip0[:, 0:1], inc=s_dve)
            wait('t', s_dve, s_dve.n)
            for (tabs, dstb, sc) in ((c_C, M2reT, 1.0 / NST), (c_S, M2imT, -1.0 / NST)):
                for dt_ in range(2):
                    op('t', 'matmul', ip0[:, :], tabs[:, 0, :], c_WCT[:, 0, dt_ * 512:(dt_ + 1) * 512],
                       start=True, stop=False)
                    op('t', 'matmul', ip0[:, :], tabs[:, 1, :], c_WCT[:, 1, dt_ * 512:(dt_ + 1) * 512],
                       start=False, stop=True, inc=s_mm)
                    wait('v', s_mm, s_mm.n)
                    op('v', 'tensor_scalar', dstb[:, dt_ * 512:(dt_ + 1) * 512], ip0[:, :],
                       fac[:, 0:1], sc, ALU.mult, ALU.mult, inc=s_dve)
                    wait('t', s_dve, s_dve.n)
            for dt_ in range(2):
                op('t', 'matmul', ip1[:, :], c_c128[:, 0:1], c_WCT[:, 0, dt_ * 512:(dt_ + 1) * 512],
                   start=True, stop=False)
                op('t', 'matmul', ip1[:, :], c_c128[:, 1:2], c_WCT[:, 1, dt_ * 512:(dt_ + 1) * 512],
                   start=False, stop=True, inc=s_mm)
                wait('v', s_mm, s_mm.n)
                op('v', 'tensor_scalar', Mrow[0:1, dt_ * 512:(dt_ + 1) * 512], ip1[:, :],
                   1.0 / NST, None, ALU.mult, inc=s_dve)
                wait('t', s_dve, s_dve.n)
            wait('y', s_dve, s_dve.n)
            op('y', 'dma_start', out=mrow_scr[:, :], in_=Mrow[0:1, :], inc=s_ld0, k=16)
            wait('y', s_ld0, s_ld0.n)
            op('t', 'matmul', ip0[:, 0:1], c_C[:, 0, :], c_WBb[:, 0:1], start=True, stop=False)
            op('t', 'matmul', ip0[:, 0:1], c_C[:, 1, :], c_WBb[:, 1:2], start=False, stop=True, inc=s_mm)
            wait('v', s_mm, s_mm.n)
            op('v', 'tensor_copy', fbre[:, :], ip0[:, 0:1], inc=s_dve)
            wait('t', s_dve, s_dve.n)
            op('t', 'matmul', ip0[:, 0:1], c_S[:, 0, :], c_WBb[:, 0:1], start=True, stop=False)
            op('t', 'matmul', ip0[:, 0:1], c_S[:, 1, :], c_WBb[:, 1:2], start=False, stop=True, inc=s_mm)
            wait('v', s_mm, s_mm.n)
            op('v', 'tensor_scalar', fbim[:, :], ip0[:, 0:1], -1.0, None, ALU.mult, inc=s_dve)
            wait('t', s_dve, s_dve.n)
            op('t', 'matmul', ip1[:, 0:1], c_c128[:, 0:1], c_WBb[:, 0:1], start=True, stop=False)
            op('t', 'matmul', ip1[:, 0:1], c_c128[:, 1:2], c_WBb[:, 1:2], start=False, stop=True, inc=s_mm)
            wait('v', s_mm, s_mm.n)
            op('v', 'tensor_copy', fb128[:, :], ip1[0:1, 0:1], inc=s_dve)
            em.run_block()


        def emit_ffn(src, dst, tag, src_events=None, entry=()):
            TGRP = 1024
            NGRP = TLOC // TGRP
            NTM = TGRP // 128
            NTT = TGRP // 512
            NSC = 4
            NHS = 8
            units = [(None, 0), (None, 1)] + [(e, h) for e in range(NE) for h in range(2)]
            NU = len(units)
            store_events = [None] * (TLOC // 128)
            with contextlib.ExitStack() as fs:
                def fsb(name, shape, dt):
                    return fs.enter_context(nc.sbuf_tensor(name, shape, dt))
                xn_fm = fsb("xn_fm" + tag, [128, 2, DK, TGRP], BF16)
                actb = [fsb(f"actb{j}{tag}", [128, NHS, TGRP], BF16) for j in range(2)]
                wg = [fsb(f"wg{j}{tag}", [128, DK, 256], BF16) for j in range(3)]
                wu = [fsb(f"wu{j}{tag}", [128, DK, 256], BF16) for j in range(3)]
                wd = [fsb(f"wd{j}{tag}", [128, NHS, D], BF16) for j in range(2)]
                acc = fsb("acc" + tag, [128, NTM, D], F32)
                xring = fsb("xring" + tag, [128, 2, D], F32)
                oring = fsb("oring" + tag, [128, 2, D], F32)
                xnt = fsb("xnt" + tag, [128, 2, D], BF16)
                silb = [fsb(f"silb{j}{tag}", [128, 512], F32) for j in range(2)]
                cwb = fsb("cwb" + tag, [128, 2, NTM * 8], F32)
                bn6 = fsb("bn6" + tag, [128, 12], F32)
                bn2 = fsb("bn2" + tag, [128, 2], F32)
                sdv = fsb("sdv" + tag, [128, 1], F32)
                rstd = fsb("rstd" + tag, [128, 1], F32)
                lg8 = fsb("lg8" + tag, [128, 8], F32)
                m1 = fsb("m1" + tag, [128, 1], F32)
                nm1 = fsb("nm1" + tag, [128, 1], F32)
                m2 = fsb("m2" + tag, [128, 1], F32)
                dm = fsb("dm" + tag, [128, 1], F32)
                eqm = fsb("eqm" + tag, [128, 8], F32)
                lmb = fsb("lmb" + tag, [128, 8], F32)
                ex8 = fsb("ex8" + tag, [128, 8], F32)
                ed1 = fsb("ed1" + tag, [128, 1], F32)
                den = fsb("den" + tag, [128, 1], F32)
                rden = fsb("rden" + tag, [128, 1], F32)
                cwt = fsb("cwt" + tag, [128, 8], F32)

                with nc.psum_tensor("pGU" + tag, [128, 4, 512], F32) as pGU, \
                     nc.psum_tensor("pDD" + tag, [128, 2, 512], F32) as pD, \
                     nc.psum_tensor("pTR" + tag, [128, DK, 128], BF16) as pTR, \
                     nc.psum_tensor("pRT" + tag, [128, 512], F32) as pRT:

                    rGU = [Res(), Res()]
                    rD = [Res(), Res()]
                    rACT = [Res(), Res()]
                    rSIL = [Res(), Res()]
                    rW = [Res(), Res(), Res()]
                    rWD = [Res(), Res()]
                    rX = [Res(), Res()]
                    rO = [Res(), Res()]
                    rXN = [Res(), Res()]
                    rTR = Res()
                    rRT = Res()

                    for eng in 'tvsgy':
                        for bsem, bval in entry:
                            wait(eng, bsem, bval)

                    ctr = {'x': 0, 'o': 0, 'xn': 0, 'gu': 0, 'dsl': 0}
                    side1 = []
                    side5 = []

                    def pop_side(q):
                        if q:
                            q.pop(0)()

                    def flush(q):
                        while q:
                            q.pop(0)()

                    def load_src(gt):
                        xs = ctr['x'] % 2
                        ctr['x'] += 1
                        em.wait_res('y', rX[xs])
                        if src_events is not None:
                            gsem, gval = src_events[gt]
                            wait('y', gsem, gval)
                        op('y', 'dma_start', out=xring[:, xs, :],
                           in_=src[gt * 128:(gt + 1) * 128, :], inc=lx[xs], k=16)
                        return xs, lx[xs], lx[xs].n

                    def stage1_tile(g, i):
                        def f():
                            gt = g * NTM + i
                            xs, lsem, lval = load_src(gt)
                            wait('v', lsem, lval)
                            drain('v')
                            op('v', 'bn_stats', bn6[:, 0:6], xring[:, xs, 0:512])
                            op('v', 'bn_stats', bn6[:, 6:12], xring[:, xs, 512:1024])
                            drain('v')
                            op('v', 'bn_aggr', bn2[:, :], bn6[:, :], inc=s_dve)
                            wait('s', s_dve, s_dve.n)
                            op('s', 'activation', sdv[:, :], bn2[:, 1:2], AF.Sqrt,
                               bias=c_eps[:, 0:1], inc=s_act)
                            wait('v', s_act, s_act.n)
                            op('v', 'reciprocal', rstd[:, :], sdv[:, :])
                            drain('v')
                            ns = ctr['xn'] % 2
                            ctr['xn'] += 1
                            em.wait_res('v', rXN[ns])
                            op('v', 'tensor_scalar', xnt[:, ns, :], xring[:, xs, :],
                               bn2[:, 0:1], rstd[:, 0:1], ALU.subtract, ALU.mult, inc=s_dve)
                            rX[xs].ready(s_dve)
                            xn_ready = s_dve.n
                            wait('t', s_dve, xn_ready)
                            em.wait_res('t', rTR)
                            for dk in range(DK):
                                op('t', 'transpose', pTR[:, dk, :],
                                   xnt[:, ns, dk * 128:(dk + 1) * 128], c_identb[:, :],
                                   inc=(s_mm if dk == DK - 1 else None))
                            wait('v', s_mm, s_mm.n)
                            op('v', 'tensor_copy', xn_fm[:, g % 2, :, i * 128:(i + 1) * 128],
                               pTR[:, :, :], inc=s_dve)
                            rTR.ready(s_dve)
                            rXN[ns].ready(s_dve)
                            xnfm_tile_done = s_dve.n
                            em.wait_res('t', rRT)
                            wait('t', s_dve, xnfm_tile_done)
                            for dk in range(DK):
                                op('t', 'matmul', pRT[:, 0:8],
                                   xn_fm[:, g % 2, dk, i * 128:(i + 1) * 128], c_rtW[:, dk, :],
                                   start=(dk == 0), stop=(dk == DK - 1),
                                   inc=(s_mm if dk == DK - 1 else None))
                            wait('v', s_mm, s_mm.n)
                            drain('v')
                            op('v', 'tensor_copy', lg8[:, :], pRT[:, 0:8], inc=s_dve)
                            rRT.ready(s_dve)
                            drain('v')
                            op('v', 'tensor_reduce', m1[:, :], lg8[:, :], AX.X, ALU.max)
                            drain('v')
                            op('v', 'tensor_scalar', eqm[:, :], lg8[:, :], m1[:, 0:1], None, ALU.is_ge)
                            op('v', 'tensor_scalar', nm1[:, :], m1[:, :], -1.0, None, ALU.mult)
                            drain('v')
                            op('v', 'scalar_tensor_tensor', lmb[:, :], eqm[:, :], -1e30, lg8[:, :],
                               ALU.mult, ALU.add)
                            drain('v')
                            op('v', 'tensor_reduce', m2[:, :], lmb[:, :], AX.X, ALU.max)
                            drain('v')
                            op('v', 'tensor_tensor', dm[:, :], m2[:, :], m1[:, :], ALU.subtract,
                               inc=s_dve)
                            wait('s', s_dve, s_dve.n)
                            op('s', 'activation', ex8[:, :], lg8[:, :], AF.Exp,
                               bias=nm1[:, 0:1], scale=1.0)
                            op('s', 'activation', ed1[:, :], dm[:, :], AF.Exp, inc=s_act)
                            wait('v', s_act, s_act.n)
                            op('v', 'tensor_scalar', den[:, :], ed1[:, :], 1.0, None, ALU.add)
                            drain('v')
                            op('v', 'reciprocal', rden[:, :], den[:, :])
                            op('v', 'tensor_scalar', eqm[:, :], lg8[:, :], m2[:, 0:1], None, ALU.is_ge)
                            drain('v')
                            op('v', 'tensor_tensor', cwt[:, :], ex8[:, :], eqm[:, :], ALU.mult)
                            drain('v')
                            op('v', 'tensor_scalar', cwb[:, g % 2, i * 8:(i + 1) * 8], cwt[:, :],
                               rden[:, 0:1], 0.5, ALU.mult, ALU.mult, inc=s_dve)
                        return f

                    def stage5_tile(g, tm):
                        def f():
                            gt = g * NTM + tm
                            xs, lsem, lval = load_src(gt)
                            os_ = ctr['o'] % 2
                            ctr['o'] += 1
                            wait('v', lsem, lval)
                            em.wait_res('v', rO[os_])
                            drain('v')
                            op('v', 'tensor_tensor', oring[:, os_, :], xring[:, xs, :],
                               acc[:, tm, :], ALU.add, inc=s_dve)
                            rX[xs].ready(s_dve)
                            wait('y', s_dve, s_dve.n)
                            op('y', 'dma_start', out=dst[gt * 128:(gt + 1) * 128, :],
                               in_=oring[:, os_, :], inc=so[os_], k=16)
                            store_events[gt] = (so[os_], so[os_].n)
                            rO[os_].ready(so[os_])
                        return f

                    guw_ev = {}
                    wd_ev = {}
                    GSEQ_MAX = NGRP * NU * NSC
                    USEQ_MAX = NGRP * NU

                    def sched_guw(seq):
                        if seq >= GSEQ_MAX:
                            return
                        u = (seq // NSC) % NU
                        sc = seq % NSC
                        slot = seq % 3
                        e, half = units[u]
                        h0 = half * 1024 + sc * 256
                        if e is None:
                            gsrc = d_sgT.rearrange("(c p) h -> p c h", p=128)[:, :, h0:h0 + 256]
                            usrc = d_suT.rearrange("(c p) h -> p c h", p=128)[:, :, h0:h0 + 256]
                        else:
                            gsrc = d_egT[e].rearrange("(c p) h -> p c h", p=128)[:, :, h0:h0 + 256]
                            usrc = d_euT[e].rearrange("(c p) h -> p c h", p=128)[:, :, h0:h0 + 256]
                        em.wait_res('g', rW[slot])
                        op('g', 'dma_start', out=wg[slot][:, :, :], in_=gsrc, inc=wl[slot], k=16)
                        op('g', 'dma_start', out=wu[slot][:, :, :], in_=usrc, inc=wl[slot], k=16)
                        guw_ev[seq] = (slot, wl[slot], wl[slot].n)

                    def sched_wd(useq):
                        if useq >= USEQ_MAX:
                            return
                        u = useq % NU
                        slot = useq % 2
                        e, half = units[u]
                        if e is None:
                            dsrc = d_sdT.rearrange("(c p) d -> p c d", p=128)[:, half * 8:(half + 1) * 8, :]
                        else:
                            dsrc = d_edT[e].rearrange("(c p) d -> p c d", p=128)[:, half * 8:(half + 1) * 8, :]
                        em.wait_res('g', rWD[slot])
                        op('g', 'dma_start', out=wd[slot][:, :, :], in_=dsrc, inc=wl2[slot], k=16)
                        wd_ev[useq] = (slot, wl2[slot], wl2[slot].n)

                    act_done = [0, 0]

                    def emit_down(g, u):
                        e, half = units[u]
                        uslot = u % 2
                        wdslot, wdsem, wdval = wd_ev[g * NU + u]
                        wait('t', wdsem, wdval)
                        wait('t', s_dve, act_done[uslot])
                        for tm in range(NTM):
                            for dt_ in range(2):
                                pp = ctr['dsl'] % 2
                                ctr['dsl'] += 1
                                em.wait_res('t', rD[pp])
                                for hsub in range(NHS):
                                    op('t', 'matmul', pD[:, pp, :],
                                       actb[uslot][:, hsub, tm * 128:(tm + 1) * 128],
                                       wd[wdslot][:, hsub, dt_ * 512:(dt_ + 1) * 512],
                                       start=(hsub == 0), stop=(hsub == NHS - 1),
                                       inc=(s_mm if hsub == NHS - 1 else None))
                                wait('v', s_mm, s_mm.n)
                                aslab = acc[:, tm, dt_ * 512:(dt_ + 1) * 512]
                                if u == 0:
                                    op('v', 'tensor_scalar', aslab, pD[:, pp, :],
                                       0.5, None, ALU.mult, inc=s_dve)
                                elif e is None:
                                    op('v', 'scalar_tensor_tensor', aslab, pD[:, pp, :],
                                       0.5, aslab, ALU.mult, ALU.add, inc=s_dve)
                                else:
                                    col = cwb[:, g % 2, tm * 8 + e: tm * 8 + e + 1]
                                    op('v', 'scalar_tensor_tensor', aslab, pD[:, pp, :],
                                       col, aslab, ALU.mult, ALU.add, inc=s_dve)
                                rD[pp].ready(s_dve)
                                pop_side(side1)
                        rACT[uslot].ready(s_mm, s_mm.n)
                        rWD[wdslot].ready(s_mm, s_mm.n)
                        sched_wd(g * NU + u + 2)

                    for sq in range(3):
                        sched_guw(sq)
                    sched_wd(0)
                    sched_wd(1)
                    for i in range(NTM):
                        side1.append(stage1_tile(0, i))
                    flush(side1)

                    for g in range(NGRP):
                        xn_all = s_dve.n
                        for u in range(NU):
                            e, half = units[u]
                            uslot = u % 2
                            first_mult = True
                            for sc in range(NSC):
                                seq = (g * NU + u) * NSC + sc
                                slot, wsem, wval = guw_ev[seq]
                                wait('t', wsem, wval)
                                for hs in range(2):
                                    hsub = sc * 2 + hs
                                    for tt in range(NTT):
                                        pp = ctr['gu'] % 2
                                        ctr['gu'] += 1
                                        em.wait_res('t', rGU[pp])
                                        if u == 0 and sc == 0 and hs == 0 and tt == 0:
                                            wait('t', s_dve, xn_all)
                                        for dk in range(DK):
                                            op('t', 'matmul', pGU[:, 2 * pp, :],
                                               wg[slot][:, dk, hs * 128:(hs + 1) * 128],
                                               xn_fm[:, g % 2, dk, tt * 512:(tt + 1) * 512],
                                               start=(dk == 0), stop=(dk == DK - 1),
                                               inc=(s_mm if dk == DK - 1 else None))
                                        gval = s_mm.n
                                        for dk in range(DK):
                                            op('t', 'matmul', pGU[:, 2 * pp + 1, :],
                                               wu[slot][:, dk, hs * 128:(hs + 1) * 128],
                                               xn_fm[:, g % 2, dk, tt * 512:(tt + 1) * 512],
                                               start=(dk == 0), stop=(dk == DK - 1),
                                               inc=(s_mm if dk == DK - 1 else None))
                                        uval = s_mm.n
                                        wait('s', s_mm, gval)
                                        em.wait_res('s', rSIL[pp])
                                        op('s', 'activation', silb[pp][:, :], pGU[:, 2 * pp, :],
                                           AF.Silu, inc=s_act)
                                        wait('v', s_act, s_act.n)
                                        wait('v', s_mm, uval)
                                        if first_mult:
                                            em.wait_res('v', rACT[uslot])
                                            first_mult = False
                                        op('v', 'tensor_tensor',
                                           actb[uslot][:, hsub, tt * 512:(tt + 1) * 512],
                                           silb[pp][:, :], pGU[:, 2 * pp + 1, :], ALU.mult,
                                           inc=s_dve)
                                        rGU[pp].ready(s_dve)
                                        rSIL[pp].ready(s_dve)
                                        pop_side(side5)
                                rW[slot].ready(s_mm, s_mm.n)
                                sched_guw(seq + 3)
                            act_done[uslot] = s_dve.n
                            if u == 12 and g + 1 < NGRP:
                                for i in range(NTM):
                                    side1.append(stage1_tile(g + 1, i))
                            if u > 0:
                                if u == 1:
                                    flush(side5)
                                emit_down(g, u - 1)
                        emit_down(g, NU - 1)
                        flush(side1)
                        for tm in range(NTM):
                            side5.append(stage5_tile(g, tm))
                    flush(side5)
                    em.run_block()
            return store_events

        def emit_attn(dst, x1_events, entry=()):
            store_events = []
            with contextlib.ExitStack() as fs:
                def fsb(name, shape, dt):
                    return fs.enter_context(nc.sbuf_tensor(name, shape, dt))
                sc_ar = fsb("sc_ar", [128, TLOC], F32)
                sc_ai = fsb("sc_ai", [128, TLOC], F32)
                sc_ur = fsb("sc_ur", [128, TLOC], F32)
                sc_ui = fsb("sc_ui", [128, TLOC], F32)
                hrebf = fsb("hrebf", [128, TLOC], BF16)
                himbf = fsb("himbf", [128, TLOC], BF16)
                a128b = fsb("a128b", [1, TLOC], F32)
                u128b = fsb("u128b", [1, TLOC], F32)
                h128bf = fsb("h128bf", [1, TLOC], BF16)
                r128 = [fsb(f"r128{q}", [1, 1], F32) for q in range(2)]
                c0re = fsb("c0re", [128, 1], F32)
                c0im = fsb("c0im", [128, 1], F32)
                c128c = fsb("c128c", [1, 1], F32)

                bnm = fsb("bnm", [128, NTILE], F32)
                bnr = fsb("bnr", [128, NTILE], F32)
                pngG = [[fsb(f"pgg{j}{q}", [128, CH], F32) for q in range(4)] for j in range(2)]
                tmpG = [fsb(f"tmg{q}", [128, CH], F32) for q in range(2)]

                with nc.psum_tensor("pTRB", [128, DK, 128], BF16) as pTR, \
                     nc.psum_tensor("pPJ", [128, 4, 512], F32) as pPJ:

                    rTR = Res()
                    rPJ = [Res(), Res(), Res(), Res()]
                    rFF = [Res(), Res()]
                    rYp = [Res(), Res()]
                    rX1 = [Res(), Res()]
                    rZ = [Res(), Res()]
                    rO = [Res(), Res()]
                    rSCR = Res()

                    for eng in 'tvsgy':
                        for bsem, bval in entry:
                            wait(eng, bsem, bval)

                    with contextlib.ExitStack() as f1, \
                         nc.psum_tensor("pFF", [128, 2, 512], F32) as pFF:
                        def f1b(name, shape, dt):
                            return f1.enter_context(nc.sbuf_tensor(name, shape, dt))
                        c_WaT = f1b("c_WaT", [128, DK, NST], BF16)
                        c_WBT = f1b("c_WBT", [128, DK, NST], BF16)
                        op('g', 'dma_start', out=c_WaT[:, :, :],
                           in_=d_WaT.rearrange("(c p) n -> p c n", p=128), inc=s_ld0, k=16)
                        op('g', 'dma_start', out=c_WBT[:, :, :],
                           in_=d_WBT.rearrange("(c p) n -> p c n", p=128), inc=s_ld0, k=16)
                        waW_val = s_ld0.n
                        wait('t', s_ld0, waW_val)
                        x1ring = f1b("x1ringF", [128, 2, D], F32)
                        zfr = f1b("zfrF", [128, 2, D], F32)
                        xntr = f1b("xntrF", [128, 2, D], BF16)
                        xfa = [f1b(f"xfa{j}", [128, DK, 128], BF16) for j in range(2)]
                        abf = f1b("abf", [128, 2, 128], BF16)
                        ubf = f1b("ubf", [128, 2, 128], BF16)
                        tanb = f1b("tanb", [128, 2, 128], BF16)
                        t1b = f1b("t1b", [128, D], F32)
                        g1b = f1b("g1b", [128, 128], F32)
                        g2b = f1b("g2b", [128, 128], F32)
                        ggb = g1b
                        bn6 = f1b("bn6B", [128, 12], F32)
                        bn6b = bn6
                        bn2r = f1b("bn2rB", [128, 2, 2], F32)
                        bn2br = f1b("bn2brB", [128, 2, 2], F32)
                        sdq = f1b("sdqB", [128, 4, 1], F32)
                        sdvr = sdq[:, 0:2, :]
                        sdv2r = sdq[:, 2:4, :]
                        rstd2 = f1b("rstd2B", [128, 1], F32)
                        row3 = f1b("row3", [1, 384], F32)
                        ab128 = row3[0:1, 0:128]
                        gg128 = row3[0:1, 128:256]
                        raw128 = row3[0:1, 256:384]
                        ggv = [0, 0, 0]

                        rZF = [Res(), Res()]
                        rXN4 = [Res() for _ in range(4)]
                        rFA = [Res(), Res()]
                        rB2 = [Res(), Res()]
                        rB2b = [Res(), Res()]
                        rSD = [Res(), Res()]
                        rSD2 = [Res(), Res()]
                        rTAN = [Res(), Res()]

                        sq1v = [0] * NTILE
                        sq2v = [0] * NTILE
                        xnt_done = [0] * NTILE
                        aval = [None] * NTILE
                        uval = [None] * NTILE

                        def A1(i):
                            xs = i % 2
                            em.wait_res('y', rX1[xs])
                            gsem, gval = x1_events[i]
                            wait('y', gsem, gval)
                            op('y', 'dma_start', out=x1ring[:, xs, :],
                               in_=x1s[i * 128:(i + 1) * 128, :], inc=l1[xs], k=16)
                            wait('v', l1[xs], l1[xs].n)
                            drain('v')
                            op('v', 'bn_stats', bn6[:, 0:6], x1ring[:, xs, 0:512])
                            op('v', 'bn_stats', bn6[:, 6:12], x1ring[:, xs, 512:1024])
                            drain('v')
                            em.wait_res('v', rB2[xs])
                            op('v', 'bn_aggr', bn2r[:, xs, :], bn6[:, :], inc=s_dve)
                            wait('s', s_dve, s_dve.n)
                            em.wait_res('s', rSD[xs])
                            op('s', 'activation', sdvr[:, xs, :], bn2r[:, xs, 1:2], AF.Sqrt,
                               bias=c_eps[:, 0:1], inc=s_act)
                            sq1v[i] = s_act.n

                        def A2(i):
                            xs = i % 2
                            zs = i % 2
                            wait('v', s_act, sq1v[i])
                            op('v', 'reciprocal', bnr[:, i:i + 1], sdvr[:, xs, :])
                            drain('v')
                            op('v', 'tensor_copy', bnm[:, i:i + 1], bn2r[:, xs, 0:1], inc=s_dve)
                            rSD[xs].ready(s_dve)
                            rB2[xs].ready(s_dve)
                            em.wait_res('v', rZF[zs])
                            drain('v')
                            op('v', 'tensor_scalar', zfr[:, zs, :], x1ring[:, xs, :],
                               bnm[:, i:i + 1], bnr[:, i:i + 1], ALU.subtract, ALU.mult, inc=s_dve)
                            rX1[xs].ready(s_dve)
                            drain('v')
                            op('v', 'bn_stats', bn6b[:, 0:6], zfr[:, zs, 0:512])
                            op('v', 'bn_stats', bn6b[:, 6:12], zfr[:, zs, 512:1024])
                            drain('v')
                            em.wait_res('v', rB2b[xs])
                            op('v', 'bn_aggr', bn2br[:, xs, :], bn6b[:, :], inc=s_dve)
                            wait('s', s_dve, s_dve.n)
                            em.wait_res('s', rSD2[xs])
                            op('s', 'activation', sdv2r[:, xs, :], bn2br[:, xs, 1:2], AF.Sqrt,
                               bias=c_eps[:, 0:1], inc=s_act)
                            sq2v[i] = s_act.n

                        def A3(i):
                            xs = i % 2
                            zs = i % 2
                            ns = i % 2
                            wait('v', s_act, sq2v[i])
                            op('v', 'reciprocal', rstd2[:, :], sdv2r[:, xs, :], inc=s_dve)
                            rSD2[xs].ready(s_dve)
                            drain('v')
                            op('v', 'tensor_scalar', t1b[:, :], zfr[:, zs, :],
                               bn2br[:, xs, 0:1], rstd2[:, 0:1], ALU.subtract, ALU.mult, inc=s_dve)
                            rB2b[xs].ready(s_dve)
                            drain('v')
                            op('v', 'tensor_tensor', zfr[:, zs, :], t1b[:, :], c_lnw[:, :], ALU.mult)
                            drain('v')
                            em.wait_res('v', rXN4[ns])
                            op('v', 'tensor_tensor', xntr[:, ns, :], zfr[:, zs, :], c_lnb[:, :], ALU.add,
                               inc=s_dve)
                            rZF[zs].ready(s_dve)
                            xnt_done[i] = s_dve.n

                        def B1b1(i):
                            ns = i % 2
                            fa = i % 2
                            wait('t', s_dve, xnt_done[i])
                            em.wait_res('t', rTR)
                            for dk in range(DK):
                                op('t', 'transpose', pTR[:, dk, :],
                                   xntr[:, ns, dk * 128:(dk + 1) * 128], c_identb[:, :],
                                   inc=(s_mm if dk == DK - 1 else None))
                            wait('v', s_mm, s_mm.n)
                            em.wait_res('v', rFA[fa])
                            op('v', 'tensor_copy', xfa[fa][:, :, :], pTR[:, :, :], inc=s_dve)
                            rTR.ready(s_dve)
                            rXN4[ns].ready(s_dve)
                            fcv = s_dve.n
                            wait('t', s_dve, fcv)
                            av = []
                            for ng in range(2):
                                em.wait_res('t', rPJ[ng])
                                for dk in range(DK):
                                    op('t', 'matmul', pPJ[:, ng, 0:128],
                                       c_WaT[:, dk, ng * 128:(ng + 1) * 128], xfa[fa][:, dk, :],
                                       start=(dk == 0), stop=(dk == DK - 1),
                                       inc=(s_mm if dk == DK - 1 else None))
                                av.append(s_mm.n)
                            uv = []
                            for ng in range(2):
                                em.wait_res('t', rPJ[2 + ng])
                                for dk in range(DK):
                                    op('t', 'matmul', pPJ[:, 2 + ng, 0:128],
                                       c_WBT[:, dk, ng * 128:(ng + 1) * 128], xfa[fa][:, dk, :],
                                       start=(dk == 0), stop=(dk == DK - 1),
                                       inc=(s_mm if dk == DK - 1 else None))
                                uv.append(s_mm.n)
                            aval[i] = av
                            uval[i] = uv
                            rFA[fa].ready(s_mm, s_mm.n)

                        def B1b2(i):
                            toff = i * 128
                            for ng in range(2):
                                wait('s', s_mm, aval[i][ng])
                                em.wait_res('s', rTAN[ng])
                                op('s', 'activation', tanb[:, ng, :], pPJ[:, ng, 0:128], AF.Tanh,
                                   bias=c_Wab[:, ng:ng + 1], scale=1.0, inc=s_act)
                            tanv = s_act.n
                            wait('v', s_act, tanv)
                            op('v', 'tensor_scalar', abf[:, 0, :], tanb[:, 0, :],
                               c_gamma[:, 0:1], None, ALU.mult)
                            op('v', 'tensor_scalar', abf[:, 1, :], tanb[:, 1, :],
                               c_gamma[:, 0:1], None, ALU.mult, inc=s_dve)
                            rTAN[0].ready(s_dve)
                            rTAN[1].ready(s_dve)
                            rPJ[1].ready(s_dve)
                            abf_done = s_dve.n
                            wait('v', s_mm, uval[i][1])
                            drain('v')
                            op('v', 'tensor_copy', ubf[:, 0, :], pPJ[:, 2, 0:128])
                            op('v', 'tensor_copy', ubf[:, 1, :], pPJ[:, 3, 0:128], inc=s_dve)
                            rPJ[3].ready(s_dve)
                            ubf_done = s_dve.n
                            em.wait_res('t', rFF[0])
                            em.wait_res('t', rFF[1])
                            wait('t', s_dve, abf_done)
                            op('t', 'matmul', pFF[:, 0, 0:128], c_C[:, 0, :], abf[:, 0, :], start=True, stop=False)
                            op('t', 'matmul', pFF[:, 0, 0:128], c_C[:, 1, :], abf[:, 1, :], start=False, stop=True)
                            op('t', 'matmul', pFF[:, 1, 0:128], c_S[:, 0, :], abf[:, 0, :], start=True, stop=False)
                            op('t', 'matmul', pFF[:, 1, 0:128], c_S[:, 1, :], abf[:, 1, :], start=False, stop=True)
                            op('t', 'matmul', pPJ[0:1, 0, 0:128], c_c128[:, 0:1], abf[:, 0, :], start=True, stop=False)
                            op('t', 'matmul', pPJ[0:1, 0, 0:128], c_c128[:, 1:2], abf[:, 1, :], start=False, stop=True,
                               inc=s_mm)
                            ffta = s_mm.n
                            wait('s', s_mm, ffta)
                            if ggv[0]:
                                wait('s', s_dve, ggv[0])
                            op('s', 'activation', g1b[:, :], pFF[:, 0, 0:128], AF.Square)
                            op('s', 'activation', g2b[:, :], pFF[:, 1, 0:128], AF.Square, inc=s_act)
                            wait('v', s_act, s_act.n)
                            op('v', 'tensor_tensor', g1b[:, :], g1b[:, :], g2b[:, :], ALU.add, inc=s_dve)
                            wait('s', s_dve, s_dve.n)
                            op('s', 'activation', g2b[:, :], g1b[:, :], AF.Sqrt, inc=s_act)
                            wait('v', s_act, s_act.n)
                            op('v', 'tensor_scalar', g2b[:, :], g2b[:, :], 1e-8, None, ALU.max, inc=s_dve)
                            ggv[0] = s_dve.n
                            wait('s', s_dve, s_dve.n)
                            if ggv[2]:
                                wait('s', s_dve, ggv[2])
                            op('s', 'activation', ggb[:, :], g2b[:, :], AF.Sigmoid, bias=c_two[:, 0:1], scale=-2.0,
                               inc=s_act)
                            ggend = s_act.n
                            wait('v', s_act, ggend)
                            op('v', 'tensor_tensor', sc_ar[:, toff:toff + 128], pFF[:, 0, 0:128],
                               ggb[:, :], ALU.mult)
                            op('v', 'scalar_tensor_tensor', sc_ai[:, toff:toff + 128], pFF[:, 1, 0:128],
                               -1.0, ggb[:, :], ALU.mult, ALU.mult, inc=s_dve)
                            ggv[2] = s_dve.n
                            rFF[0].ready(s_dve)
                            rFF[1].ready(s_dve)
                            wait('s', s_mm, ffta)
                            if ggv[1]:
                                wait('s', s_dve, ggv[1])
                            op('s', 'activation', ab128, pPJ[0:1, 0, 0:128], AF.Abs)
                            op('s', 'activation', raw128, pPJ[0:1, 0, 0:128], AF.Identity, inc=s_act)
                            wait('v', s_act, s_act.n)
                            op('v', 'tensor_scalar', ab128, ab128, 1e-8, None, ALU.max, inc=s_dve)
                            wait('s', s_dve, s_dve.n)
                            op('s', 'activation', gg128, ab128, AF.Sigmoid, bias=c_two[0:1, 0:1], scale=-2.0,
                               inc=s_act)
                            wait('v', s_act, s_act.n)
                            op('v', 'tensor_tensor', a128b[0:1, toff:toff + 128], raw128,
                               gg128, ALU.mult, inc=s_dve)
                            ggv[1] = s_dve.n
                            rPJ[0].ready(s_dve)
                            em.wait_res('t', rFF[0])
                            em.wait_res('t', rFF[1])
                            wait('t', s_dve, ubf_done)
                            op('t', 'matmul', pFF[:, 0, 0:128], c_C[:, 0, :], ubf[:, 0, :], start=True, stop=False)
                            op('t', 'matmul', pFF[:, 0, 0:128], c_C[:, 1, :], ubf[:, 1, :], start=False, stop=True)
                            op('t', 'matmul', pFF[:, 1, 0:128], c_S[:, 0, :], ubf[:, 0, :], start=True, stop=False)
                            op('t', 'matmul', pFF[:, 1, 0:128], c_S[:, 1, :], ubf[:, 1, :], start=False, stop=True)
                            op('t', 'matmul', pPJ[0:1, 2, 0:128], c_c128[:, 0:1], ubf[:, 0, :], start=True, stop=False)
                            op('t', 'matmul', pPJ[0:1, 2, 0:128], c_c128[:, 1:2], ubf[:, 1, :], start=False, stop=True,
                               inc=s_mm)
                            fftu = s_mm.n
                            wait('s', s_mm, fftu)
                            op('s', 'activation', sc_ur[:, toff:toff + 128], pFF[:, 0, 0:128],
                               AF.Identity, bias=fbre[:, 0:1], scale=1.0)
                            op('s', 'activation', sc_ui[:, toff:toff + 128], pFF[:, 1, 0:128],
                               AF.Identity, bias=fbim[:, 0:1], scale=-1.0)
                            op('s', 'activation', u128b[0:1, toff:toff + 128], pPJ[0:1, 2, 0:128],
                               AF.Identity, bias=fb128[0:1, 0:1], scale=1.0, inc=s_act)
                            rFF[0].ready(s_act)
                            rFF[1].ready(s_act)
                            rPJ[2].ready(s_act)

                        chunk_rdy = [None] * NCHK
                        for r in range(NTILE + 4):
                            if r < NTILE:
                                A1(r)
                            if 1 <= r < NTILE + 1:
                                A2(r - 1)
                            if 2 <= r < NTILE + 2:
                                A3(r - 2)
                            if 4 <= r:
                                B1b2(r - 4)
                                if (r - 4) % 4 == 3:
                                    chunk_rdy[(r - 4) // 4] = (s_dve.n, s_act.n)
                            if 3 <= r < NTILE + 3:
                                B1b1(r - 3)
                        front_done = s_dve.n
                        front_act = s_act.n
                        front_gp = s_gp.n
                        front_mm = s_mm.n
                        em.run_block()

                    with contextlib.ExitStack() as f2, \
                         nc.psum_tensor("pYB", [128, 3, 512], F32) as pY:
                        def f2b(name, shape, dt):
                            return f2.enter_context(nc.sbuf_tensor(name, shape, dt))
                        scr128 = f2b("scr128", [1, CH], F32)
                        Fre = f2b("Fre", [128, 1], F32)
                        Fim = f2b("Fim", [128, 1], F32)
                        nFre = f2b("nFre", [128, 1], F32)
                        nFim = f2b("nFim", [128, 1], F32)
                        tc4 = [f2b(f"tc{q}", [128, 1], F32) for q in range(4)]
                        ccre = [f2b(f"ccre{q}", [128, 1], F32) for q in range(2)]
                        ccim = [f2b(f"ccim{q}", [128, 1], F32) for q in range(2)]
                        ncim = f2b("ncim", [128, 1], F32)
                        f2a = f2.enter_context(contextlib.ExitStack())
                        def f2ab(name, shape, dt):
                            return f2a.enter_context(nc.sbuf_tensor(name, shape, dt))
                        pngV = [[f2ab(f"pgv{j}{q}", [128, PAD + CH], F32) for q in range(4)] for j in range(2)]
                        tmpV = [f2ab(f"tmv{q}", [128, CH], F32) for q in range(2)]

                        u_fin = [0] * NCHK

                        def scan_chunks(eng, sem_, png, tmp4, chunks, pad):
                            if pad:
                                for j in range(2):
                                    op(eng, 'memset', png[j][0][:, 0:pad], 1.0)
                                    op(eng, 'memset', png[j][1][:, 0:pad], 0.0)
                                    op(eng, 'memset', png[j][2][:, 0:pad], 0.0)
                                    op(eng, 'memset', png[j][3][:, 0:pad], 0.0)
                                drain(eng)
                            for c in chunks:
                                cs = slice(c * CH, (c + 1) * CH)
                                rv, rs = chunk_rdy[c]
                                wait(eng, s_dve, rv)
                                wait(eng, s_act, rs)
                                for q, srcb in enumerate((sc_ar, sc_ai, sc_ur, sc_ui)):
                                    op(eng, 'tensor_copy', png[0][q][:, pad:], srcb[:, cs])
                                drain(eng)
                                cur = 0
                                s_ = 1
                                for lev in range(NLEV):
                                    A = png[cur]
                                    B = png[1 - cur]
                                    if pad:
                                        cu = slice(pad, pad + CH)
                                        sh = slice(pad - s_, pad - s_ + CH)
                                        TL = CH
                                        pre = None
                                    else:
                                        cu = slice(s_, CH)
                                        sh = slice(0, CH - s_)
                                        TL = CH - s_
                                        pre = slice(0, s_)
                                    op(eng, 'tensor_tensor', B[0][:, cu], A[0][:, cu], A[0][:, sh], ALU.mult)
                                    op(eng, 'tensor_tensor', tmp4[0][:, 0:TL], A[1][:, cu], A[1][:, sh], ALU.mult)
                                    op(eng, 'tensor_tensor', B[1][:, cu], A[0][:, cu], A[1][:, sh], ALU.mult)
                                    op(eng, 'tensor_tensor', tmp4[1][:, 0:TL], A[1][:, cu], A[0][:, sh], ALU.mult)
                                    drain(eng)
                                    op(eng, 'tensor_tensor', B[0][:, cu], B[0][:, cu], tmp4[0][:, 0:TL], ALU.subtract)
                                    op(eng, 'tensor_tensor', B[1][:, cu], B[1][:, cu], tmp4[1][:, 0:TL], ALU.add)
                                    op(eng, 'tensor_tensor', B[2][:, cu], A[0][:, cu], A[2][:, sh], ALU.mult)
                                    op(eng, 'tensor_tensor', tmp4[0][:, 0:TL], A[1][:, cu], A[3][:, sh], ALU.mult)
                                    op(eng, 'tensor_tensor', B[3][:, cu], A[0][:, cu], A[3][:, sh], ALU.mult)
                                    op(eng, 'tensor_tensor', tmp4[1][:, 0:TL], A[1][:, cu], A[2][:, sh], ALU.mult)
                                    drain(eng)
                                    op(eng, 'tensor_tensor', B[2][:, cu], B[2][:, cu], tmp4[0][:, 0:TL], ALU.subtract)
                                    op(eng, 'tensor_tensor', B[3][:, cu], B[3][:, cu], tmp4[1][:, 0:TL], ALU.add)
                                    drain(eng)
                                    op(eng, 'tensor_tensor', B[2][:, cu], B[2][:, cu], A[2][:, cu], ALU.add)
                                    op(eng, 'tensor_tensor', B[3][:, cu], B[3][:, cu], A[3][:, cu], ALU.add)
                                    if pre is not None:
                                        for q in range(4):
                                            op(eng, 'tensor_copy', B[q][:, pre], A[q][:, pre])
                                    drain(eng)
                                    cur = 1 - cur
                                    s_ *= 2
                                Afin = png[cur]
                                op(eng, 'tensor_copy', sc_ar[:, cs], Afin[0][:, pad:])
                                op(eng, 'tensor_copy', sc_ai[:, cs], Afin[1][:, pad:])
                                op(eng, 'tensor_copy', sc_ur[:, cs], Afin[2][:, pad:])
                                op(eng, 'tensor_copy', sc_ui[:, cs], Afin[3][:, pad:], inc=sem_)
                                u_fin[c] = sem_.n
                                drain(eng)

                        scan_chunks('g', s_gp, pngG, tmpG, [0, 1, 2, 3], 0)
                        wait('v', s_dve, front_done)
                        wait('v', s_act, front_act)
                        wait('v', s_mm, front_mm)
                        scan_chunks('v', s_dve, pngV, tmpV, [4, 5, 6, 7], PAD)
                        f2a.close()
                        agrow = f2b("agrow", [1, 512], F32)
                        gath = f2b("gath", [world, 512], F32)
                        selrow = f2b("selrow", [1, 512], F32)
                        Mrow = f2b("Mrow3", [1, D], BF16)
                        x1ring3 = f2b("x1ring3", [128, 2, D], F32)
                        obuf = f2b("obufB", [128, 2, D], F32)
                        zt = f2b("ztB", [128, 512], F32)
                        zf = f2b("zfB", [128, 2, D], F32)
                        wait('y', s_gp, s_gp.n)
                        wait('y', s_dve, s_dve.n)
                        op('y', 'dma_start', out=Mrow[0:1, :], in_=mrow_scr[:, :], inc=s_ld0, k=16)
                        wait('t', s_ld0, s_ld0.n)
                        wait('v', s_gp, front_gp)
                        wait('v', s_act, front_act)
                        for c in range(NCHK):
                            cs = slice(c * CH, (c + 1) * CH)
                            init = 0.0 if c == 0 else r128[0][0:1, 0:1]
                            op('v', 'tensor_tensor_scan', scr128[0:1, :], a128b[0:1, cs], u128b[0:1, cs],
                               init, ALU.mult, ALU.add)
                            drain('v')
                            op('v', 'tensor_copy', r128[0][0:1, :], scr128[0:1, CH - 1:CH])
                            drain('v')
                        wait('v', s_gp, u_fin[3])
                        op('v', 'tensor_copy', Fre[:, :], sc_ur[:, CH - 1:CH])
                        op('v', 'tensor_copy', Fim[:, :], sc_ui[:, CH - 1:CH])
                        drain('v')
                        for c in range(1, NCHK):
                            le = (c + 1) * CH - 1
                            ALr = sc_ar[:, le:le + 1]
                            ALi = sc_ai[:, le:le + 1]
                            hLr = sc_ur[:, le:le + 1]
                            hLi = sc_ui[:, le:le + 1]
                            op('v', 'tensor_tensor', tc4[0][:, :], ALr, Fre[:, :], ALU.mult)
                            op('v', 'tensor_tensor', tc4[1][:, :], ALi, Fim[:, :], ALU.mult)
                            op('v', 'tensor_tensor', tc4[2][:, :], ALr, Fim[:, :], ALU.mult)
                            op('v', 'tensor_tensor', tc4[3][:, :], ALi, Fre[:, :], ALU.mult)
                            drain('v')
                            op('v', 'tensor_tensor', nFre[:, :], tc4[0][:, :], tc4[1][:, :], ALU.subtract)
                            op('v', 'tensor_tensor', nFim[:, :], tc4[2][:, :], tc4[3][:, :], ALU.add)
                            drain('v')
                            op('v', 'tensor_tensor', Fre[:, :], nFre[:, :], hLr, ALU.add)
                            op('v', 'tensor_tensor', Fim[:, :], nFim[:, :], hLi, ALU.add)
                            drain('v')
                        op('v', 'memset', agrow[0:1, :], 0.0, inc=s_dve)
                        fcol_done = s_dve.n
                        wait('t', s_dve, fcol_done)
                        em.wait_res('t', rPJ[0])
                        op('t', 'transpose', pPJ[0:1, 0, 0:128], Fre[:, :], c_identf[:, :], inc=s_mm)
                        wait('v', s_mm, s_mm.n)
                        drain('v')
                        op('v', 'tensor_copy', agrow[0:1, 0:128], pPJ[0:1, 0, 0:128], inc=s_dve)
                        rPJ[0].ready(s_dve)
                        em.wait_res('t', rPJ[1])
                        op('t', 'transpose', pPJ[0:1, 1, 0:128], Fim[:, :], c_identf[:, :], inc=s_mm)
                        wait('v', s_mm, s_mm.n)
                        drain('v')
                        op('v', 'tensor_copy', agrow[0:1, 128:256], pPJ[0:1, 1, 0:128], inc=s_dve)
                        rPJ[1].ready(s_dve)
                        op('v', 'tensor_copy', agrow[0:1, 256:257], r128[0][0:1, 0:1], inc=s_dve)
                        ag_ready = s_dve.n
                        wait('g', s_dve, ag_ready)
                        op('g', 'dma_start', out=cc_in[:, :], in_=agrow[0:1, :], inc=s_wl0, k=16)
                        wait('g', s_wl0, s_wl0.n)
                        op('g', 'collective_compute', "AllGather", ALU.bypass,
                           replica_groups=[list(range(world))],
                           ins=[cc_in.ap().opt()], outs=[cc_out.ap().opt()], inc=s_cc)
                        wait('g', s_cc, s_cc.n)
                        op('g', 'dma_start', out=gath[:, :], in_=cc_out[:, :], inc=s_wl0, k=16)
                        gath_val = s_wl0.n
                        wait('t', s_wl0, gath_val)
                        em.wait_res('t', rPJ[0])
                        op('t', 'matmul', pPJ[0:1, 0, :], c_sel[:, :], gath[:, :], start=True, stop=True,
                           inc=s_mm)
                        wait('v', s_mm, s_mm.n)
                        drain('v')
                        op('v', 'tensor_copy', selrow[0:1, :], pPJ[0:1, 0, :], inc=s_dve)
                        rPJ[0].ready(s_dve)
                        wait('t', s_dve, s_dve.n)
                        em.wait_res('t', rPJ[0])
                        op('t', 'transpose', pPJ[:, 0, 0:1], selrow[0:1, 0:128], c_identf[0:1, 0:1], inc=s_mm)
                        em.wait_res('t', rPJ[1])
                        op('t', 'transpose', pPJ[:, 1, 0:1], selrow[0:1, 128:256], c_identf[0:1, 0:1], inc=s_mm)
                        wait('v', s_mm, s_mm.n)
                        drain('v')
                        op('v', 'tensor_copy', c0re[:, :], pPJ[:, 0, 0:1])
                        op('v', 'tensor_copy', c0im[:, :], pPJ[:, 1, 0:1])
                        op('v', 'tensor_copy', c128c[0:1, :], selrow[0:1, 256:257], inc=s_dve)
                        rPJ[0].ready(s_dve)
                        rPJ[1].ready(s_dve)
                        drain('v')
                        rY3 = [Res(), Res(), Res()]
                        yv = [None] * NTILE
                        ysl = [None] * NTILE
                        ctr3 = {'y': 0}

                        def stP(i, gate):
                            toff = i * 128
                            xs = i % 2
                            em.wait_res('y', rX1[xs])
                            op('y', 'dma_start', out=x1ring3[:, xs, :], in_=x1s[toff:toff + 128, :],
                               inc=l1[xs], k=16)
                            wait('v', l1[xs], l1[xs].n)
                            em.wait_res('v', rZ[xs])
                            drain('v')
                            op('v', 'tensor_scalar', zf[:, xs, :], x1ring3[:, xs, :],
                               bnm[:, i:i + 1], bnr[:, i:i + 1], ALU.subtract, ALU.mult, inc=s_dve)
                            vals = []
                            slots = []
                            wait('t', s_act, gate)
                            for dt_ in range(2):
                                dsl = slice(dt_ * 512, (dt_ + 1) * 512)
                                yp = ctr3['y'] % 3
                                ctr3['y'] += 1
                                em.wait_res('t', rY3[yp])
                                op('t', 'matmul', pY[:, yp, :], hrebf[:, toff:toff + 128],
                                   M2reT[:, dsl], start=True, stop=False)
                                op('t', 'matmul', pY[:, yp, :], himbf[:, toff:toff + 128],
                                   M2imT[:, dsl], start=False, stop=False)
                                op('t', 'matmul', pY[:, yp, :], h128bf[0:1, toff:toff + 128],
                                   Mrow[0:1, dsl], start=False, stop=True, inc=s_mm)
                                vals.append(s_mm.n)
                                slots.append(yp)
                            yv[i] = vals
                            ysl[i] = slots

                        def stQ(i):
                            toff = i * 128
                            xs = i % 2
                            os_ = i % 2
                            for dt_ in range(2):
                                dsl = slice(dt_ * 512, (dt_ + 1) * 512)
                                yp = ysl[i][dt_]
                                wait('v', s_mm, yv[i][dt_])
                                if dt_ == 0:
                                    em.wait_res('v', rO[os_])
                                drain('v')
                                op('v', 'tensor_tensor', zt[:, :], zf[:, xs, dsl], c_Dsk[:, dsl],
                                   ALU.mult)
                                op('v', 'tensor_tensor', obuf[:, os_, dsl], pY[:, yp, :], c_WCb[:, dsl],
                                   ALU.add)
                                drain('v')
                                op('v', 'tensor_tensor', obuf[:, os_, dsl], obuf[:, os_, dsl], zt[:, :],
                                   ALU.add)
                                drain('v')
                                op('v', 'tensor_tensor', obuf[:, os_, dsl], obuf[:, os_, dsl],
                                   x1ring3[:, xs, dsl], ALU.add, inc=s_dve)
                                rY3[yp].ready(s_dve)
                            rX1[xs].ready(s_dve)
                            rZ[xs].ready(s_dve)
                            wait('y', s_dve, s_dve.n)
                            op('y', 'dma_start', out=dst[toff:toff + 128, :], in_=obuf[:, os_, :],
                               inc=so[os_], k=16)
                            store_events.append((so[os_], so[os_].n))
                            rO[os_].ready(so[os_])

                        prev_tile = None
                        for c in range(NCHK):
                            cs = slice(c * CH, (c + 1) * CH)
                            cre = c0re if c == 0 else ccre[(c - 1) % 2]
                            cim = c0im if c == 0 else ccim[(c - 1) % 2]
                            op('v', 'tensor_scalar', ncim[:, :], cim[:, :], -1.0, None, ALU.mult)
                            drain('v')
                            op('v', 'scalar_tensor_tensor', sc_ur[:, cs], sc_ar[:, cs], cre[:, 0:1],
                               sc_ur[:, cs], ALU.mult, ALU.add)
                            op('v', 'scalar_tensor_tensor', sc_ui[:, cs], sc_ar[:, cs], cim[:, 0:1],
                               sc_ui[:, cs], ALU.mult, ALU.add)
                            drain('v')
                            op('v', 'scalar_tensor_tensor', sc_ur[:, cs], sc_ai[:, cs], ncim[:, 0:1],
                               sc_ur[:, cs], ALU.mult, ALU.add)
                            op('v', 'scalar_tensor_tensor', sc_ui[:, cs], sc_ai[:, cs], cre[:, 0:1],
                               sc_ui[:, cs], ALU.mult, ALU.add)
                            drain('v')
                            le = (c + 1) * CH - 1
                            op('v', 'tensor_copy', ccre[c % 2][:, :], sc_ur[:, le:le + 1])
                            op('v', 'tensor_copy', ccim[c % 2][:, :], sc_ui[:, le:le + 1], inc=s_dve)
                            drain('v')
                            casc_v = s_dve.n
                            wait('s', s_dve, casc_v)
                            op('s', 'copy', hrebf[:, cs], sc_ur[:, cs])
                            op('s', 'copy', himbf[:, cs], sc_ui[:, cs], inc=s_act)
                            init = c128c[0:1, 0:1] if c == 0 else r128[1][0:1, 0:1]
                            em.wait_res('v', rSCR)
                            op('v', 'tensor_tensor_scan', scr128[0:1, :], a128b[0:1, cs], u128b[0:1, cs],
                               init, ALU.mult, ALU.add)
                            drain('v')
                            op('v', 'tensor_copy', r128[1][0:1, :], scr128[0:1, CH - 1:CH], inc=s_dve)
                            drain('v')
                            wait('s', s_dve, s_dve.n)
                            op('s', 'copy', h128bf[0:1, cs], scr128[0:1, :], inc=s_act)
                            rSCR.ready(s_act)
                            hbf_c = s_act.n
                            for k in range(4):
                                i = c * 4 + k
                                if prev_tile is not None:
                                    stQ(prev_tile)
                                stP(i, hbf_c)
                                prev_tile = i
                        stQ(prev_tile)
                        em.run_block()
            return store_events

        def so_totals():
            return ((so[0], so[0].n), (so[1], so[1].n))

        if phases == 1:
            emit_ffn(x_in, y_out, "A")
        elif phases == 2:
            ev1 = emit_ffn(x_in, x1s, "A")
            emit_attn(y_out, ev1, entry=so_totals())
        else:
            ev1 = emit_ffn(x_in, x1s, "A")
            ev2 = emit_attn(x2s, ev1, entry=so_totals())
            emit_ffn(x2s, y_out, "C", src_events=ev2, entry=so_totals())

        fin = Emit(nc)
        fin.wait('y', so[0], so[0].n)
        fin.wait('y', so[1], so[1].n)
        fin.run_block()

    return nc


def _prep_inputs(L0_attn_Wa_w, L0_attn_Wa_b, L0_attn_log_gamma, L0_attn_WB_w, L0_attn_WB_b,
                 L0_attn_WC_w, L0_attn_WC_b, L0_attn_D_skip, L0_attn_ln_w, L0_attn_ln_b,
                 L0_ffn_router_w, L0_ffn_sg, L0_ffn_su, L0_ffn_sd, L0_ffn_eg, L0_ffn_eu,
                 L0_ffn_ed):
    f32 = np.float32
    ac = np.ascontiguousarray
    n = np.arange(NST)
    ang = 2.0 * np.pi * np.outer(n, n) / NST
    Cfull = np.cos(ang)
    Sfull = np.sin(ang)
    rep = lambda v: ac(np.broadcast_to(np.asarray(v, f32)[None, :], (128, D)))
    return dict(
        rtW=ac(np.asarray(L0_ffn_router_w, f32).T),
        sgT=ac(np.asarray(L0_ffn_sg, f32).T),
        suT=ac(np.asarray(L0_ffn_su, f32).T),
        sdT=ac(np.asarray(L0_ffn_sd, f32).T),
        egT=ac(np.asarray(L0_ffn_eg, f32).transpose(0, 2, 1)),
        euT=ac(np.asarray(L0_ffn_eu, f32).transpose(0, 2, 1)),
        edT=ac(np.asarray(L0_ffn_ed, f32).transpose(0, 2, 1)),
        WaT=ac(np.asarray(L0_attn_Wa_w, f32).T),
        WBT=ac(np.asarray(L0_attn_WB_w, f32).T),
        WCT=ac(np.asarray(L0_attn_WC_w, f32).T),
        Wab=ac(np.asarray(L0_attn_Wa_b, f32).reshape(2, 128).T),
        WBb=ac(np.asarray(L0_attn_WB_b, f32).reshape(2, 128).T),
        lgam=np.asarray(L0_attn_log_gamma, f32).reshape(1, 1),
        Ctab=ac(Cfull[:, 0:KH].astype(f32)),
        Stab=ac(Sfull[:, 0:KH].astype(f32)),
        c128=ac(Cfull[:, KH].astype(f32).reshape(2, 128).T),
        ident=np.eye(128, dtype=f32),
        lnw_rep=rep(L0_attn_ln_w),
        lnb_rep=rep(L0_attn_ln_b),
        WCb_rep=rep(L0_attn_WC_b),
        Dsk_rep=rep(L0_attn_D_skip),
    )


def kernel(L0_attn_x, **kw):
    f32 = np.float32
    TLOC = 4096
    xflat = np.ascontiguousarray(np.asarray(L0_attn_x, f32).reshape(-1, D))
    common = _prep_inputs(**kw)
    nc = build_program(TLOC=TLOC, phases=3)
    in_maps = []
    for c in range(8):
        sel = np.zeros((8, 1), f32)
        if c % 2 == 1:
            sel[c - 1, 0] = 1.0
        m = dict(common)
        m["x"] = xflat[c * TLOC:(c + 1) * TLOC]
        m["sel"] = sel
        in_maps.append(m)
    res = run_bass_kernel_spmd(nc, in_maps, core_ids=list(range(8)))
    out = np.concatenate([res.results[c]["y"] for c in range(8)], axis=0)
    return out.reshape(L0_attn_x.shape).astype(f32)


# version 15
# speedup vs baseline: 1.0016x; 1.0016x over previous
import contextlib
import numpy as np
import concourse.bass as bass
import concourse.mybir as mybir
from concourse.bass_utils import run_bass_kernel_spmd

F32 = mybir.dt.float32
BF16 = mybir.dt.bfloat16
AF = mybir.ActivationFunctionType
ALU = mybir.AluOpType
AX = mybir.AxisListType

D = 1024
DK = 8
NST = 256
KH = 128
H = 2048
HC = 256
NHC = H // HC
HS = HC // 128
NE = 8
LN_EPS = 1e-5


class Sem:
    def __init__(self, h):
        self.h = h
        self.n = 0


class Res:
    def __init__(self):
        self.cons = []

    def ready(self, sem, val=None):
        self.cons = [(sem, sem.n if val is None else val)]

    def also(self, sem, val=None):
        self.cons.append((sem, sem.n if val is None else val))


class Emit:
    def __init__(self, nc):
        self.nc = nc
        self.ops = {k: [] for k in 'tvsgy'}

    def op(self, eng, method, *args, inc=None, k=1, **kw):
        semh = inc.h if inc is not None else None
        if inc is not None:
            inc.n += k

        def f(e, m=method, a=args, kw=kw, s=semh, k=k):
            r = getattr(e, m)(*a, **kw)
            if s is not None:
                r.then_inc(s, k)
        self.ops[eng].append(f)
        return inc.n if inc is not None else None

    def wait(self, eng, sem, val=None):
        v = sem.n if val is None else val
        self.ops[eng].append(lambda e, s=sem.h, v=v: e.wait_ge(s, v))

    def wait_res(self, eng, res):
        for sem, val in res.cons:
            self.wait(eng, sem, val)

    def drain(self, eng):
        self.ops[eng].append(lambda e: e.drain())

    def run_block(self):
        nc = self.nc
        ops = self.ops
        with nc.Block() as blk:
            @blk.tensor
            def _(t):
                for f in ops['t']:
                    f(t)

            @blk.vector
            def _(v):
                for f in ops['v']:
                    f(v)

            @blk.scalar
            def _(s):
                for f in ops['s']:
                    f(s)

            @blk.gpsimd
            def _(g):
                for f in ops['g']:
                    f(g)

            @blk.sync
            def _(y):
                for f in ops['y']:
                    f(y)
        self.ops = {k: [] for k in 'tvsgy'}


def build_program(TLOC=4096, phases=3, world=8):
    TGRP = min(2048, TLOC)
    NGRP = TLOC // TGRP
    NTT = TGRP // 512
    NTM = TGRP // 128
    NTILE = TLOC // 128
    CH = min(512, TLOC)
    NCHK = TLOC // CH
    PAD = CH // 2
    NLEV = CH.bit_length() - 1

    nc = bass.Bass("TRN2", target_bir_lowering=False)

    x_in = nc.dram_tensor("x", [TLOC, D], F32, kind="ExternalInput").ap()
    y_out = nc.dram_tensor("y", [TLOC, D], F32, kind="ExternalOutput").ap()
    d_sel = nc.dram_tensor("sel", [world, 1], F32, kind="ExternalInput").ap()

    d_rtW = nc.dram_tensor("rtW", [D, NE], F32, kind="ExternalInput").ap()
    d_sgT = nc.dram_tensor("sgT", [D, H], F32, kind="ExternalInput").ap()
    d_suT = nc.dram_tensor("suT", [D, H], F32, kind="ExternalInput").ap()
    d_sdT = nc.dram_tensor("sdT", [H, D], F32, kind="ExternalInput").ap()
    d_egT = nc.dram_tensor("egT", [NE, D, H], F32, kind="ExternalInput").ap()
    d_euT = nc.dram_tensor("euT", [NE, D, H], F32, kind="ExternalInput").ap()
    d_edT = nc.dram_tensor("edT", [NE, H, D], F32, kind="ExternalInput").ap()

    d_WaT = nc.dram_tensor("WaT", [D, NST], F32, kind="ExternalInput").ap()
    d_WBT = nc.dram_tensor("WBT", [D, NST], F32, kind="ExternalInput").ap()
    d_WCT = nc.dram_tensor("WCT", [NST, D], F32, kind="ExternalInput").ap()
    d_Wab = nc.dram_tensor("Wab", [128, 2], F32, kind="ExternalInput").ap()
    d_WBb = nc.dram_tensor("WBb", [128, 2], F32, kind="ExternalInput").ap()
    d_lgam = nc.dram_tensor("lgam", [1, 1], F32, kind="ExternalInput").ap()
    d_Ctab = nc.dram_tensor("Ctab", [NST, KH], F32, kind="ExternalInput").ap()
    d_Stab = nc.dram_tensor("Stab", [NST, KH], F32, kind="ExternalInput").ap()
    d_c128 = nc.dram_tensor("c128", [128, 2], F32, kind="ExternalInput").ap()
    d_ident = nc.dram_tensor("ident", [128, 128], F32, kind="ExternalInput").ap()
    d_lnw = nc.dram_tensor("lnw_rep", [128, D], F32, kind="ExternalInput").ap()
    d_lnb = nc.dram_tensor("lnb_rep", [128, D], F32, kind="ExternalInput").ap()
    d_WCb = nc.dram_tensor("WCb_rep", [128, D], F32, kind="ExternalInput").ap()
    d_Dsk = nc.dram_tensor("Dsk_rep", [128, D], F32, kind="ExternalInput").ap()

    x1s = nc.dram_tensor("x1s", [TLOC, D], F32).ap()
    mrow_scr = nc.dram_tensor("mrow_scr", [1, D], BF16).ap()
    x2s = nc.dram_tensor("x2s", [TLOC, D], F32).ap()
    cc_in = nc.dram_tensor("cc_in", [1, 512], F32)
    cc_out = nc.dram_tensor("cc_out", [world, 512], F32)

    em = Emit(nc)
    stk = contextlib.ExitStack()

    def sb(name, shape, dt):
        return stk.enter_context(nc.sbuf_tensor(name, shape, dt))

    def sem(name):
        return Sem(stk.enter_context(nc.semaphore(name)))

    with stk:
        s_mm = sem("s_mm")
        s_act = sem("s_act")
        s_dve = sem("s_dve")
        s_gp = sem("s_gp")
        s_cc = sem("s_cc")
        s_ld0 = sem("s_ld0")
        s_wl0 = sem("s_wl0")
        lx = [sem(f"lx{j}") for j in range(2)]
        so = [sem(f"so{j}") for j in range(2)]
        wl = [sem(f"wl{j}") for j in range(3)]
        wl2 = [sem(f"wl2{j}") for j in range(2)]
        l1 = [sem(f"l1{j}") for j in range(2)]

        c_identb = sb("c_identb", [128, 128], BF16)
        c_identf = sb("c_identf", [128, 128], F32)
        c_rtW = sb("c_rtW", [128, DK, NE], BF16)
        c_WCT = sb("c_WCT", [128, 2, D], BF16)
        c_C = sb("c_C", [128, 2, KH], BF16)
        c_S = sb("c_S", [128, 2, KH], BF16)
        c_c128 = sb("c_c128", [128, 2], BF16)
        c_Wab = sb("c_Wab", [128, 2], F32)
        c_WBb = sb("c_WBb", [128, 2], BF16)
        c_lnw = sb("c_lnw", [128, D], F32)
        c_lnb = sb("c_lnb", [128, D], F32)
        c_WCb = sb("c_WCb", [128, D], F32)
        c_Dsk = sb("c_Dsk", [128, D], F32)
        c_gamma = sb("c_gamma", [128, 1], F32)
        c_ones = sb("c_ones", [1, 128], F32)
        c_lgam = sb("c_lgam", [1, 1], F32)
        c_sel = sb("c_sel", [world, 1], F32)
        M2reT = sb("M2reT", [128, D], BF16)
        M2imT = sb("M2imT", [128, D], BF16)
        fac = sb("fac", [128, 1], F32)
        fbre = sb("fbre", [128, 1], F32)
        fbim = sb("fbim", [128, 1], F32)
        fb128 = sb("fb128", [1, 1], F32)
        c_eps = sb("c_eps", [128, 1], F32)
        c_two = sb("c_two", [128, 1], F32)

        op, wait, drain = em.op, em.wait, em.drain

        for dst, src in ((c_identf, d_ident), (c_lnw, d_lnw), (c_lnb, d_lnb),
                         (c_WCb, d_WCb), (c_Dsk, d_Dsk)):
            op('y', 'dma_start', out=dst[:, :], in_=src[:, :], inc=s_ld0, k=16)
        op('y', 'dma_start', out=c_lgam[:, :], in_=d_lgam[:, :], inc=s_ld0, k=16)
        op('y', 'dma_start', out=c_sel[:, :], in_=d_sel[:, :], inc=s_ld0, k=16)
        op('y', 'dma_start', out=c_Wab[:, :], in_=d_Wab[:, :], inc=s_ld0, k=16)
        op('g', 'dma_start', out=c_identb[:, :], in_=d_ident[:, :], inc=s_wl0, k=16)
        op('g', 'dma_start', out=c_rtW[:, :, :],
           in_=d_rtW.rearrange("(c p) e -> p c e", p=128), inc=s_wl0, k=16)
        op('g', 'dma_start', out=c_WCT[:, :, :],
           in_=d_WCT.rearrange("(g p) d -> p g d", p=128), inc=s_wl0, k=16)
        op('g', 'dma_start', out=c_C[:, :, :],
           in_=d_Ctab.rearrange("(g p) k -> p g k", p=128), inc=s_wl0, k=16)
        op('g', 'dma_start', out=c_S[:, :, :],
           in_=d_Stab.rearrange("(g p) k -> p g k", p=128), inc=s_wl0, k=16)
        op('g', 'dma_start', out=c_c128[:, :], in_=d_c128[:, :], inc=s_wl0, k=16)
        op('g', 'dma_start', out=c_WBb[:, :], in_=d_WBb[:, :], inc=s_wl0, k=16)
        ld_all = s_ld0.n
        wl_all = s_wl0.n

        wait('s', s_ld0, ld_all)
        op('s', 'activation', c_lgam[:, :], c_lgam[:, :], AF.Sigmoid, inc=s_act)
        op('v', 'memset', c_ones[:, :], 1.0)
        op('v', 'memset', fac[:, :], 2.0)
        op('v', 'memset', c_eps[:, :], LN_EPS)
        op('v', 'memset', c_two[:, :], 2.0)
        drain('v')
        op('v', 'memset', fac[0:1, :], 1.0, inc=s_dve)
        drain('v')

        with nc.psum_tensor("ip0", [128, 512], F32) as ip0, \
             nc.psum_tensor("ip1", [1, 512], F32) as ip1, \
             nc.sbuf_tensor("MrowT", [1, D], BF16) as Mrow:
            wait('t', s_act, s_act.n)
            wait('t', s_dve, s_dve.n)
            wait('t', s_wl0, wl_all)
            wait('t', s_ld0, ld_all)
            op('t', 'matmul', ip0[:, 0:1], c_ones[:, :], c_lgam[:, :], start=True, stop=True,
               inc=s_mm)
            wait('v', s_mm, s_mm.n)
            op('v', 'tensor_copy', c_gamma[:, :], ip0[:, 0:1], inc=s_dve)
            wait('t', s_dve, s_dve.n)
            for (tabs, dstb, sc) in ((c_C, M2reT, 1.0 / NST), (c_S, M2imT, -1.0 / NST)):
                for dt_ in range(2):
                    op('t', 'matmul', ip0[:, :], tabs[:, 0, :], c_WCT[:, 0, dt_ * 512:(dt_ + 1) * 512],
                       start=True, stop=False)
                    op('t', 'matmul', ip0[:, :], tabs[:, 1, :], c_WCT[:, 1, dt_ * 512:(dt_ + 1) * 512],
                       start=False, stop=True, inc=s_mm)
                    wait('v', s_mm, s_mm.n)
                    op('v', 'tensor_scalar', dstb[:, dt_ * 512:(dt_ + 1) * 512], ip0[:, :],
                       fac[:, 0:1], sc, ALU.mult, ALU.mult, inc=s_dve)
                    wait('t', s_dve, s_dve.n)
            for dt_ in range(2):
                op('t', 'matmul', ip1[:, :], c_c128[:, 0:1], c_WCT[:, 0, dt_ * 512:(dt_ + 1) * 512],
                   start=True, stop=False)
                op('t', 'matmul', ip1[:, :], c_c128[:, 1:2], c_WCT[:, 1, dt_ * 512:(dt_ + 1) * 512],
                   start=False, stop=True, inc=s_mm)
                wait('v', s_mm, s_mm.n)
                op('v', 'tensor_scalar', Mrow[0:1, dt_ * 512:(dt_ + 1) * 512], ip1[:, :],
                   1.0 / NST, None, ALU.mult, inc=s_dve)
                wait('t', s_dve, s_dve.n)
            wait('y', s_dve, s_dve.n)
            op('y', 'dma_start', out=mrow_scr[:, :], in_=Mrow[0:1, :], inc=s_ld0, k=16)
            wait('y', s_ld0, s_ld0.n)
            op('t', 'matmul', ip0[:, 0:1], c_C[:, 0, :], c_WBb[:, 0:1], start=True, stop=False)
            op('t', 'matmul', ip0[:, 0:1], c_C[:, 1, :], c_WBb[:, 1:2], start=False, stop=True, inc=s_mm)
            wait('v', s_mm, s_mm.n)
            op('v', 'tensor_copy', fbre[:, :], ip0[:, 0:1], inc=s_dve)
            wait('t', s_dve, s_dve.n)
            op('t', 'matmul', ip0[:, 0:1], c_S[:, 0, :], c_WBb[:, 0:1], start=True, stop=False)
            op('t', 'matmul', ip0[:, 0:1], c_S[:, 1, :], c_WBb[:, 1:2], start=False, stop=True, inc=s_mm)
            wait('v', s_mm, s_mm.n)
            op('v', 'tensor_scalar', fbim[:, :], ip0[:, 0:1], -1.0, None, ALU.mult, inc=s_dve)
            wait('t', s_dve, s_dve.n)
            op('t', 'matmul', ip1[:, 0:1], c_c128[:, 0:1], c_WBb[:, 0:1], start=True, stop=False)
            op('t', 'matmul', ip1[:, 0:1], c_c128[:, 1:2], c_WBb[:, 1:2], start=False, stop=True, inc=s_mm)
            wait('v', s_mm, s_mm.n)
            op('v', 'tensor_copy', fb128[:, :], ip1[0:1, 0:1], inc=s_dve)
            em.run_block()


        def emit_ffn(src, dst, tag, src_events=None, entry=()):
            TGRP = 1024
            NGRP = TLOC // TGRP
            NTM = TGRP // 128
            NTT = TGRP // 512
            NSC = 4
            NHS = 8
            units = [(None, 0), (None, 1)] + [(e, h) for e in range(NE) for h in range(2)]
            NU = len(units)
            store_events = [None] * (TLOC // 128)
            with contextlib.ExitStack() as fs:
                def fsb(name, shape, dt):
                    return fs.enter_context(nc.sbuf_tensor(name, shape, dt))
                xn_fm = fsb("xn_fm" + tag, [128, 2, DK, TGRP], BF16)
                actb = [fsb(f"actb{j}{tag}", [128, NHS, TGRP], BF16) for j in range(2)]
                wg = [fsb(f"wg{j}{tag}", [128, DK, 256], BF16) for j in range(3)]
                wu = [fsb(f"wu{j}{tag}", [128, DK, 256], BF16) for j in range(3)]
                wd = [fsb(f"wd{j}{tag}", [128, NHS, D], BF16) for j in range(2)]
                acc = fsb("acc" + tag, [128, NTM, D], F32)
                xring = fsb("xring" + tag, [128, 2, D], F32)
                oring = fsb("oring" + tag, [128, 2, D], F32)
                xnt = fsb("xnt" + tag, [128, 2, D], BF16)
                silb = [fsb(f"silb{j}{tag}", [128, 512], F32) for j in range(2)]
                uevb = [fsb(f"uevb{j}{tag}", [128, 512], BF16) for j in range(2)]
                cwb = fsb("cwb" + tag, [128, 2, NTM * 8], F32)
                bn6 = fsb("bn6" + tag, [128, 12], F32)
                bn2 = fsb("bn2" + tag, [128, 2], F32)
                sdv = fsb("sdv" + tag, [128, 1], F32)
                rstd = fsb("rstd" + tag, [128, 1], F32)
                lg8 = fsb("lg8" + tag, [128, 8], F32)
                m1 = fsb("m1" + tag, [128, 1], F32)
                nm1 = fsb("nm1" + tag, [128, 1], F32)
                m2 = fsb("m2" + tag, [128, 1], F32)
                dm = fsb("dm" + tag, [128, 1], F32)
                eqm = fsb("eqm" + tag, [128, 8], F32)
                lmb = fsb("lmb" + tag, [128, 8], F32)
                ex8 = fsb("ex8" + tag, [128, 8], F32)
                ed1 = fsb("ed1" + tag, [128, 1], F32)
                den = fsb("den" + tag, [128, 1], F32)
                rden = fsb("rden" + tag, [128, 1], F32)
                cwt = fsb("cwt" + tag, [128, 8], F32)

                with nc.psum_tensor("pGU" + tag, [128, 4, 512], F32) as pGU, \
                     nc.psum_tensor("pDD" + tag, [128, 2, 512], F32) as pD, \
                     nc.psum_tensor("pTR" + tag, [128, DK, 128], BF16) as pTR, \
                     nc.psum_tensor("pRT" + tag, [128, 512], F32) as pRT:

                    rGU = [Res(), Res()]
                    rD = [Res(), Res()]
                    rACT = [Res(), Res()]
                    rSIL = [Res(), Res()]
                    rW = [Res(), Res(), Res()]
                    rWD = [Res(), Res()]
                    rX = [Res(), Res()]
                    rO = [Res(), Res()]
                    rXN = [Res(), Res()]
                    rTR = Res()
                    rRT = Res()

                    for eng in 'tvsgy':
                        for bsem, bval in entry:
                            wait(eng, bsem, bval)

                    ctr = {'x': 0, 'o': 0, 'xn': 0, 'gu': 0, 'dsl': 0}
                    side1 = []
                    side5 = []

                    def pop_side(q):
                        if q:
                            q.pop(0)()

                    def flush(q):
                        while q:
                            q.pop(0)()

                    def load_src(gt):
                        xs = ctr['x'] % 2
                        ctr['x'] += 1
                        em.wait_res('y', rX[xs])
                        if src_events is not None:
                            gsem, gval = src_events[gt]
                            wait('y', gsem, gval)
                        op('y', 'dma_start', out=xring[:, xs, :],
                           in_=src[gt * 128:(gt + 1) * 128, :], inc=lx[xs], k=16)
                        return xs, lx[xs], lx[xs].n

                    def stage1_tile(g, i):
                        def f():
                            gt = g * NTM + i
                            xs, lsem, lval = load_src(gt)
                            wait('v', lsem, lval)
                            drain('v')
                            op('v', 'bn_stats', bn6[:, 0:6], xring[:, xs, 0:512])
                            op('v', 'bn_stats', bn6[:, 6:12], xring[:, xs, 512:1024])
                            drain('v')
                            op('v', 'bn_aggr', bn2[:, :], bn6[:, :], inc=s_dve)
                            wait('s', s_dve, s_dve.n)
                            op('s', 'activation', sdv[:, :], bn2[:, 1:2], AF.Sqrt,
                               bias=c_eps[:, 0:1], inc=s_act)
                            wait('v', s_act, s_act.n)
                            op('v', 'reciprocal', rstd[:, :], sdv[:, :])
                            drain('v')
                            ns = ctr['xn'] % 2
                            ctr['xn'] += 1
                            em.wait_res('v', rXN[ns])
                            op('v', 'tensor_scalar', xnt[:, ns, :], xring[:, xs, :],
                               bn2[:, 0:1], rstd[:, 0:1], ALU.subtract, ALU.mult, inc=s_dve)
                            rX[xs].ready(s_dve)
                            xn_ready = s_dve.n
                            wait('t', s_dve, xn_ready)
                            em.wait_res('t', rTR)
                            for dk in range(DK):
                                op('t', 'transpose', pTR[:, dk, :],
                                   xnt[:, ns, dk * 128:(dk + 1) * 128], c_identb[:, :],
                                   inc=(s_mm if dk == DK - 1 else None))
                            wait('v', s_mm, s_mm.n)
                            op('v', 'tensor_copy', xn_fm[:, g % 2, :, i * 128:(i + 1) * 128],
                               pTR[:, :, :], inc=s_dve)
                            rTR.ready(s_dve)
                            rXN[ns].ready(s_dve)
                            xnfm_tile_done = s_dve.n
                            em.wait_res('t', rRT)
                            wait('t', s_dve, xnfm_tile_done)
                            for dk in range(DK):
                                op('t', 'matmul', pRT[:, 0:8],
                                   xn_fm[:, g % 2, dk, i * 128:(i + 1) * 128], c_rtW[:, dk, :],
                                   start=(dk == 0), stop=(dk == DK - 1),
                                   inc=(s_mm if dk == DK - 1 else None))
                            wait('v', s_mm, s_mm.n)
                            drain('v')
                            op('v', 'tensor_copy', lg8[:, :], pRT[:, 0:8], inc=s_dve)
                            rRT.ready(s_dve)
                            drain('v')
                            op('v', 'tensor_reduce', m1[:, :], lg8[:, :], AX.X, ALU.max)
                            drain('v')
                            op('v', 'tensor_scalar', eqm[:, :], lg8[:, :], m1[:, 0:1], None, ALU.is_ge)
                            op('v', 'tensor_scalar', nm1[:, :], m1[:, :], -1.0, None, ALU.mult)
                            drain('v')
                            op('v', 'scalar_tensor_tensor', lmb[:, :], eqm[:, :], -1e30, lg8[:, :],
                               ALU.mult, ALU.add)
                            drain('v')
                            op('v', 'tensor_reduce', m2[:, :], lmb[:, :], AX.X, ALU.max)
                            drain('v')
                            op('v', 'tensor_tensor', dm[:, :], m2[:, :], m1[:, :], ALU.subtract,
                               inc=s_dve)
                            wait('s', s_dve, s_dve.n)
                            op('s', 'activation', ex8[:, :], lg8[:, :], AF.Exp,
                               bias=nm1[:, 0:1], scale=1.0)
                            op('s', 'activation', ed1[:, :], dm[:, :], AF.Exp, inc=s_act)
                            wait('v', s_act, s_act.n)
                            op('v', 'tensor_scalar', den[:, :], ed1[:, :], 1.0, None, ALU.add)
                            drain('v')
                            op('v', 'reciprocal', rden[:, :], den[:, :])
                            op('v', 'tensor_scalar', eqm[:, :], lg8[:, :], m2[:, 0:1], None, ALU.is_ge)
                            drain('v')
                            op('v', 'tensor_tensor', cwt[:, :], ex8[:, :], eqm[:, :], ALU.mult)
                            drain('v')
                            op('v', 'tensor_scalar', cwb[:, g % 2, i * 8:(i + 1) * 8], cwt[:, :],
                               rden[:, 0:1], 0.5, ALU.mult, ALU.mult, inc=s_dve)
                        return f

                    def stage5_tile(g, tm, acc_done):
                        def f():
                            gt = g * NTM + tm
                            xs, lsem, lval = load_src(gt)
                            os_ = ctr['o'] % 2
                            ctr['o'] += 1
                            wait('v', lsem, lval)
                            em.wait_res('v', rO[os_])
                            drain('v')
                            op('v', 'tensor_tensor', oring[:, os_, :], xring[:, xs, :],
                               acc[:, tm, :], ALU.add, inc=s_dve)
                            rX[xs].ready(s_dve)
                            wait('y', s_dve, s_dve.n)
                            op('y', 'dma_start', out=dst[gt * 128:(gt + 1) * 128, :],
                               in_=oring[:, os_, :], inc=so[os_], k=16)
                            store_events[gt] = (so[os_], so[os_].n)
                            rO[os_].ready(so[os_])
                        return f

                    guw_ev = {}
                    wd_ev = {}
                    GSEQ_MAX = NGRP * NU * NSC
                    USEQ_MAX = NGRP * NU

                    def sched_guw(seq):
                        if seq >= GSEQ_MAX:
                            return
                        u = (seq // NSC) % NU
                        sc = seq % NSC
                        slot = seq % 3
                        e, half = units[u]
                        h0 = half * 1024 + sc * 256
                        if e is None:
                            gsrc = d_sgT.rearrange("(c p) h -> p c h", p=128)[:, :, h0:h0 + 256]
                            usrc = d_suT.rearrange("(c p) h -> p c h", p=128)[:, :, h0:h0 + 256]
                        else:
                            gsrc = d_egT[e].rearrange("(c p) h -> p c h", p=128)[:, :, h0:h0 + 256]
                            usrc = d_euT[e].rearrange("(c p) h -> p c h", p=128)[:, :, h0:h0 + 256]
                        em.wait_res('g', rW[slot])
                        op('g', 'dma_start', out=wg[slot][:, :, :], in_=gsrc, inc=wl[slot], k=16)
                        op('g', 'dma_start', out=wu[slot][:, :, :], in_=usrc, inc=wl[slot], k=16)
                        guw_ev[seq] = (slot, wl[slot], wl[slot].n)

                    def sched_wd(useq):
                        if useq >= USEQ_MAX:
                            return
                        u = useq % NU
                        slot = useq % 2
                        e, half = units[u]
                        if e is None:
                            dsrc = d_sdT.rearrange("(c p) d -> p c d", p=128)[:, half * 8:(half + 1) * 8, :]
                        else:
                            dsrc = d_edT[e].rearrange("(c p) d -> p c d", p=128)[:, half * 8:(half + 1) * 8, :]
                        em.wait_res('g', rWD[slot])
                        op('g', 'dma_start', out=wd[slot][:, :, :], in_=dsrc, inc=wl2[slot], k=16)
                        wd_ev[useq] = (slot, wl2[slot], wl2[slot].n)

                    act_done = [0, 0]

                    def emit_down(g, u):
                        e, half = units[u]
                        uslot = u % 2
                        wdslot, wdsem, wdval = wd_ev[g * NU + u]
                        wait('t', wdsem, wdval)
                        wait('t', s_dve, act_done[uslot])

                        for tm in range(NTM):
                            for dt_ in range(2):
                                pp = ctr['dsl'] % 2
                                ctr['dsl'] += 1
                                em.wait_res('t', rD[pp])
                                for hsub in range(NHS):
                                    op('t', 'matmul', pD[:, pp, :],
                                       actb[uslot][:, hsub, tm * 128:(tm + 1) * 128],
                                       wd[wdslot][:, hsub, dt_ * 512:(dt_ + 1) * 512],
                                       start=(hsub == 0), stop=(hsub == NHS - 1),
                                       inc=(s_mm if hsub == NHS - 1 else None))
                                wait('v', s_mm, s_mm.n)
                                aslab = acc[:, tm, dt_ * 512:(dt_ + 1) * 512]
                                if u == 0:
                                    op('v', 'tensor_scalar', aslab, pD[:, pp, :],
                                       0.5, None, ALU.mult, inc=s_dve)
                                elif e is None:
                                    op('v', 'scalar_tensor_tensor', aslab, pD[:, pp, :],
                                       0.5, aslab, ALU.mult, ALU.add, inc=s_dve)
                                else:
                                    col = cwb[:, g % 2, tm * 8 + e: tm * 8 + e + 1]
                                    op('v', 'scalar_tensor_tensor', aslab, pD[:, pp, :],
                                       col, aslab, ALU.mult, ALU.add, inc=s_dve)
                                rD[pp].ready(s_dve)
                                pop_side(side1)
                        rACT[uslot].ready(s_mm, s_mm.n)
                        rWD[wdslot].ready(s_mm, s_mm.n)
                        sched_wd(g * NU + u + 2)

                    for sq in range(3):
                        sched_guw(sq)
                    sched_wd(0)
                    sched_wd(1)
                    for i in range(NTM):
                        side1.append(stage1_tile(0, i))
                    flush(side1)

                    for g in range(NGRP):
                        xn_all = s_dve.n
                        for u in range(NU):
                            e, half = units[u]
                            uslot = u % 2
                            first_mult = True
                            for sc in range(NSC):
                                seq = (g * NU + u) * NSC + sc
                                slot, wsem, wval = guw_ev[seq]
                                wait('t', wsem, wval)
                                for hs in range(2):
                                    hsub = sc * 2 + hs
                                    for tt in range(NTT):
                                        pp = ctr['gu'] % 2
                                        ctr['gu'] += 1
                                        em.wait_res('t', rGU[pp])
                                        if u == 0 and sc == 0 and hs == 0 and tt == 0:
                                            wait('t', s_dve, xn_all)
                                        for dk in range(DK):
                                            op('t', 'matmul', pGU[:, 2 * pp, :],
                                               wg[slot][:, dk, hs * 128:(hs + 1) * 128],
                                               xn_fm[:, g % 2, dk, tt * 512:(tt + 1) * 512],
                                               start=(dk == 0), stop=(dk == DK - 1),
                                               inc=(s_mm if dk == DK - 1 else None))
                                        gval = s_mm.n
                                        for dk in range(DK):
                                            op('t', 'matmul', pGU[:, 2 * pp + 1, :],
                                               wu[slot][:, dk, hs * 128:(hs + 1) * 128],
                                               xn_fm[:, g % 2, dk, tt * 512:(tt + 1) * 512],
                                               start=(dk == 0), stop=(dk == DK - 1),
                                               inc=(s_mm if dk == DK - 1 else None))
                                        uval = s_mm.n
                                        wait('s', s_mm, gval)
                                        em.wait_res('s', rSIL[pp])
                                        op('s', 'activation', silb[pp][:, :], pGU[:, 2 * pp, :],
                                           AF.Silu)
                                        wait('s', s_mm, uval)
                                        op('s', 'activation', uevb[pp][:, :], pGU[:, 2 * pp + 1, :],
                                           AF.Identity, inc=s_act)
                                        rGU[pp].ready(s_act)
                                        wait('v', s_act, s_act.n)
                                        if first_mult:
                                            em.wait_res('v', rACT[uslot])
                                            first_mult = False
                                        op('v', 'tensor_tensor',
                                           actb[uslot][:, hsub, tt * 512:(tt + 1) * 512],
                                           silb[pp][:, :], uevb[pp][:, :], ALU.mult,
                                           inc=s_dve)
                                        rSIL[pp].ready(s_dve)
                                        pop_side(side5)
                                rW[slot].ready(s_mm, s_mm.n)
                                sched_guw(seq + 3)
                            act_done[uslot] = s_dve.n
                            if u == 12 and g + 1 < NGRP:
                                for i in range(NTM):
                                    side1.append(stage1_tile(g + 1, i))
                            if u > 0:
                                if u == 1:
                                    flush(side5)
                                emit_down(g, u - 1)
                        emit_down(g, NU - 1)
                        flush(side1)
                        acc_done = s_dve.n
                        for tm in range(NTM):
                            side5.append(stage5_tile(g, tm, acc_done))
                    flush(side5)
                    em.run_block()
            return store_events

        def emit_attn(dst, x1_events, entry=()):
            store_events = []
            with contextlib.ExitStack() as fs:
                def fsb(name, shape, dt):
                    return fs.enter_context(nc.sbuf_tensor(name, shape, dt))
                sc_ar = fsb("sc_ar", [128, TLOC], F32)
                sc_ai = fsb("sc_ai", [128, TLOC], F32)
                sc_ur = fsb("sc_ur", [128, TLOC], F32)
                sc_ui = fsb("sc_ui", [128, TLOC], F32)
                hrebf = fsb("hrebf", [128, TLOC], BF16)
                himbf = fsb("himbf", [128, TLOC], BF16)
                a128b = fsb("a128b", [1, TLOC], F32)
                u128b = fsb("u128b", [1, TLOC], F32)
                h128bf = fsb("h128bf", [1, TLOC], BF16)
                r128 = [fsb(f"r128{q}", [1, 1], F32) for q in range(2)]
                c0re = fsb("c0re", [128, 1], F32)
                c0im = fsb("c0im", [128, 1], F32)
                c128c = fsb("c128c", [1, 1], F32)

                bnm = fsb("bnm", [128, NTILE], F32)
                bnr = fsb("bnr", [128, NTILE], F32)
                pngG = [[fsb(f"pgg{j}{q}", [128, CH], F32) for q in range(4)] for j in range(2)]
                tmpG = [fsb(f"tmg{q}", [128, CH], F32) for q in range(2)]

                with nc.psum_tensor("pTRB", [128, DK, 128], BF16) as pTR, \
                     nc.psum_tensor("pPJ", [128, 4, 512], F32) as pPJ:

                    rTR = Res()
                    rPJ = [Res(), Res(), Res(), Res()]
                    rFF = [Res(), Res()]
                    rYp = [Res(), Res()]
                    rX1 = [Res(), Res()]
                    rZ = [Res(), Res()]
                    rO = [Res(), Res()]
                    rSCR = Res()

                    for eng in 'tvsgy':
                        for bsem, bval in entry:
                            wait(eng, bsem, bval)

                    with contextlib.ExitStack() as f1, \
                         nc.psum_tensor("pFF", [128, 2, 512], F32) as pFF:
                        def f1b(name, shape, dt):
                            return f1.enter_context(nc.sbuf_tensor(name, shape, dt))
                        c_WaT = f1b("c_WaT", [128, DK, NST], BF16)
                        c_WBT = f1b("c_WBT", [128, DK, NST], BF16)
                        op('g', 'dma_start', out=c_WaT[:, :, :],
                           in_=d_WaT.rearrange("(c p) n -> p c n", p=128), inc=s_ld0, k=16)
                        op('g', 'dma_start', out=c_WBT[:, :, :],
                           in_=d_WBT.rearrange("(c p) n -> p c n", p=128), inc=s_ld0, k=16)
                        waW_val = s_ld0.n
                        wait('t', s_ld0, waW_val)
                        x1ring = f1b("x1ringF", [128, 2, D], F32)
                        zfr = f1b("zfrF", [128, 2, D], F32)
                        xntr = f1b("xntrF", [128, 2, D], BF16)
                        xfa = [f1b(f"xfa{j}", [128, DK, 128], BF16) for j in range(2)]
                        abf = f1b("abf", [128, 2, 128], BF16)
                        ubf = f1b("ubf", [128, 2, 128], BF16)
                        tanb = f1b("tanb", [128, 2, 128], BF16)
                        t1b = f1b("t1b", [128, D], F32)
                        g1b = f1b("g1b", [128, 128], F32)
                        g2b = f1b("g2b", [128, 128], F32)
                        ggb = g1b
                        bn6 = f1b("bn6B", [128, 12], F32)
                        bn6b = bn6
                        bn2r = f1b("bn2rB", [128, 2, 2], F32)
                        bn2br = f1b("bn2brB", [128, 2, 2], F32)
                        sdq = f1b("sdqB", [128, 4, 1], F32)
                        sdvr = sdq[:, 0:2, :]
                        sdv2r = sdq[:, 2:4, :]
                        rstd2 = f1b("rstd2B", [128, 1], F32)
                        row3 = f1b("row3", [1, 384], F32)
                        ab128 = row3[0:1, 0:128]
                        gg128 = row3[0:1, 128:256]
                        raw128 = row3[0:1, 256:384]
                        ggv = [0, 0, 0]

                        rZF = [Res(), Res()]
                        rXN4 = [Res() for _ in range(4)]
                        rFA = [Res(), Res()]
                        rB2 = [Res(), Res()]
                        rB2b = [Res(), Res()]
                        rSD = [Res(), Res()]
                        rSD2 = [Res(), Res()]
                        rTAN = [Res(), Res()]

                        sq1v = [0] * NTILE
                        sq2v = [0] * NTILE
                        xnt_done = [0] * NTILE
                        aval = [None] * NTILE
                        uval = [None] * NTILE

                        def A1(i):
                            xs = i % 2
                            em.wait_res('y', rX1[xs])
                            gsem, gval = x1_events[i]
                            wait('y', gsem, gval)
                            op('y', 'dma_start', out=x1ring[:, xs, :],
                               in_=x1s[i * 128:(i + 1) * 128, :], inc=l1[xs], k=16)
                            wait('v', l1[xs], l1[xs].n)
                            drain('v')
                            op('v', 'bn_stats', bn6[:, 0:6], x1ring[:, xs, 0:512])
                            op('v', 'bn_stats', bn6[:, 6:12], x1ring[:, xs, 512:1024])
                            drain('v')
                            em.wait_res('v', rB2[xs])
                            op('v', 'bn_aggr', bn2r[:, xs, :], bn6[:, :], inc=s_dve)
                            wait('s', s_dve, s_dve.n)
                            em.wait_res('s', rSD[xs])
                            op('s', 'activation', sdvr[:, xs, :], bn2r[:, xs, 1:2], AF.Sqrt,
                               bias=c_eps[:, 0:1], inc=s_act)
                            sq1v[i] = s_act.n

                        def A2(i):
                            xs = i % 2
                            zs = i % 2
                            wait('v', s_act, sq1v[i])
                            op('v', 'reciprocal', bnr[:, i:i + 1], sdvr[:, xs, :])
                            drain('v')
                            op('v', 'tensor_copy', bnm[:, i:i + 1], bn2r[:, xs, 0:1], inc=s_dve)
                            rSD[xs].ready(s_dve)
                            rB2[xs].ready(s_dve)
                            em.wait_res('v', rZF[zs])
                            drain('v')
                            op('v', 'tensor_scalar', zfr[:, zs, :], x1ring[:, xs, :],
                               bnm[:, i:i + 1], bnr[:, i:i + 1], ALU.subtract, ALU.mult, inc=s_dve)
                            rX1[xs].ready(s_dve)
                            drain('v')
                            op('v', 'bn_stats', bn6b[:, 0:6], zfr[:, zs, 0:512])
                            op('v', 'bn_stats', bn6b[:, 6:12], zfr[:, zs, 512:1024])
                            drain('v')
                            em.wait_res('v', rB2b[xs])
                            op('v', 'bn_aggr', bn2br[:, xs, :], bn6b[:, :], inc=s_dve)
                            wait('s', s_dve, s_dve.n)
                            em.wait_res('s', rSD2[xs])
                            op('s', 'activation', sdv2r[:, xs, :], bn2br[:, xs, 1:2], AF.Sqrt,
                               bias=c_eps[:, 0:1], inc=s_act)
                            sq2v[i] = s_act.n

                        def A3(i):
                            xs = i % 2
                            zs = i % 2
                            ns = i % 2
                            wait('v', s_act, sq2v[i])
                            op('v', 'reciprocal', rstd2[:, :], sdv2r[:, xs, :], inc=s_dve)
                            rSD2[xs].ready(s_dve)
                            drain('v')
                            op('v', 'tensor_scalar', t1b[:, :], zfr[:, zs, :],
                               bn2br[:, xs, 0:1], rstd2[:, 0:1], ALU.subtract, ALU.mult, inc=s_dve)
                            rB2b[xs].ready(s_dve)
                            drain('v')
                            op('v', 'tensor_tensor', zfr[:, zs, :], t1b[:, :], c_lnw[:, :], ALU.mult)
                            drain('v')
                            em.wait_res('v', rXN4[ns])
                            op('v', 'tensor_tensor', xntr[:, ns, :], zfr[:, zs, :], c_lnb[:, :], ALU.add,
                               inc=s_dve)
                            rZF[zs].ready(s_dve)
                            xnt_done[i] = s_dve.n

                        def B1b1(i):
                            ns = i % 2
                            fa = i % 2
                            wait('t', s_dve, xnt_done[i])
                            em.wait_res('t', rTR)
                            for dk in range(DK):
                                op('t', 'transpose', pTR[:, dk, :],
                                   xntr[:, ns, dk * 128:(dk + 1) * 128], c_identb[:, :],
                                   inc=(s_mm if dk == DK - 1 else None))
                            wait('v', s_mm, s_mm.n)
                            em.wait_res('v', rFA[fa])
                            op('v', 'tensor_copy', xfa[fa][:, :, :], pTR[:, :, :], inc=s_dve)
                            rTR.ready(s_dve)
                            rXN4[ns].ready(s_dve)
                            fcv = s_dve.n
                            wait('t', s_dve, fcv)
                            av = []
                            for ng in range(2):
                                em.wait_res('t', rPJ[ng])
                                for dk in range(DK):
                                    op('t', 'matmul', pPJ[:, ng, 0:128],
                                       c_WaT[:, dk, ng * 128:(ng + 1) * 128], xfa[fa][:, dk, :],
                                       start=(dk == 0), stop=(dk == DK - 1),
                                       inc=(s_mm if dk == DK - 1 else None))
                                av.append(s_mm.n)
                            uv = []
                            for ng in range(2):
                                em.wait_res('t', rPJ[2 + ng])
                                for dk in range(DK):
                                    op('t', 'matmul', pPJ[:, 2 + ng, 0:128],
                                       c_WBT[:, dk, ng * 128:(ng + 1) * 128], xfa[fa][:, dk, :],
                                       start=(dk == 0), stop=(dk == DK - 1),
                                       inc=(s_mm if dk == DK - 1 else None))
                                uv.append(s_mm.n)
                            aval[i] = av
                            uval[i] = uv
                            rFA[fa].ready(s_mm, s_mm.n)

                        def B1b2(i):
                            toff = i * 128
                            for ng in range(2):
                                wait('s', s_mm, aval[i][ng])
                                em.wait_res('s', rTAN[ng])
                                op('s', 'activation', tanb[:, ng, :], pPJ[:, ng, 0:128], AF.Tanh,
                                   bias=c_Wab[:, ng:ng + 1], scale=1.0, inc=s_act)
                            tanv = s_act.n
                            wait('v', s_act, tanv)
                            op('v', 'tensor_scalar', abf[:, 0, :], tanb[:, 0, :],
                               c_gamma[:, 0:1], None, ALU.mult)
                            op('v', 'tensor_scalar', abf[:, 1, :], tanb[:, 1, :],
                               c_gamma[:, 0:1], None, ALU.mult, inc=s_dve)
                            rTAN[0].ready(s_dve)
                            rTAN[1].ready(s_dve)
                            rPJ[1].ready(s_dve)
                            abf_done = s_dve.n
                            wait('v', s_mm, uval[i][1])
                            drain('v')
                            op('v', 'tensor_copy', ubf[:, 0, :], pPJ[:, 2, 0:128])
                            op('v', 'tensor_copy', ubf[:, 1, :], pPJ[:, 3, 0:128], inc=s_dve)
                            rPJ[3].ready(s_dve)
                            ubf_done = s_dve.n
                            em.wait_res('t', rFF[0])
                            em.wait_res('t', rFF[1])
                            wait('t', s_dve, abf_done)
                            op('t', 'matmul', pFF[:, 0, 0:128], c_C[:, 0, :], abf[:, 0, :], start=True, stop=False)
                            op('t', 'matmul', pFF[:, 0, 0:128], c_C[:, 1, :], abf[:, 1, :], start=False, stop=True)
                            op('t', 'matmul', pFF[:, 1, 0:128], c_S[:, 0, :], abf[:, 0, :], start=True, stop=False)
                            op('t', 'matmul', pFF[:, 1, 0:128], c_S[:, 1, :], abf[:, 1, :], start=False, stop=True)
                            op('t', 'matmul', pPJ[0:1, 0, 0:128], c_c128[:, 0:1], abf[:, 0, :], start=True, stop=False)
                            op('t', 'matmul', pPJ[0:1, 0, 0:128], c_c128[:, 1:2], abf[:, 1, :], start=False, stop=True,
                               inc=s_mm)
                            ffta = s_mm.n
                            wait('s', s_mm, ffta)
                            if ggv[0]:
                                wait('s', s_dve, ggv[0])
                            op('s', 'activation', g1b[:, :], pFF[:, 0, 0:128], AF.Square)
                            op('s', 'activation', g2b[:, :], pFF[:, 1, 0:128], AF.Square, inc=s_act)
                            wait('v', s_act, s_act.n)
                            op('v', 'tensor_tensor', g1b[:, :], g1b[:, :], g2b[:, :], ALU.add, inc=s_dve)
                            wait('s', s_dve, s_dve.n)
                            op('s', 'activation', g2b[:, :], g1b[:, :], AF.Sqrt, inc=s_act)
                            wait('v', s_act, s_act.n)
                            op('v', 'tensor_scalar', g2b[:, :], g2b[:, :], 1e-8, None, ALU.max, inc=s_dve)
                            ggv[0] = s_dve.n
                            wait('s', s_dve, s_dve.n)
                            if ggv[2]:
                                wait('s', s_dve, ggv[2])
                            op('s', 'activation', ggb[:, :], g2b[:, :], AF.Sigmoid, bias=c_two[:, 0:1], scale=-2.0,
                               inc=s_act)
                            ggend = s_act.n
                            wait('v', s_act, ggend)
                            op('v', 'tensor_tensor', sc_ar[:, toff:toff + 128], pFF[:, 0, 0:128],
                               ggb[:, :], ALU.mult)
                            op('v', 'scalar_tensor_tensor', sc_ai[:, toff:toff + 128], pFF[:, 1, 0:128],
                               -1.0, ggb[:, :], ALU.mult, ALU.mult, inc=s_dve)
                            ggv[2] = s_dve.n
                            rFF[0].ready(s_dve)
                            rFF[1].ready(s_dve)
                            wait('s', s_mm, ffta)
                            if ggv[1]:
                                wait('s', s_dve, ggv[1])
                            op('s', 'activation', ab128, pPJ[0:1, 0, 0:128], AF.Abs)
                            op('s', 'activation', raw128, pPJ[0:1, 0, 0:128], AF.Identity, inc=s_act)
                            wait('v', s_act, s_act.n)
                            op('v', 'tensor_scalar', ab128, ab128, 1e-8, None, ALU.max, inc=s_dve)
                            wait('s', s_dve, s_dve.n)
                            op('s', 'activation', gg128, ab128, AF.Sigmoid, bias=c_two[0:1, 0:1], scale=-2.0,
                               inc=s_act)
                            wait('v', s_act, s_act.n)
                            op('v', 'tensor_tensor', a128b[0:1, toff:toff + 128], raw128,
                               gg128, ALU.mult, inc=s_dve)
                            ggv[1] = s_dve.n
                            rPJ[0].ready(s_dve)
                            em.wait_res('t', rFF[0])
                            em.wait_res('t', rFF[1])
                            wait('t', s_dve, ubf_done)
                            op('t', 'matmul', pFF[:, 0, 0:128], c_C[:, 0, :], ubf[:, 0, :], start=True, stop=False)
                            op('t', 'matmul', pFF[:, 0, 0:128], c_C[:, 1, :], ubf[:, 1, :], start=False, stop=True)
                            op('t', 'matmul', pFF[:, 1, 0:128], c_S[:, 0, :], ubf[:, 0, :], start=True, stop=False)
                            op('t', 'matmul', pFF[:, 1, 0:128], c_S[:, 1, :], ubf[:, 1, :], start=False, stop=True)
                            op('t', 'matmul', pPJ[0:1, 2, 0:128], c_c128[:, 0:1], ubf[:, 0, :], start=True, stop=False)
                            op('t', 'matmul', pPJ[0:1, 2, 0:128], c_c128[:, 1:2], ubf[:, 1, :], start=False, stop=True,
                               inc=s_mm)
                            fftu = s_mm.n
                            wait('s', s_mm, fftu)
                            op('s', 'activation', sc_ur[:, toff:toff + 128], pFF[:, 0, 0:128],
                               AF.Identity, bias=fbre[:, 0:1], scale=1.0)
                            op('s', 'activation', sc_ui[:, toff:toff + 128], pFF[:, 1, 0:128],
                               AF.Identity, bias=fbim[:, 0:1], scale=-1.0)
                            op('s', 'activation', u128b[0:1, toff:toff + 128], pPJ[0:1, 2, 0:128],
                               AF.Identity, bias=fb128[0:1, 0:1], scale=1.0, inc=s_act)
                            rFF[0].ready(s_act)
                            rFF[1].ready(s_act)
                            rPJ[2].ready(s_act)

                        chunk_rdy = [None] * NCHK
                        for r in range(NTILE + 4):
                            if r < NTILE:
                                A1(r)
                            if 1 <= r < NTILE + 1:
                                A2(r - 1)
                            if 2 <= r < NTILE + 2:
                                A3(r - 2)
                            if 4 <= r:
                                B1b2(r - 4)
                                if (r - 4) % 4 == 3:
                                    chunk_rdy[(r - 4) // 4] = (s_dve.n, s_act.n)
                            if 3 <= r < NTILE + 3:
                                B1b1(r - 3)
                        front_done = s_dve.n
                        front_act = s_act.n
                        front_gp = s_gp.n
                        front_mm = s_mm.n
                        em.run_block()

                    with contextlib.ExitStack() as f2, \
                         nc.psum_tensor("pYB", [128, 3, 512], F32) as pY:
                        def f2b(name, shape, dt):
                            return f2.enter_context(nc.sbuf_tensor(name, shape, dt))
                        scr128 = f2b("scr128", [1, CH], F32)
                        Fre = f2b("Fre", [128, 1], F32)
                        Fim = f2b("Fim", [128, 1], F32)
                        nFre = f2b("nFre", [128, 1], F32)
                        nFim = f2b("nFim", [128, 1], F32)
                        tc4 = [f2b(f"tc{q}", [128, 1], F32) for q in range(4)]
                        ccre = [f2b(f"ccre{q}", [128, 1], F32) for q in range(2)]
                        ccim = [f2b(f"ccim{q}", [128, 1], F32) for q in range(2)]
                        ncim = f2b("ncim", [128, 1], F32)
                        f2a = f2.enter_context(contextlib.ExitStack())
                        def f2ab(name, shape, dt):
                            return f2a.enter_context(nc.sbuf_tensor(name, shape, dt))
                        pngV = [[f2ab(f"pgv{j}{q}", [128, PAD + CH], F32) for q in range(4)] for j in range(2)]
                        tmpV = [f2ab(f"tmv{q}", [128, CH], F32) for q in range(2)]

                        u_fin = [0] * NCHK

                        def scan_chunks(eng, sem_, png, tmp4, chunks, pad):
                            if pad:
                                for j in range(2):
                                    op(eng, 'memset', png[j][0][:, 0:pad], 1.0)
                                    op(eng, 'memset', png[j][1][:, 0:pad], 0.0)
                                    op(eng, 'memset', png[j][2][:, 0:pad], 0.0)
                                    op(eng, 'memset', png[j][3][:, 0:pad], 0.0)
                                drain(eng)
                            for c in chunks:
                                cs = slice(c * CH, (c + 1) * CH)
                                rv, rs = chunk_rdy[c]
                                wait(eng, s_dve, rv)
                                wait(eng, s_act, rs)
                                for q, srcb in enumerate((sc_ar, sc_ai, sc_ur, sc_ui)):
                                    op(eng, 'tensor_copy', png[0][q][:, pad:], srcb[:, cs])
                                drain(eng)
                                cur = 0
                                s_ = 1
                                for lev in range(NLEV):
                                    A = png[cur]
                                    B = png[1 - cur]
                                    if pad:
                                        cu = slice(pad, pad + CH)
                                        sh = slice(pad - s_, pad - s_ + CH)
                                        TL = CH
                                        pre = None
                                    else:
                                        cu = slice(s_, CH)
                                        sh = slice(0, CH - s_)
                                        TL = CH - s_
                                        pre = slice(0, s_)
                                    op(eng, 'tensor_tensor', B[0][:, cu], A[0][:, cu], A[0][:, sh], ALU.mult)
                                    op(eng, 'tensor_tensor', tmp4[0][:, 0:TL], A[1][:, cu], A[1][:, sh], ALU.mult)
                                    op(eng, 'tensor_tensor', B[1][:, cu], A[0][:, cu], A[1][:, sh], ALU.mult)
                                    op(eng, 'tensor_tensor', tmp4[1][:, 0:TL], A[1][:, cu], A[0][:, sh], ALU.mult)
                                    drain(eng)
                                    op(eng, 'tensor_tensor', B[0][:, cu], B[0][:, cu], tmp4[0][:, 0:TL], ALU.subtract)
                                    op(eng, 'tensor_tensor', B[1][:, cu], B[1][:, cu], tmp4[1][:, 0:TL], ALU.add)
                                    op(eng, 'tensor_tensor', B[2][:, cu], A[0][:, cu], A[2][:, sh], ALU.mult)
                                    op(eng, 'tensor_tensor', tmp4[0][:, 0:TL], A[1][:, cu], A[3][:, sh], ALU.mult)
                                    op(eng, 'tensor_tensor', B[3][:, cu], A[0][:, cu], A[3][:, sh], ALU.mult)
                                    op(eng, 'tensor_tensor', tmp4[1][:, 0:TL], A[1][:, cu], A[2][:, sh], ALU.mult)
                                    drain(eng)
                                    op(eng, 'tensor_tensor', B[2][:, cu], B[2][:, cu], tmp4[0][:, 0:TL], ALU.subtract)
                                    op(eng, 'tensor_tensor', B[3][:, cu], B[3][:, cu], tmp4[1][:, 0:TL], ALU.add)
                                    drain(eng)
                                    op(eng, 'tensor_tensor', B[2][:, cu], B[2][:, cu], A[2][:, cu], ALU.add)
                                    op(eng, 'tensor_tensor', B[3][:, cu], B[3][:, cu], A[3][:, cu], ALU.add)
                                    if pre is not None:
                                        for q in range(4):
                                            op(eng, 'tensor_copy', B[q][:, pre], A[q][:, pre])
                                    drain(eng)
                                    cur = 1 - cur
                                    s_ *= 2
                                Afin = png[cur]
                                op(eng, 'tensor_copy', sc_ar[:, cs], Afin[0][:, pad:])
                                op(eng, 'tensor_copy', sc_ai[:, cs], Afin[1][:, pad:])
                                op(eng, 'tensor_copy', sc_ur[:, cs], Afin[2][:, pad:])
                                op(eng, 'tensor_copy', sc_ui[:, cs], Afin[3][:, pad:], inc=sem_)
                                u_fin[c] = sem_.n
                                drain(eng)

                        scan_chunks('g', s_gp, pngG, tmpG, [0, 1, 2, 3], 0)
                        wait('v', s_dve, front_done)
                        wait('v', s_act, front_act)
                        wait('v', s_mm, front_mm)
                        scan_chunks('v', s_dve, pngV, tmpV, [4, 5, 6, 7], PAD)
                        f2a.close()
                        agrow = f2b("agrow", [1, 512], F32)
                        gath = f2b("gath", [world, 512], F32)
                        selrow = f2b("selrow", [1, 512], F32)
                        Mrow = f2b("Mrow3", [1, D], BF16)
                        x1ring3 = f2b("x1ring3", [128, 2, D], F32)
                        obuf = f2b("obufB", [128, 2, D], F32)
                        zt = f2b("ztB", [128, 512], F32)
                        zf = f2b("zfB", [128, 2, D], F32)
                        wait('y', s_gp, s_gp.n)
                        wait('y', s_dve, s_dve.n)
                        op('y', 'dma_start', out=Mrow[0:1, :], in_=mrow_scr[:, :], inc=s_ld0, k=16)
                        wait('t', s_ld0, s_ld0.n)
                        wait('v', s_gp, front_gp)
                        wait('v', s_act, front_act)
                        for c in range(NCHK):
                            cs = slice(c * CH, (c + 1) * CH)
                            init = 0.0 if c == 0 else r128[0][0:1, 0:1]
                            op('v', 'tensor_tensor_scan', scr128[0:1, :], a128b[0:1, cs], u128b[0:1, cs],
                               init, ALU.mult, ALU.add)
                            drain('v')
                            op('v', 'tensor_copy', r128[0][0:1, :], scr128[0:1, CH - 1:CH])
                            drain('v')
                        wait('v', s_gp, u_fin[3])
                        op('v', 'tensor_copy', Fre[:, :], sc_ur[:, CH - 1:CH])
                        op('v', 'tensor_copy', Fim[:, :], sc_ui[:, CH - 1:CH])
                        drain('v')
                        for c in range(1, NCHK):
                            le = (c + 1) * CH - 1
                            ALr = sc_ar[:, le:le + 1]
                            ALi = sc_ai[:, le:le + 1]
                            hLr = sc_ur[:, le:le + 1]
                            hLi = sc_ui[:, le:le + 1]
                            op('v', 'tensor_tensor', tc4[0][:, :], ALr, Fre[:, :], ALU.mult)
                            op('v', 'tensor_tensor', tc4[1][:, :], ALi, Fim[:, :], ALU.mult)
                            op('v', 'tensor_tensor', tc4[2][:, :], ALr, Fim[:, :], ALU.mult)
                            op('v', 'tensor_tensor', tc4[3][:, :], ALi, Fre[:, :], ALU.mult)
                            drain('v')
                            op('v', 'tensor_tensor', nFre[:, :], tc4[0][:, :], tc4[1][:, :], ALU.subtract)
                            op('v', 'tensor_tensor', nFim[:, :], tc4[2][:, :], tc4[3][:, :], ALU.add)
                            drain('v')
                            op('v', 'tensor_tensor', Fre[:, :], nFre[:, :], hLr, ALU.add)
                            op('v', 'tensor_tensor', Fim[:, :], nFim[:, :], hLi, ALU.add)
                            drain('v')
                        op('v', 'memset', agrow[0:1, :], 0.0, inc=s_dve)
                        fcol_done = s_dve.n
                        wait('t', s_dve, fcol_done)
                        em.wait_res('t', rPJ[0])
                        op('t', 'transpose', pPJ[0:1, 0, 0:128], Fre[:, :], c_identf[:, :], inc=s_mm)
                        wait('v', s_mm, s_mm.n)
                        drain('v')
                        op('v', 'tensor_copy', agrow[0:1, 0:128], pPJ[0:1, 0, 0:128], inc=s_dve)
                        rPJ[0].ready(s_dve)
                        em.wait_res('t', rPJ[1])
                        op('t', 'transpose', pPJ[0:1, 1, 0:128], Fim[:, :], c_identf[:, :], inc=s_mm)
                        wait('v', s_mm, s_mm.n)
                        drain('v')
                        op('v', 'tensor_copy', agrow[0:1, 128:256], pPJ[0:1, 1, 0:128], inc=s_dve)
                        rPJ[1].ready(s_dve)
                        op('v', 'tensor_copy', agrow[0:1, 256:257], r128[0][0:1, 0:1], inc=s_dve)
                        ag_ready = s_dve.n
                        wait('g', s_dve, ag_ready)
                        op('g', 'dma_start', out=cc_in[:, :], in_=agrow[0:1, :], inc=s_wl0, k=16)
                        wait('g', s_wl0, s_wl0.n)
                        op('g', 'collective_compute', "AllGather", ALU.bypass,
                           replica_groups=[list(range(world))],
                           ins=[cc_in.ap().opt()], outs=[cc_out.ap().opt()], inc=s_cc)
                        wait('g', s_cc, s_cc.n)
                        op('g', 'dma_start', out=gath[:, :], in_=cc_out[:, :], inc=s_wl0, k=16)
                        gath_val = s_wl0.n
                        wait('t', s_wl0, gath_val)
                        em.wait_res('t', rPJ[0])
                        op('t', 'matmul', pPJ[0:1, 0, :], c_sel[:, :], gath[:, :], start=True, stop=True,
                           inc=s_mm)
                        wait('v', s_mm, s_mm.n)
                        drain('v')
                        op('v', 'tensor_copy', selrow[0:1, :], pPJ[0:1, 0, :], inc=s_dve)
                        rPJ[0].ready(s_dve)
                        wait('t', s_dve, s_dve.n)
                        em.wait_res('t', rPJ[0])
                        op('t', 'transpose', pPJ[:, 0, 0:1], selrow[0:1, 0:128], c_identf[0:1, 0:1], inc=s_mm)
                        em.wait_res('t', rPJ[1])
                        op('t', 'transpose', pPJ[:, 1, 0:1], selrow[0:1, 128:256], c_identf[0:1, 0:1], inc=s_mm)
                        wait('v', s_mm, s_mm.n)
                        drain('v')
                        op('v', 'tensor_copy', c0re[:, :], pPJ[:, 0, 0:1])
                        op('v', 'tensor_copy', c0im[:, :], pPJ[:, 1, 0:1])
                        op('v', 'tensor_copy', c128c[0:1, :], selrow[0:1, 256:257], inc=s_dve)
                        rPJ[0].ready(s_dve)
                        rPJ[1].ready(s_dve)
                        drain('v')
                        rY3 = [Res(), Res(), Res()]
                        yv = [None] * NTILE
                        ysl = [None] * NTILE
                        ctr3 = {'y': 0}

                        def stP(i, gate):
                            toff = i * 128
                            xs = i % 2
                            em.wait_res('y', rX1[xs])
                            op('y', 'dma_start', out=x1ring3[:, xs, :], in_=x1s[toff:toff + 128, :],
                               inc=l1[xs], k=16)
                            wait('v', l1[xs], l1[xs].n)
                            em.wait_res('v', rZ[xs])
                            drain('v')
                            op('v', 'tensor_scalar', zf[:, xs, :], x1ring3[:, xs, :],
                               bnm[:, i:i + 1], bnr[:, i:i + 1], ALU.subtract, ALU.mult, inc=s_dve)
                            vals = []
                            slots = []
                            wait('t', s_act, gate)
                            for dt_ in range(2):
                                dsl = slice(dt_ * 512, (dt_ + 1) * 512)
                                yp = ctr3['y'] % 3
                                ctr3['y'] += 1
                                em.wait_res('t', rY3[yp])
                                op('t', 'matmul', pY[:, yp, :], hrebf[:, toff:toff + 128],
                                   M2reT[:, dsl], start=True, stop=False)
                                op('t', 'matmul', pY[:, yp, :], himbf[:, toff:toff + 128],
                                   M2imT[:, dsl], start=False, stop=False)
                                op('t', 'matmul', pY[:, yp, :], h128bf[0:1, toff:toff + 128],
                                   Mrow[0:1, dsl], start=False, stop=True, inc=s_mm)
                                vals.append(s_mm.n)
                                slots.append(yp)
                            yv[i] = vals
                            ysl[i] = slots

                        def stQ(i):
                            toff = i * 128
                            xs = i % 2
                            os_ = i % 2
                            for dt_ in range(2):
                                dsl = slice(dt_ * 512, (dt_ + 1) * 512)
                                yp = ysl[i][dt_]
                                wait('v', s_mm, yv[i][dt_])
                                if dt_ == 0:
                                    em.wait_res('v', rO[os_])
                                drain('v')
                                op('v', 'tensor_tensor', zt[:, :], zf[:, xs, dsl], c_Dsk[:, dsl],
                                   ALU.mult)
                                op('v', 'tensor_tensor', obuf[:, os_, dsl], pY[:, yp, :], c_WCb[:, dsl],
                                   ALU.add)
                                drain('v')
                                op('v', 'tensor_tensor', obuf[:, os_, dsl], obuf[:, os_, dsl], zt[:, :],
                                   ALU.add)
                                drain('v')
                                op('v', 'tensor_tensor', obuf[:, os_, dsl], obuf[:, os_, dsl],
                                   x1ring3[:, xs, dsl], ALU.add, inc=s_dve)
                                rY3[yp].ready(s_dve)
                            rX1[xs].ready(s_dve)
                            rZ[xs].ready(s_dve)
                            wait('y', s_dve, s_dve.n)
                            op('y', 'dma_start', out=dst[toff:toff + 128, :], in_=obuf[:, os_, :],
                               inc=so[os_], k=16)
                            store_events.append((so[os_], so[os_].n))
                            rO[os_].ready(so[os_])

                        prev_tile = None
                        for c in range(NCHK):
                            cs = slice(c * CH, (c + 1) * CH)
                            cre = c0re if c == 0 else ccre[(c - 1) % 2]
                            cim = c0im if c == 0 else ccim[(c - 1) % 2]
                            op('v', 'tensor_scalar', ncim[:, :], cim[:, :], -1.0, None, ALU.mult)
                            drain('v')
                            op('v', 'scalar_tensor_tensor', sc_ur[:, cs], sc_ar[:, cs], cre[:, 0:1],
                               sc_ur[:, cs], ALU.mult, ALU.add)
                            op('v', 'scalar_tensor_tensor', sc_ui[:, cs], sc_ar[:, cs], cim[:, 0:1],
                               sc_ui[:, cs], ALU.mult, ALU.add)
                            drain('v')
                            op('v', 'scalar_tensor_tensor', sc_ur[:, cs], sc_ai[:, cs], ncim[:, 0:1],
                               sc_ur[:, cs], ALU.mult, ALU.add)
                            op('v', 'scalar_tensor_tensor', sc_ui[:, cs], sc_ai[:, cs], cre[:, 0:1],
                               sc_ui[:, cs], ALU.mult, ALU.add)
                            drain('v')
                            le = (c + 1) * CH - 1
                            op('v', 'tensor_copy', ccre[c % 2][:, :], sc_ur[:, le:le + 1])
                            op('v', 'tensor_copy', ccim[c % 2][:, :], sc_ui[:, le:le + 1], inc=s_dve)
                            drain('v')
                            casc_v = s_dve.n
                            wait('s', s_dve, casc_v)
                            op('s', 'copy', hrebf[:, cs], sc_ur[:, cs])
                            op('s', 'copy', himbf[:, cs], sc_ui[:, cs], inc=s_act)
                            init = c128c[0:1, 0:1] if c == 0 else r128[1][0:1, 0:1]
                            em.wait_res('v', rSCR)
                            op('v', 'tensor_tensor_scan', scr128[0:1, :], a128b[0:1, cs], u128b[0:1, cs],
                               init, ALU.mult, ALU.add)
                            drain('v')
                            op('v', 'tensor_copy', r128[1][0:1, :], scr128[0:1, CH - 1:CH], inc=s_dve)
                            drain('v')
                            wait('s', s_dve, s_dve.n)
                            op('s', 'copy', h128bf[0:1, cs], scr128[0:1, :], inc=s_act)
                            rSCR.ready(s_act)
                            hbf_c = s_act.n
                            for k in range(4):
                                i = c * 4 + k
                                if prev_tile is not None:
                                    stQ(prev_tile)
                                stP(i, hbf_c)
                                prev_tile = i
                        stQ(prev_tile)
                        em.run_block()
            return store_events

        def so_totals():
            return ((so[0], so[0].n), (so[1], so[1].n))

        if phases == 1:
            emit_ffn(x_in, y_out, "A")
        elif phases == 2:
            ev1 = emit_ffn(x_in, x1s, "A")
            emit_attn(y_out, ev1, entry=so_totals())
        else:
            ev1 = emit_ffn(x_in, x1s, "A")
            ev2 = emit_attn(x2s, ev1, entry=so_totals())
            emit_ffn(x2s, y_out, "C", src_events=ev2, entry=so_totals())

        fin = Emit(nc)
        fin.wait('y', so[0], so[0].n)
        fin.wait('y', so[1], so[1].n)
        fin.run_block()

    return nc


def _prep_inputs(L0_attn_Wa_w, L0_attn_Wa_b, L0_attn_log_gamma, L0_attn_WB_w, L0_attn_WB_b,
                 L0_attn_WC_w, L0_attn_WC_b, L0_attn_D_skip, L0_attn_ln_w, L0_attn_ln_b,
                 L0_ffn_router_w, L0_ffn_sg, L0_ffn_su, L0_ffn_sd, L0_ffn_eg, L0_ffn_eu,
                 L0_ffn_ed):
    f32 = np.float32
    ac = np.ascontiguousarray
    n = np.arange(NST)
    ang = 2.0 * np.pi * np.outer(n, n) / NST
    Cfull = np.cos(ang)
    Sfull = np.sin(ang)
    rep = lambda v: ac(np.broadcast_to(np.asarray(v, f32)[None, :], (128, D)))
    return dict(
        rtW=ac(np.asarray(L0_ffn_router_w, f32).T),
        sgT=ac(np.asarray(L0_ffn_sg, f32).T),
        suT=ac(np.asarray(L0_ffn_su, f32).T),
        sdT=ac(np.asarray(L0_ffn_sd, f32).T),
        egT=ac(np.asarray(L0_ffn_eg, f32).transpose(0, 2, 1)),
        euT=ac(np.asarray(L0_ffn_eu, f32).transpose(0, 2, 1)),
        edT=ac(np.asarray(L0_ffn_ed, f32).transpose(0, 2, 1)),
        WaT=ac(np.asarray(L0_attn_Wa_w, f32).T),
        WBT=ac(np.asarray(L0_attn_WB_w, f32).T),
        WCT=ac(np.asarray(L0_attn_WC_w, f32).T),
        Wab=ac(np.asarray(L0_attn_Wa_b, f32).reshape(2, 128).T),
        WBb=ac(np.asarray(L0_attn_WB_b, f32).reshape(2, 128).T),
        lgam=np.asarray(L0_attn_log_gamma, f32).reshape(1, 1),
        Ctab=ac(Cfull[:, 0:KH].astype(f32)),
        Stab=ac(Sfull[:, 0:KH].astype(f32)),
        c128=ac(Cfull[:, KH].astype(f32).reshape(2, 128).T),
        ident=np.eye(128, dtype=f32),
        lnw_rep=rep(L0_attn_ln_w),
        lnb_rep=rep(L0_attn_ln_b),
        WCb_rep=rep(L0_attn_WC_b),
        Dsk_rep=rep(L0_attn_D_skip),
    )


def kernel(L0_attn_x, **kw):
    f32 = np.float32
    TLOC = 4096
    xflat = np.ascontiguousarray(np.asarray(L0_attn_x, f32).reshape(-1, D))
    common = _prep_inputs(**kw)
    nc = build_program(TLOC=TLOC, phases=3)
    in_maps = []
    for c in range(8):
        sel = np.zeros((8, 1), f32)
        if c % 2 == 1:
            sel[c - 1, 0] = 1.0
        m = dict(common)
        m["x"] = xflat[c * TLOC:(c + 1) * TLOC]
        m["sel"] = sel
        in_maps.append(m)
    res = run_bass_kernel_spmd(nc, in_maps, core_ids=list(range(8)))
    out = np.concatenate([res.results[c]["y"] for c in range(8)], axis=0)
    return out.reshape(L0_attn_x.shape).astype(f32)


# version 16
# speedup vs baseline: 1.0041x; 1.0024x over previous
import contextlib
import numpy as np
import concourse.bass as bass
import concourse.mybir as mybir
from concourse.bass_utils import run_bass_kernel_spmd

F32 = mybir.dt.float32
BF16 = mybir.dt.bfloat16
AF = mybir.ActivationFunctionType
ALU = mybir.AluOpType
AX = mybir.AxisListType

D = 1024
DK = 8
NST = 256
KH = 128
H = 2048
HC = 256
NHC = H // HC
HS = HC // 128
NE = 8
LN_EPS = 1e-5


class Sem:
    def __init__(self, h):
        self.h = h
        self.n = 0


class Res:
    def __init__(self):
        self.cons = []

    def ready(self, sem, val=None):
        self.cons = [(sem, sem.n if val is None else val)]

    def also(self, sem, val=None):
        self.cons.append((sem, sem.n if val is None else val))


class Emit:
    def __init__(self, nc):
        self.nc = nc
        self.ops = {k: [] for k in 'tvsgy'}

    def op(self, eng, method, *args, inc=None, k=1, **kw):
        semh = inc.h if inc is not None else None
        if inc is not None:
            inc.n += k

        def f(e, m=method, a=args, kw=kw, s=semh, k=k):
            r = getattr(e, m)(*a, **kw)
            if s is not None:
                r.then_inc(s, k)
        self.ops[eng].append(f)
        return inc.n if inc is not None else None

    def wait(self, eng, sem, val=None):
        v = sem.n if val is None else val
        self.ops[eng].append(lambda e, s=sem.h, v=v: e.wait_ge(s, v))

    def wait_res(self, eng, res):
        for sem, val in res.cons:
            self.wait(eng, sem, val)

    def drain(self, eng):
        self.ops[eng].append(lambda e: e.drain())

    def run_block(self):
        nc = self.nc
        ops = self.ops
        with nc.Block() as blk:
            @blk.tensor
            def _(t):
                for f in ops['t']:
                    f(t)

            @blk.vector
            def _(v):
                for f in ops['v']:
                    f(v)

            @blk.scalar
            def _(s):
                for f in ops['s']:
                    f(s)

            @blk.gpsimd
            def _(g):
                for f in ops['g']:
                    f(g)

            @blk.sync
            def _(y):
                for f in ops['y']:
                    f(y)
        self.ops = {k: [] for k in 'tvsgy'}


def build_program(TLOC=4096, phases=3, world=8):
    TGRP = min(2048, TLOC)
    NGRP = TLOC // TGRP
    NTT = TGRP // 512
    NTM = TGRP // 128
    NTILE = TLOC // 128
    CH = min(512, TLOC)
    NCHK = TLOC // CH
    PAD = CH // 2
    NLEV = CH.bit_length() - 1

    nc = bass.Bass("TRN2", target_bir_lowering=False)

    x_in = nc.dram_tensor("x", [TLOC, D], F32, kind="ExternalInput").ap()
    y_out = nc.dram_tensor("y", [TLOC, D], F32, kind="ExternalOutput").ap()
    d_sel = nc.dram_tensor("sel", [world, 1], F32, kind="ExternalInput").ap()

    d_rtW = nc.dram_tensor("rtW", [D, NE], F32, kind="ExternalInput").ap()
    d_sgT = nc.dram_tensor("sgT", [D, H], F32, kind="ExternalInput").ap()
    d_suT = nc.dram_tensor("suT", [D, H], F32, kind="ExternalInput").ap()
    d_sdT = nc.dram_tensor("sdT", [H, D], F32, kind="ExternalInput").ap()
    d_egT = nc.dram_tensor("egT", [NE, D, H], F32, kind="ExternalInput").ap()
    d_euT = nc.dram_tensor("euT", [NE, D, H], F32, kind="ExternalInput").ap()
    d_edT = nc.dram_tensor("edT", [NE, H, D], F32, kind="ExternalInput").ap()

    d_WaT = nc.dram_tensor("WaT", [D, NST], F32, kind="ExternalInput").ap()
    d_WBT = nc.dram_tensor("WBT", [D, NST], F32, kind="ExternalInput").ap()
    d_WCT = nc.dram_tensor("WCT", [NST, D], F32, kind="ExternalInput").ap()
    d_Wab = nc.dram_tensor("Wab", [128, 2], F32, kind="ExternalInput").ap()
    d_WBb = nc.dram_tensor("WBb", [128, 2], F32, kind="ExternalInput").ap()
    d_lgam = nc.dram_tensor("lgam", [1, 1], F32, kind="ExternalInput").ap()
    d_Ctab = nc.dram_tensor("Ctab", [NST, KH], F32, kind="ExternalInput").ap()
    d_Stab = nc.dram_tensor("Stab", [NST, KH], F32, kind="ExternalInput").ap()
    d_c128 = nc.dram_tensor("c128", [128, 2], F32, kind="ExternalInput").ap()
    d_ident = nc.dram_tensor("ident", [128, 128], F32, kind="ExternalInput").ap()
    d_lnwrs = nc.dram_tensor("lnwrs", [128, DK], F32, kind="ExternalInput").ap()
    d_lnbrs = nc.dram_tensor("lnbrs", [128, DK], F32, kind="ExternalInput").ap()
    d_WCb = nc.dram_tensor("WCb_rep", [128, D], F32, kind="ExternalInput").ap()
    d_Dsk = nc.dram_tensor("Dsk_rep", [128, D], F32, kind="ExternalInput").ap()

    x1s = nc.dram_tensor("x1s", [TLOC, D], F32).ap()
    mrow_scr = nc.dram_tensor("mrow_scr", [1, D], BF16).ap()
    x2s = nc.dram_tensor("x2s", [TLOC, D], F32).ap()
    cc_in = nc.dram_tensor("cc_in", [1, 512], F32)
    cc_out = nc.dram_tensor("cc_out", [world, 512], F32)

    em = Emit(nc)
    stk = contextlib.ExitStack()

    def sb(name, shape, dt):
        return stk.enter_context(nc.sbuf_tensor(name, shape, dt))

    def sem(name):
        return Sem(stk.enter_context(nc.semaphore(name)))

    with stk:
        s_mm = sem("s_mm")
        s_act = sem("s_act")
        s_dve = sem("s_dve")
        s_gp = sem("s_gp")
        s_cc = sem("s_cc")
        s_ld0 = sem("s_ld0")
        s_wl0 = sem("s_wl0")
        lx = [sem(f"lx{j}") for j in range(2)]
        so = [sem(f"so{j}") for j in range(2)]
        wl = [sem(f"wl{j}") for j in range(3)]
        wl2 = [sem(f"wl2{j}") for j in range(2)]
        l1 = [sem(f"l1{j}") for j in range(2)]

        c_identb = sb("c_identb", [128, 128], BF16)
        c_identf = sb("c_identf", [128, 128], F32)
        c_rtW = sb("c_rtW", [128, DK, NE], BF16)
        c_WCT = sb("c_WCT", [128, 2, D], BF16)
        c_C = sb("c_C", [128, 2, KH], BF16)
        c_S = sb("c_S", [128, 2, KH], BF16)
        c_c128 = sb("c_c128", [128, 2], BF16)
        c_Wab = sb("c_Wab", [128, 2], F32)
        c_WBb = sb("c_WBb", [128, 2], BF16)
        c_WCb = sb("c_WCb", [128, D], F32)
        c_Dsk = sb("c_Dsk", [128, D], F32)
        c_gamma = sb("c_gamma", [128, 1], F32)
        c_ones = sb("c_ones", [1, 128], F32)
        c_lgam = sb("c_lgam", [1, 1], F32)
        c_sel = sb("c_sel", [world, 1], F32)
        M2reT = sb("M2reT", [128, D], BF16)
        M2imT = sb("M2imT", [128, D], BF16)
        fac = sb("fac", [128, 1], F32)
        fbre = sb("fbre", [128, 1], F32)
        fbim = sb("fbim", [128, 1], F32)
        fb128 = sb("fb128", [1, 1], F32)
        c_eps = sb("c_eps", [128, 1], F32)
        c_two = sb("c_two", [128, 1], F32)

        op, wait, drain = em.op, em.wait, em.drain

        for dst, src in ((c_identf, d_ident),
                         (c_WCb, d_WCb), (c_Dsk, d_Dsk)):
            op('y', 'dma_start', out=dst[:, :], in_=src[:, :], inc=s_ld0, k=16)
        op('y', 'dma_start', out=c_lgam[:, :], in_=d_lgam[:, :], inc=s_ld0, k=16)
        op('y', 'dma_start', out=c_sel[:, :], in_=d_sel[:, :], inc=s_ld0, k=16)
        op('y', 'dma_start', out=c_Wab[:, :], in_=d_Wab[:, :], inc=s_ld0, k=16)
        op('g', 'dma_start', out=c_identb[:, :], in_=d_ident[:, :], inc=s_wl0, k=16)
        op('g', 'dma_start', out=c_rtW[:, :, :],
           in_=d_rtW.rearrange("(c p) e -> p c e", p=128), inc=s_wl0, k=16)
        op('g', 'dma_start', out=c_WCT[:, :, :],
           in_=d_WCT.rearrange("(g p) d -> p g d", p=128), inc=s_wl0, k=16)
        op('g', 'dma_start', out=c_C[:, :, :],
           in_=d_Ctab.rearrange("(g p) k -> p g k", p=128), inc=s_wl0, k=16)
        op('g', 'dma_start', out=c_S[:, :, :],
           in_=d_Stab.rearrange("(g p) k -> p g k", p=128), inc=s_wl0, k=16)
        op('g', 'dma_start', out=c_c128[:, :], in_=d_c128[:, :], inc=s_wl0, k=16)
        op('g', 'dma_start', out=c_WBb[:, :], in_=d_WBb[:, :], inc=s_wl0, k=16)
        ld_all = s_ld0.n
        wl_all = s_wl0.n

        wait('s', s_ld0, ld_all)
        op('s', 'activation', c_lgam[:, :], c_lgam[:, :], AF.Sigmoid, inc=s_act)
        op('v', 'memset', c_ones[:, :], 1.0)
        op('v', 'memset', fac[:, :], 2.0)
        op('v', 'memset', c_eps[:, :], LN_EPS)
        op('v', 'memset', c_two[:, :], 2.0)
        drain('v')
        op('v', 'memset', fac[0:1, :], 1.0, inc=s_dve)
        drain('v')

        with nc.psum_tensor("ip0", [128, 512], F32) as ip0, \
             nc.psum_tensor("ip1", [1, 512], F32) as ip1, \
             nc.sbuf_tensor("MrowT", [1, D], BF16) as Mrow:
            wait('t', s_act, s_act.n)
            wait('t', s_dve, s_dve.n)
            wait('t', s_wl0, wl_all)
            wait('t', s_ld0, ld_all)
            op('t', 'matmul', ip0[:, 0:1], c_ones[:, :], c_lgam[:, :], start=True, stop=True,
               inc=s_mm)
            wait('v', s_mm, s_mm.n)
            op('v', 'tensor_copy', c_gamma[:, :], ip0[:, 0:1], inc=s_dve)
            wait('t', s_dve, s_dve.n)
            for (tabs, dstb, sc) in ((c_C, M2reT, 1.0 / NST), (c_S, M2imT, -1.0 / NST)):
                for dt_ in range(2):
                    op('t', 'matmul', ip0[:, :], tabs[:, 0, :], c_WCT[:, 0, dt_ * 512:(dt_ + 1) * 512],
                       start=True, stop=False)
                    op('t', 'matmul', ip0[:, :], tabs[:, 1, :], c_WCT[:, 1, dt_ * 512:(dt_ + 1) * 512],
                       start=False, stop=True, inc=s_mm)
                    wait('v', s_mm, s_mm.n)
                    op('v', 'tensor_scalar', dstb[:, dt_ * 512:(dt_ + 1) * 512], ip0[:, :],
                       fac[:, 0:1], sc, ALU.mult, ALU.mult, inc=s_dve)
                    wait('t', s_dve, s_dve.n)
            for dt_ in range(2):
                op('t', 'matmul', ip1[:, :], c_c128[:, 0:1], c_WCT[:, 0, dt_ * 512:(dt_ + 1) * 512],
                   start=True, stop=False)
                op('t', 'matmul', ip1[:, :], c_c128[:, 1:2], c_WCT[:, 1, dt_ * 512:(dt_ + 1) * 512],
                   start=False, stop=True, inc=s_mm)
                wait('v', s_mm, s_mm.n)
                op('v', 'tensor_scalar', Mrow[0:1, dt_ * 512:(dt_ + 1) * 512], ip1[:, :],
                   1.0 / NST, None, ALU.mult, inc=s_dve)
                wait('t', s_dve, s_dve.n)
            wait('y', s_dve, s_dve.n)
            op('y', 'dma_start', out=mrow_scr[:, :], in_=Mrow[0:1, :], inc=s_ld0, k=16)
            wait('y', s_ld0, s_ld0.n)
            em.run_block()


        def emit_ffn(src, dst, tag, src_events=None, entry=()):
            TGRP = 1024
            NGRP = TLOC // TGRP
            NTM = TGRP // 128
            NTT = TGRP // 512
            NSC = 4
            NHS = 8
            units = [(None, 0), (None, 1)] + [(e, h) for e in range(NE) for h in range(2)]
            NU = len(units)
            store_events = [None] * (TLOC // 128)
            with contextlib.ExitStack() as fs:
                def fsb(name, shape, dt):
                    return fs.enter_context(nc.sbuf_tensor(name, shape, dt))
                xn_fm = fsb("xn_fm" + tag, [128, 2, DK, TGRP], BF16)
                actb = [fsb(f"actb{j}{tag}", [128, NHS, TGRP], BF16) for j in range(2)]
                wg = [fsb(f"wg{j}{tag}", [128, DK, 256], BF16) for j in range(3)]
                wu = [fsb(f"wu{j}{tag}", [128, DK, 256], BF16) for j in range(3)]
                wd = [fsb(f"wd{j}{tag}", [128, NHS, D], BF16) for j in range(2)]
                acc = fsb("acc" + tag, [128, NTM, D], F32)
                xring = fsb("xring" + tag, [128, 2, D], F32)
                oring = fsb("oring" + tag, [128, 2, D], F32)
                xnt = fsb("xnt" + tag, [128, 2, D], BF16)
                silb = [fsb(f"silb{j}{tag}", [128, 512], F32) for j in range(2)]
                uevb = [fsb(f"uevb{j}{tag}", [128, 512], BF16) for j in range(2)]
                cwb = fsb("cwb" + tag, [128, 2, NTM * 8], F32)
                bn6 = fsb("bn6" + tag, [128, 12], F32)
                bn2 = fsb("bn2" + tag, [128, 2], F32)
                sdv = fsb("sdv" + tag, [128, 1], F32)
                rstd = fsb("rstd" + tag, [128, 1], F32)
                lg8 = fsb("lg8" + tag, [128, 8], F32)
                m1 = fsb("m1" + tag, [128, 1], F32)
                nm1 = fsb("nm1" + tag, [128, 1], F32)
                m2 = fsb("m2" + tag, [128, 1], F32)
                dm = fsb("dm" + tag, [128, 1], F32)
                eqm = fsb("eqm" + tag, [128, 8], F32)
                lmb = fsb("lmb" + tag, [128, 8], F32)
                ex8 = fsb("ex8" + tag, [128, 8], F32)
                ed1 = fsb("ed1" + tag, [128, 1], F32)
                den = fsb("den" + tag, [128, 1], F32)
                rden = fsb("rden" + tag, [128, 1], F32)
                cwt = fsb("cwt" + tag, [128, 8], F32)

                with nc.psum_tensor("pGU" + tag, [128, 4, 512], F32) as pGU, \
                     nc.psum_tensor("pDD" + tag, [128, 2, 512], F32) as pD, \
                     nc.psum_tensor("pTR" + tag, [128, DK, 128], BF16) as pTR, \
                     nc.psum_tensor("pRT" + tag, [128, 512], F32) as pRT:

                    rGU = [Res(), Res()]
                    rD = [Res(), Res()]
                    rACT = [Res(), Res()]
                    rSIL = [Res(), Res()]
                    rW = [Res(), Res(), Res()]
                    rWD = [Res(), Res()]
                    rX = [Res(), Res()]
                    rO = [Res(), Res()]
                    rXN = [Res(), Res()]
                    rTR = Res()
                    rRT = Res()

                    for eng in 'tvsgy':
                        for bsem, bval in entry:
                            wait(eng, bsem, bval)

                    ctr = {'x': 0, 'o': 0, 'xn': 0, 'gu': 0, 'dsl': 0}
                    side1 = []
                    side5 = []

                    def pop_side(q):
                        if q:
                            q.pop(0)()

                    def flush(q):
                        while q:
                            q.pop(0)()

                    def load_src(gt):
                        xs = ctr['x'] % 2
                        ctr['x'] += 1
                        em.wait_res('y', rX[xs])
                        if src_events is not None:
                            gsem, gval = src_events[gt]
                            wait('y', gsem, gval)
                        op('y', 'dma_start', out=xring[:, xs, :],
                           in_=src[gt * 128:(gt + 1) * 128, :], inc=lx[xs], k=16)
                        return xs, lx[xs], lx[xs].n

                    def stage1_tile(g, i):
                        def f():
                            gt = g * NTM + i
                            xs, lsem, lval = load_src(gt)
                            wait('v', lsem, lval)
                            drain('v')
                            op('v', 'bn_stats', bn6[:, 0:6], xring[:, xs, 0:512])
                            op('v', 'bn_stats', bn6[:, 6:12], xring[:, xs, 512:1024])
                            drain('v')
                            op('v', 'bn_aggr', bn2[:, :], bn6[:, :], inc=s_dve)
                            wait('s', s_dve, s_dve.n)
                            op('s', 'activation', sdv[:, :], bn2[:, 1:2], AF.Sqrt,
                               bias=c_eps[:, 0:1], inc=s_act)
                            wait('v', s_act, s_act.n)
                            op('v', 'reciprocal', rstd[:, :], sdv[:, :])
                            drain('v')
                            ns = ctr['xn'] % 2
                            ctr['xn'] += 1
                            em.wait_res('v', rXN[ns])
                            op('v', 'tensor_scalar', xnt[:, ns, :], xring[:, xs, :],
                               bn2[:, 0:1], rstd[:, 0:1], ALU.subtract, ALU.mult, inc=s_dve)
                            rX[xs].ready(s_dve)
                            xn_ready = s_dve.n
                            wait('t', s_dve, xn_ready)
                            em.wait_res('t', rTR)
                            for dk in range(DK):
                                op('t', 'transpose', pTR[:, dk, :],
                                   xnt[:, ns, dk * 128:(dk + 1) * 128], c_identb[:, :],
                                   inc=(s_mm if dk == DK - 1 else None))
                            wait('v', s_mm, s_mm.n)
                            op('v', 'tensor_copy', xn_fm[:, g % 2, :, i * 128:(i + 1) * 128],
                               pTR[:, :, :], inc=s_dve)
                            rTR.ready(s_dve)
                            rXN[ns].ready(s_dve)
                            xnfm_tile_done = s_dve.n
                            em.wait_res('t', rRT)
                            wait('t', s_dve, xnfm_tile_done)
                            for dk in range(DK):
                                op('t', 'matmul', pRT[:, 0:8],
                                   xn_fm[:, g % 2, dk, i * 128:(i + 1) * 128], c_rtW[:, dk, :],
                                   start=(dk == 0), stop=(dk == DK - 1),
                                   inc=(s_mm if dk == DK - 1 else None))
                            wait('v', s_mm, s_mm.n)
                            drain('v')
                            op('v', 'tensor_copy', lg8[:, :], pRT[:, 0:8], inc=s_dve)
                            rRT.ready(s_dve)
                            drain('v')
                            op('v', 'tensor_reduce', m1[:, :], lg8[:, :], AX.X, ALU.max)
                            drain('v')
                            op('v', 'tensor_scalar', eqm[:, :], lg8[:, :], m1[:, 0:1], None, ALU.is_ge)
                            op('v', 'tensor_scalar', nm1[:, :], m1[:, :], -1.0, None, ALU.mult)
                            drain('v')
                            op('v', 'scalar_tensor_tensor', lmb[:, :], eqm[:, :], -1e30, lg8[:, :],
                               ALU.mult, ALU.add)
                            drain('v')
                            op('v', 'tensor_reduce', m2[:, :], lmb[:, :], AX.X, ALU.max)
                            drain('v')
                            op('v', 'tensor_tensor', dm[:, :], m2[:, :], m1[:, :], ALU.subtract,
                               inc=s_dve)
                            wait('s', s_dve, s_dve.n)
                            op('s', 'activation', ex8[:, :], lg8[:, :], AF.Exp,
                               bias=nm1[:, 0:1], scale=1.0)
                            op('s', 'activation', ed1[:, :], dm[:, :], AF.Exp, inc=s_act)
                            wait('v', s_act, s_act.n)
                            op('v', 'tensor_scalar', den[:, :], ed1[:, :], 1.0, None, ALU.add)
                            drain('v')
                            op('v', 'reciprocal', rden[:, :], den[:, :])
                            op('v', 'tensor_scalar', eqm[:, :], lg8[:, :], m2[:, 0:1], None, ALU.is_ge)
                            drain('v')
                            op('v', 'tensor_tensor', cwt[:, :], ex8[:, :], eqm[:, :], ALU.mult)
                            drain('v')
                            op('v', 'tensor_scalar', cwb[:, g % 2, i * 8:(i + 1) * 8], cwt[:, :],
                               rden[:, 0:1], 0.5, ALU.mult, ALU.mult, inc=s_dve)
                        return f

                    def stage5_tile(g, tm, acc_done):
                        def f():
                            gt = g * NTM + tm
                            xs, lsem, lval = load_src(gt)
                            os_ = ctr['o'] % 2
                            ctr['o'] += 1
                            wait('v', lsem, lval)
                            em.wait_res('v', rO[os_])
                            drain('v')
                            op('v', 'tensor_tensor', oring[:, os_, :], xring[:, xs, :],
                               acc[:, tm, :], ALU.add, inc=s_dve)
                            rX[xs].ready(s_dve)
                            wait('y', s_dve, s_dve.n)
                            op('y', 'dma_start', out=dst[gt * 128:(gt + 1) * 128, :],
                               in_=oring[:, os_, :], inc=so[os_], k=16)
                            store_events[gt] = (so[os_], so[os_].n)
                            rO[os_].ready(so[os_])
                        return f

                    guw_ev = {}
                    wd_ev = {}
                    GSEQ_MAX = NGRP * NU * NSC
                    USEQ_MAX = NGRP * NU

                    def sched_guw(seq):
                        if seq >= GSEQ_MAX:
                            return
                        u = (seq // NSC) % NU
                        sc = seq % NSC
                        slot = seq % 3
                        e, half = units[u]
                        h0 = half * 1024 + sc * 256
                        if e is None:
                            gsrc = d_sgT.rearrange("(c p) h -> p c h", p=128)[:, :, h0:h0 + 256]
                            usrc = d_suT.rearrange("(c p) h -> p c h", p=128)[:, :, h0:h0 + 256]
                        else:
                            gsrc = d_egT[e].rearrange("(c p) h -> p c h", p=128)[:, :, h0:h0 + 256]
                            usrc = d_euT[e].rearrange("(c p) h -> p c h", p=128)[:, :, h0:h0 + 256]
                        em.wait_res('g', rW[slot])
                        op('g', 'dma_start', out=wg[slot][:, :, :], in_=gsrc, inc=wl[slot], k=16)
                        op('g', 'dma_start', out=wu[slot][:, :, :], in_=usrc, inc=wl[slot], k=16)
                        guw_ev[seq] = (slot, wl[slot], wl[slot].n)

                    def sched_wd(useq):
                        if useq >= USEQ_MAX:
                            return
                        u = useq % NU
                        slot = useq % 2
                        e, half = units[u]
                        if e is None:
                            dsrc = d_sdT.rearrange("(c p) d -> p c d", p=128)[:, half * 8:(half + 1) * 8, :]
                        else:
                            dsrc = d_edT[e].rearrange("(c p) d -> p c d", p=128)[:, half * 8:(half + 1) * 8, :]
                        em.wait_res('g', rWD[slot])
                        op('g', 'dma_start', out=wd[slot][:, :, :], in_=dsrc, inc=wl2[slot], k=16)
                        wd_ev[useq] = (slot, wl2[slot], wl2[slot].n)

                    act_done = [0, 0]

                    def emit_down(g, u):
                        e, half = units[u]
                        uslot = u % 2
                        wdslot, wdsem, wdval = wd_ev[g * NU + u]
                        wait('t', wdsem, wdval)
                        wait('t', s_dve, act_done[uslot])

                        for tm in range(NTM):
                            for dt_ in range(2):
                                pp = ctr['dsl'] % 2
                                ctr['dsl'] += 1
                                em.wait_res('t', rD[pp])
                                for hsub in range(NHS):
                                    op('t', 'matmul', pD[:, pp, :],
                                       actb[uslot][:, hsub, tm * 128:(tm + 1) * 128],
                                       wd[wdslot][:, hsub, dt_ * 512:(dt_ + 1) * 512],
                                       start=(hsub == 0), stop=(hsub == NHS - 1),
                                       inc=(s_mm if hsub == NHS - 1 else None))
                                wait('v', s_mm, s_mm.n)
                                aslab = acc[:, tm, dt_ * 512:(dt_ + 1) * 512]
                                if u == 0:
                                    op('v', 'tensor_scalar', aslab, pD[:, pp, :],
                                       0.5, None, ALU.mult, inc=s_dve)
                                elif e is None:
                                    op('v', 'scalar_tensor_tensor', aslab, pD[:, pp, :],
                                       0.5, aslab, ALU.mult, ALU.add, inc=s_dve)
                                else:
                                    col = cwb[:, g % 2, tm * 8 + e: tm * 8 + e + 1]
                                    op('v', 'scalar_tensor_tensor', aslab, pD[:, pp, :],
                                       col, aslab, ALU.mult, ALU.add, inc=s_dve)
                                rD[pp].ready(s_dve)
                                pop_side(side1)
                        rACT[uslot].ready(s_mm, s_mm.n)
                        rWD[wdslot].ready(s_mm, s_mm.n)
                        sched_wd(g * NU + u + 2)

                    for sq in range(3):
                        sched_guw(sq)
                    sched_wd(0)
                    sched_wd(1)
                    for i in range(NTM):
                        side1.append(stage1_tile(0, i))
                    flush(side1)

                    for g in range(NGRP):
                        xn_all = s_dve.n
                        for u in range(NU):
                            e, half = units[u]
                            uslot = u % 2
                            first_mult = True
                            for sc in range(NSC):
                                seq = (g * NU + u) * NSC + sc
                                slot, wsem, wval = guw_ev[seq]
                                wait('t', wsem, wval)
                                for hs in range(2):
                                    hsub = sc * 2 + hs
                                    for tt in range(NTT):
                                        pp = ctr['gu'] % 2
                                        ctr['gu'] += 1
                                        em.wait_res('t', rGU[pp])
                                        if u == 0 and sc == 0 and hs == 0 and tt == 0:
                                            wait('t', s_dve, xn_all)
                                        for dk in range(DK):
                                            op('t', 'matmul', pGU[:, 2 * pp, :],
                                               wg[slot][:, dk, hs * 128:(hs + 1) * 128],
                                               xn_fm[:, g % 2, dk, tt * 512:(tt + 1) * 512],
                                               start=(dk == 0), stop=(dk == DK - 1),
                                               inc=(s_mm if dk == DK - 1 else None))
                                        gval = s_mm.n
                                        for dk in range(DK):
                                            op('t', 'matmul', pGU[:, 2 * pp + 1, :],
                                               wu[slot][:, dk, hs * 128:(hs + 1) * 128],
                                               xn_fm[:, g % 2, dk, tt * 512:(tt + 1) * 512],
                                               start=(dk == 0), stop=(dk == DK - 1),
                                               inc=(s_mm if dk == DK - 1 else None))
                                        uval = s_mm.n
                                        wait('s', s_mm, gval)
                                        em.wait_res('s', rSIL[pp])
                                        op('s', 'activation', silb[pp][:, :], pGU[:, 2 * pp, :],
                                           AF.Silu)
                                        wait('s', s_mm, uval)
                                        op('s', 'activation', uevb[pp][:, :], pGU[:, 2 * pp + 1, :],
                                           AF.Identity, inc=s_act)
                                        rGU[pp].ready(s_act)
                                        wait('v', s_act, s_act.n)
                                        if first_mult:
                                            em.wait_res('v', rACT[uslot])
                                            first_mult = False
                                        op('v', 'tensor_tensor',
                                           actb[uslot][:, hsub, tt * 512:(tt + 1) * 512],
                                           silb[pp][:, :], uevb[pp][:, :], ALU.mult,
                                           inc=s_dve)
                                        rSIL[pp].ready(s_dve)
                                        pop_side(side5)
                                rW[slot].ready(s_mm, s_mm.n)
                                sched_guw(seq + 3)
                            act_done[uslot] = s_dve.n
                            if u == 12 and g + 1 < NGRP:
                                for i in range(NTM):
                                    side1.append(stage1_tile(g + 1, i))
                            if u > 0:
                                if u == 1:
                                    flush(side5)
                                emit_down(g, u - 1)
                        emit_down(g, NU - 1)
                        flush(side1)
                        acc_done = s_dve.n
                        for tm in range(NTM):
                            side5.append(stage5_tile(g, tm, acc_done))
                    flush(side5)
                    em.run_block()
            return store_events

        def emit_attn(dst, x1_events, entry=()):
            store_events = []
            with contextlib.ExitStack() as fs:
                def fsb(name, shape, dt):
                    return fs.enter_context(nc.sbuf_tensor(name, shape, dt))
                sc_ar = fsb("sc_ar", [128, TLOC], F32)
                sc_ai = fsb("sc_ai", [128, TLOC], F32)
                sc_ur = fsb("sc_ur", [128, TLOC], F32)
                sc_ui = fsb("sc_ui", [128, TLOC], F32)
                hrebf = fsb("hrebf", [128, TLOC], BF16)
                himbf = fsb("himbf", [128, TLOC], BF16)
                a128b = fsb("a128b", [1, TLOC], F32)
                u128b = fsb("u128b", [1, TLOC], F32)
                h128bf = fsb("h128bf", [1, TLOC], BF16)
                r128 = [fsb(f"r128{q}", [1, 1], F32) for q in range(2)]
                c0re = fsb("c0re", [128, 1], F32)
                c0im = fsb("c0im", [128, 1], F32)
                c128c = fsb("c128c", [1, 1], F32)

                bnm = fsb("bnm", [128, NTILE], F32)
                bnr = fsb("bnr", [128, NTILE], F32)
                pngG = [[fsb(f"pgg{j}{q}", [128, CH], F32) for q in range(4)] for j in range(2)]
                tmpG = [fsb(f"tmg{q}", [128, CH], F32) for q in range(2)]

                with nc.psum_tensor("pTRB", [128, DK, 128], BF16) as pTR, \
                     nc.psum_tensor("pPJ", [128, 4, 512], F32) as pPJ:

                    rTR = Res()
                    rPJ = [Res(), Res(), Res(), Res()]
                    rFF = [Res(), Res()]
                    rYp = [Res(), Res()]
                    rX1 = [Res(), Res()]
                    rZ = [Res(), Res()]
                    rO = [Res(), Res()]
                    rSCR = Res()

                    for eng in 'tvsgy':
                        for bsem, bval in entry:
                            wait(eng, bsem, bval)

                    with contextlib.ExitStack() as f1, \
                         nc.psum_tensor("pFF", [128, 2, 512], F32) as pFF:
                        def f1b(name, shape, dt):
                            return f1.enter_context(nc.sbuf_tensor(name, shape, dt))
                        c_WaT = f1b("c_WaT", [128, DK, NST], BF16)
                        c_WBT = f1b("c_WBT", [128, DK, NST], BF16)
                        c_lnwrs = f1b("c_lnwrs", [128, DK], F32)
                        c_lnbrs = f1b("c_lnbrs", [128, DK], BF16)
                        wbbe = f1b("wbbe", [128, 2], BF16)
                        op('g', 'dma_start', out=c_WaT[:, :, :],
                           in_=d_WaT.rearrange("(c p) n -> p c n", p=128), inc=s_ld0, k=16)
                        op('g', 'dma_start', out=c_WBT[:, :, :],
                           in_=d_WBT.rearrange("(c p) n -> p c n", p=128), inc=s_ld0, k=16)
                        op('y', 'dma_start', out=c_lnwrs[:, :], in_=d_lnwrs[:, :], inc=s_ld0, k=16)
                        op('g', 'dma_start', out=c_lnbrs[:, :], in_=d_lnbrs[:, :], inc=s_ld0, k=16)
                        waW_val = s_ld0.n
                        wait('t', s_ld0, waW_val)
                        for wsrc, ng0 in ((c_WaT, 0), (c_WBT, 2)):
                            for ng in range(2):
                                for dk in range(DK):
                                    op('t', 'matmul', pPJ[:, ng0 + ng, 0:1],
                                       wsrc[:, dk, ng * 128:(ng + 1) * 128], c_lnbrs[:, dk:dk + 1],
                                       start=(dk == 0), stop=(dk == DK - 1),
                                       inc=(s_mm if dk == DK - 1 else None))
                        fold_mm = s_mm.n
                        wait('v', s_mm, fold_mm)
                        drain('v')
                        op('v', 'tensor_tensor', c_Wab[:, 0:1], c_Wab[:, 0:1], pPJ[:, 0, 0:1], ALU.add)
                        op('v', 'tensor_tensor', c_Wab[:, 1:2], c_Wab[:, 1:2], pPJ[:, 1, 0:1], ALU.add)
                        op('v', 'tensor_tensor', wbbe[:, 0:1], c_WBb[:, 0:1], pPJ[:, 2, 0:1], ALU.add)
                        op('v', 'tensor_tensor', wbbe[:, 1:2], c_WBb[:, 1:2], pPJ[:, 3, 0:1], ALU.add)
                        drain('v')
                        for dk in range(DK):
                            op('v', 'tensor_scalar', c_WaT[:, dk, :], c_WaT[:, dk, :],
                               c_lnwrs[:, dk:dk + 1], None, ALU.mult)
                            op('v', 'tensor_scalar', c_WBT[:, dk, :], c_WBT[:, dk, :],
                               c_lnwrs[:, dk:dk + 1], None, ALU.mult,
                               inc=(s_dve if dk == DK - 1 else None))
                        fold_done = s_dve.n
                        drain('v')
                        wait('t', s_dve, fold_done)
                        op('t', 'matmul', pFF[:, 0, 0:1], c_C[:, 0, :], wbbe[:, 0:1], start=True, stop=False)
                        op('t', 'matmul', pFF[:, 0, 0:1], c_C[:, 1, :], wbbe[:, 1:2], start=False, stop=True)
                        op('t', 'matmul', pFF[:, 1, 0:1], c_S[:, 0, :], wbbe[:, 0:1], start=True, stop=False)
                        op('t', 'matmul', pFF[:, 1, 0:1], c_S[:, 1, :], wbbe[:, 1:2], start=False, stop=True)
                        op('t', 'matmul', pPJ[0:1, 0, 0:1], c_c128[:, 0:1], wbbe[:, 0:1], start=True, stop=False)
                        op('t', 'matmul', pPJ[0:1, 0, 0:1], c_c128[:, 1:2], wbbe[:, 1:2], start=False, stop=True,
                           inc=s_mm)
                        wait('v', s_mm, s_mm.n)
                        drain('v')
                        op('v', 'tensor_copy', fbre[:, :], pFF[:, 0, 0:1])
                        op('v', 'tensor_scalar', fbim[:, :], pFF[:, 1, 0:1], -1.0, None, ALU.mult)
                        op('v', 'tensor_copy', fb128[:, :], pPJ[0:1, 0, 0:1], inc=s_dve)
                        fold_done = s_dve.n
                        wait('s', s_dve, fold_done)
                        wait('t', s_dve, fold_done)
                        x1ring = f1b("x1ringF", [128, 2, D], F32)
                        zfr = f1b("zfrF", [128, 2, D], F32)
                        xntr = f1b("xntrF", [128, 2, D], BF16)
                        xfa = [f1b(f"xfa{j}", [128, DK, 128], BF16) for j in range(2)]
                        abf = f1b("abf", [128, 2, 128], BF16)
                        ubf = f1b("ubf", [128, 2, 128], BF16)
                        tanb = f1b("tanb", [128, 2, 128], BF16)
                        g1b = f1b("g1b", [128, 128], F32)
                        g2b = f1b("g2b", [128, 128], F32)
                        ggb = g1b
                        bn6 = f1b("bn6B", [128, 12], F32)
                        bn6b = bn6
                        bn2r = f1b("bn2rB", [128, 2, 2], F32)
                        bn2br = f1b("bn2brB", [128, 2, 2], F32)
                        sdq = f1b("sdqB", [128, 4, 1], F32)
                        sdvr = sdq[:, 0:2, :]
                        sdv2r = sdq[:, 2:4, :]
                        rstd2 = f1b("rstd2B", [128, 1], F32)
                        row3 = f1b("row3", [1, 384], F32)
                        ab128 = row3[0:1, 0:128]
                        gg128 = row3[0:1, 128:256]
                        raw128 = row3[0:1, 256:384]
                        ggv = [0, 0, 0]

                        rZF = [Res(), Res()]
                        rXN4 = [Res() for _ in range(4)]
                        rFA = [Res(), Res()]
                        rB2 = [Res(), Res()]
                        rB2b = [Res(), Res()]
                        rSD = [Res(), Res()]
                        rSD2 = [Res(), Res()]
                        rTAN = [Res(), Res()]

                        sq1v = [0] * NTILE
                        sq2v = [0] * NTILE
                        xnt_done = [0] * NTILE
                        aval = [None] * NTILE
                        uval = [None] * NTILE

                        def A1(i):
                            xs = i % 2
                            em.wait_res('y', rX1[xs])
                            gsem, gval = x1_events[i]
                            wait('y', gsem, gval)
                            op('y', 'dma_start', out=x1ring[:, xs, :],
                               in_=x1s[i * 128:(i + 1) * 128, :], inc=l1[xs], k=16)
                            wait('v', l1[xs], l1[xs].n)
                            drain('v')
                            op('v', 'bn_stats', bn6[:, 0:6], x1ring[:, xs, 0:512])
                            op('v', 'bn_stats', bn6[:, 6:12], x1ring[:, xs, 512:1024])
                            drain('v')
                            em.wait_res('v', rB2[xs])
                            op('v', 'bn_aggr', bn2r[:, xs, :], bn6[:, :], inc=s_dve)
                            wait('s', s_dve, s_dve.n)
                            em.wait_res('s', rSD[xs])
                            op('s', 'activation', sdvr[:, xs, :], bn2r[:, xs, 1:2], AF.Sqrt,
                               bias=c_eps[:, 0:1], inc=s_act)
                            sq1v[i] = s_act.n

                        def A2(i):
                            xs = i % 2
                            zs = i % 2
                            wait('v', s_act, sq1v[i])
                            op('v', 'reciprocal', bnr[:, i:i + 1], sdvr[:, xs, :])
                            drain('v')
                            op('v', 'tensor_copy', bnm[:, i:i + 1], bn2r[:, xs, 0:1], inc=s_dve)
                            rSD[xs].ready(s_dve)
                            rB2[xs].ready(s_dve)
                            em.wait_res('v', rZF[zs])
                            drain('v')
                            op('v', 'tensor_scalar', zfr[:, zs, :], x1ring[:, xs, :],
                               bnm[:, i:i + 1], bnr[:, i:i + 1], ALU.subtract, ALU.mult, inc=s_dve)
                            rX1[xs].ready(s_dve)
                            drain('v')
                            op('v', 'bn_stats', bn6b[:, 0:6], zfr[:, zs, 0:512])
                            op('v', 'bn_stats', bn6b[:, 6:12], zfr[:, zs, 512:1024])
                            drain('v')
                            em.wait_res('v', rB2b[xs])
                            op('v', 'bn_aggr', bn2br[:, xs, :], bn6b[:, :], inc=s_dve)
                            wait('s', s_dve, s_dve.n)
                            em.wait_res('s', rSD2[xs])
                            op('s', 'activation', sdv2r[:, xs, :], bn2br[:, xs, 1:2], AF.Sqrt,
                               bias=c_eps[:, 0:1], inc=s_act)
                            sq2v[i] = s_act.n

                        def A3(i):
                            xs = i % 2
                            zs = i % 2
                            ns = i % 2
                            wait('v', s_act, sq2v[i])
                            op('v', 'reciprocal', rstd2[:, :], sdv2r[:, xs, :], inc=s_dve)
                            rSD2[xs].ready(s_dve)
                            drain('v')
                            em.wait_res('v', rXN4[ns])
                            op('v', 'tensor_scalar', xntr[:, ns, :], zfr[:, zs, :],
                               bn2br[:, xs, 0:1], rstd2[:, 0:1], ALU.subtract, ALU.mult, inc=s_dve)
                            rB2b[xs].ready(s_dve)
                            rZF[zs].ready(s_dve)
                            xnt_done[i] = s_dve.n

                        def B1b1(i):
                            ns = i % 2
                            fa = i % 2
                            wait('t', s_dve, xnt_done[i])
                            em.wait_res('t', rTR)
                            for dk in range(DK):
                                op('t', 'transpose', pTR[:, dk, :],
                                   xntr[:, ns, dk * 128:(dk + 1) * 128], c_identb[:, :],
                                   inc=(s_mm if dk == DK - 1 else None))
                            wait('v', s_mm, s_mm.n)
                            em.wait_res('v', rFA[fa])
                            op('v', 'tensor_copy', xfa[fa][:, :, :], pTR[:, :, :], inc=s_dve)
                            rTR.ready(s_dve)
                            rXN4[ns].ready(s_dve)
                            fcv = s_dve.n
                            wait('t', s_dve, fcv)
                            av = []
                            for ng in range(2):
                                em.wait_res('t', rPJ[ng])
                                for dk in range(DK):
                                    op('t', 'matmul', pPJ[:, ng, 0:128],
                                       c_WaT[:, dk, ng * 128:(ng + 1) * 128], xfa[fa][:, dk, :],
                                       start=(dk == 0), stop=(dk == DK - 1),
                                       inc=(s_mm if dk == DK - 1 else None))
                                av.append(s_mm.n)
                            uv = []
                            for ng in range(2):
                                em.wait_res('t', rPJ[2 + ng])
                                for dk in range(DK):
                                    op('t', 'matmul', pPJ[:, 2 + ng, 0:128],
                                       c_WBT[:, dk, ng * 128:(ng + 1) * 128], xfa[fa][:, dk, :],
                                       start=(dk == 0), stop=(dk == DK - 1),
                                       inc=(s_mm if dk == DK - 1 else None))
                                uv.append(s_mm.n)
                            aval[i] = av
                            uval[i] = uv
                            rFA[fa].ready(s_mm, s_mm.n)

                        def B1b2(i):
                            toff = i * 128
                            for ng in range(2):
                                wait('s', s_mm, aval[i][ng])
                                em.wait_res('s', rTAN[ng])
                                op('s', 'activation', tanb[:, ng, :], pPJ[:, ng, 0:128], AF.Tanh,
                                   bias=c_Wab[:, ng:ng + 1], scale=1.0, inc=s_act)
                            tanv = s_act.n
                            wait('v', s_act, tanv)
                            op('v', 'tensor_scalar', abf[:, 0, :], tanb[:, 0, :],
                               c_gamma[:, 0:1], None, ALU.mult)
                            op('v', 'tensor_scalar', abf[:, 1, :], tanb[:, 1, :],
                               c_gamma[:, 0:1], None, ALU.mult, inc=s_dve)
                            rTAN[0].ready(s_dve)
                            rTAN[1].ready(s_dve)
                            rPJ[1].ready(s_dve)
                            abf_done = s_dve.n
                            wait('v', s_mm, uval[i][1])
                            drain('v')
                            op('v', 'tensor_copy', ubf[:, 0, :], pPJ[:, 2, 0:128])
                            op('v', 'tensor_copy', ubf[:, 1, :], pPJ[:, 3, 0:128], inc=s_dve)
                            rPJ[3].ready(s_dve)
                            ubf_done = s_dve.n
                            em.wait_res('t', rFF[0])
                            em.wait_res('t', rFF[1])
                            wait('t', s_dve, abf_done)
                            op('t', 'matmul', pFF[:, 0, 0:128], c_C[:, 0, :], abf[:, 0, :], start=True, stop=False)
                            op('t', 'matmul', pFF[:, 0, 0:128], c_C[:, 1, :], abf[:, 1, :], start=False, stop=True)
                            op('t', 'matmul', pFF[:, 1, 0:128], c_S[:, 0, :], abf[:, 0, :], start=True, stop=False)
                            op('t', 'matmul', pFF[:, 1, 0:128], c_S[:, 1, :], abf[:, 1, :], start=False, stop=True)
                            op('t', 'matmul', pPJ[0:1, 0, 0:128], c_c128[:, 0:1], abf[:, 0, :], start=True, stop=False)
                            op('t', 'matmul', pPJ[0:1, 0, 0:128], c_c128[:, 1:2], abf[:, 1, :], start=False, stop=True,
                               inc=s_mm)
                            ffta = s_mm.n
                            wait('s', s_mm, ffta)
                            if ggv[0]:
                                wait('s', s_dve, ggv[0])
                            op('s', 'activation', g1b[:, :], pFF[:, 0, 0:128], AF.Square)
                            op('s', 'activation', g2b[:, :], pFF[:, 1, 0:128], AF.Square, inc=s_act)
                            wait('v', s_act, s_act.n)
                            op('v', 'tensor_tensor', g1b[:, :], g1b[:, :], g2b[:, :], ALU.add, inc=s_dve)
                            wait('s', s_dve, s_dve.n)
                            op('s', 'activation', g2b[:, :], g1b[:, :], AF.Sqrt, inc=s_act)
                            wait('v', s_act, s_act.n)
                            op('v', 'tensor_scalar', g2b[:, :], g2b[:, :], 1e-8, None, ALU.max, inc=s_dve)
                            ggv[0] = s_dve.n
                            wait('s', s_dve, s_dve.n)
                            if ggv[2]:
                                wait('s', s_dve, ggv[2])
                            op('s', 'activation', ggb[:, :], g2b[:, :], AF.Sigmoid, bias=c_two[:, 0:1], scale=-2.0,
                               inc=s_act)
                            ggend = s_act.n
                            wait('v', s_act, ggend)
                            op('v', 'tensor_tensor', sc_ar[:, toff:toff + 128], pFF[:, 0, 0:128],
                               ggb[:, :], ALU.mult)
                            op('v', 'scalar_tensor_tensor', sc_ai[:, toff:toff + 128], pFF[:, 1, 0:128],
                               -1.0, ggb[:, :], ALU.mult, ALU.mult, inc=s_dve)
                            ggv[2] = s_dve.n
                            rFF[0].ready(s_dve)
                            rFF[1].ready(s_dve)
                            wait('s', s_mm, ffta)
                            if ggv[1]:
                                wait('s', s_dve, ggv[1])
                            op('s', 'activation', ab128, pPJ[0:1, 0, 0:128], AF.Abs)
                            op('s', 'activation', raw128, pPJ[0:1, 0, 0:128], AF.Identity, inc=s_act)
                            wait('v', s_act, s_act.n)
                            op('v', 'tensor_scalar', ab128, ab128, 1e-8, None, ALU.max, inc=s_dve)
                            wait('s', s_dve, s_dve.n)
                            op('s', 'activation', gg128, ab128, AF.Sigmoid, bias=c_two[0:1, 0:1], scale=-2.0,
                               inc=s_act)
                            wait('v', s_act, s_act.n)
                            op('v', 'tensor_tensor', a128b[0:1, toff:toff + 128], raw128,
                               gg128, ALU.mult, inc=s_dve)
                            ggv[1] = s_dve.n
                            rPJ[0].ready(s_dve)
                            em.wait_res('t', rFF[0])
                            em.wait_res('t', rFF[1])
                            wait('t', s_dve, ubf_done)
                            op('t', 'matmul', pFF[:, 0, 0:128], c_C[:, 0, :], ubf[:, 0, :], start=True, stop=False)
                            op('t', 'matmul', pFF[:, 0, 0:128], c_C[:, 1, :], ubf[:, 1, :], start=False, stop=True)
                            op('t', 'matmul', pFF[:, 1, 0:128], c_S[:, 0, :], ubf[:, 0, :], start=True, stop=False)
                            op('t', 'matmul', pFF[:, 1, 0:128], c_S[:, 1, :], ubf[:, 1, :], start=False, stop=True)
                            op('t', 'matmul', pPJ[0:1, 2, 0:128], c_c128[:, 0:1], ubf[:, 0, :], start=True, stop=False)
                            op('t', 'matmul', pPJ[0:1, 2, 0:128], c_c128[:, 1:2], ubf[:, 1, :], start=False, stop=True,
                               inc=s_mm)
                            fftu = s_mm.n
                            wait('s', s_mm, fftu)
                            op('s', 'activation', sc_ur[:, toff:toff + 128], pFF[:, 0, 0:128],
                               AF.Identity, bias=fbre[:, 0:1], scale=1.0)
                            op('s', 'activation', sc_ui[:, toff:toff + 128], pFF[:, 1, 0:128],
                               AF.Identity, bias=fbim[:, 0:1], scale=-1.0)
                            op('s', 'activation', u128b[0:1, toff:toff + 128], pPJ[0:1, 2, 0:128],
                               AF.Identity, bias=fb128[0:1, 0:1], scale=1.0, inc=s_act)
                            rFF[0].ready(s_act)
                            rFF[1].ready(s_act)
                            rPJ[2].ready(s_act)

                        chunk_rdy = [None] * NCHK
                        for r in range(NTILE + 4):
                            if r < NTILE:
                                A1(r)
                            if 1 <= r < NTILE + 1:
                                A2(r - 1)
                            if 2 <= r < NTILE + 2:
                                A3(r - 2)
                            if 4 <= r:
                                B1b2(r - 4)
                                if (r - 4) % 4 == 3:
                                    chunk_rdy[(r - 4) // 4] = (s_dve.n, s_act.n)
                            if 3 <= r < NTILE + 3:
                                B1b1(r - 3)
                        front_done = s_dve.n
                        front_act = s_act.n
                        front_gp = s_gp.n
                        front_mm = s_mm.n
                        em.run_block()

                    with contextlib.ExitStack() as f2, \
                         nc.psum_tensor("pYB", [128, 3, 512], F32) as pY:
                        def f2b(name, shape, dt):
                            return f2.enter_context(nc.sbuf_tensor(name, shape, dt))
                        scr128 = f2b("scr128", [1, CH], F32)
                        Fre = f2b("Fre", [128, 1], F32)
                        Fim = f2b("Fim", [128, 1], F32)
                        nFre = f2b("nFre", [128, 1], F32)
                        nFim = f2b("nFim", [128, 1], F32)
                        tc4 = [f2b(f"tc{q}", [128, 1], F32) for q in range(4)]
                        ccre = [f2b(f"ccre{q}", [128, 1], F32) for q in range(2)]
                        ccim = [f2b(f"ccim{q}", [128, 1], F32) for q in range(2)]
                        ncim = f2b("ncim", [128, 1], F32)
                        f2a = f2.enter_context(contextlib.ExitStack())
                        def f2ab(name, shape, dt):
                            return f2a.enter_context(nc.sbuf_tensor(name, shape, dt))
                        pngV = [[f2ab(f"pgv{j}{q}", [128, PAD + CH], F32) for q in range(4)] for j in range(2)]
                        tmpV = [f2ab(f"tmv{q}", [128, CH], F32) for q in range(2)]

                        u_fin = [0] * NCHK

                        def scan_chunks(eng, sem_, png, tmp4, chunks, pad):
                            if pad:
                                for j in range(2):
                                    op(eng, 'memset', png[j][0][:, 0:pad], 1.0)
                                    op(eng, 'memset', png[j][1][:, 0:pad], 0.0)
                                    op(eng, 'memset', png[j][2][:, 0:pad], 0.0)
                                    op(eng, 'memset', png[j][3][:, 0:pad], 0.0)
                                drain(eng)
                            for c in chunks:
                                cs = slice(c * CH, (c + 1) * CH)
                                rv, rs = chunk_rdy[c]
                                wait(eng, s_dve, rv)
                                wait(eng, s_act, rs)
                                for q, srcb in enumerate((sc_ar, sc_ai, sc_ur, sc_ui)):
                                    op(eng, 'tensor_copy', png[0][q][:, pad:], srcb[:, cs])
                                drain(eng)
                                cur = 0
                                s_ = 1
                                for lev in range(NLEV):
                                    A = png[cur]
                                    B = png[1 - cur]
                                    if pad:
                                        cu = slice(pad, pad + CH)
                                        sh = slice(pad - s_, pad - s_ + CH)
                                        TL = CH
                                        pre = None
                                    else:
                                        cu = slice(s_, CH)
                                        sh = slice(0, CH - s_)
                                        TL = CH - s_
                                        pre = slice(0, s_)
                                    op(eng, 'tensor_tensor', B[0][:, cu], A[0][:, cu], A[0][:, sh], ALU.mult)
                                    op(eng, 'tensor_tensor', tmp4[0][:, 0:TL], A[1][:, cu], A[1][:, sh], ALU.mult)
                                    op(eng, 'tensor_tensor', B[1][:, cu], A[0][:, cu], A[1][:, sh], ALU.mult)
                                    op(eng, 'tensor_tensor', tmp4[1][:, 0:TL], A[1][:, cu], A[0][:, sh], ALU.mult)
                                    drain(eng)
                                    op(eng, 'tensor_tensor', B[0][:, cu], B[0][:, cu], tmp4[0][:, 0:TL], ALU.subtract)
                                    op(eng, 'tensor_tensor', B[1][:, cu], B[1][:, cu], tmp4[1][:, 0:TL], ALU.add)
                                    op(eng, 'tensor_tensor', B[2][:, cu], A[0][:, cu], A[2][:, sh], ALU.mult)
                                    op(eng, 'tensor_tensor', tmp4[0][:, 0:TL], A[1][:, cu], A[3][:, sh], ALU.mult)
                                    op(eng, 'tensor_tensor', B[3][:, cu], A[0][:, cu], A[3][:, sh], ALU.mult)
                                    op(eng, 'tensor_tensor', tmp4[1][:, 0:TL], A[1][:, cu], A[2][:, sh], ALU.mult)
                                    drain(eng)
                                    op(eng, 'tensor_tensor', B[2][:, cu], B[2][:, cu], tmp4[0][:, 0:TL], ALU.subtract)
                                    op(eng, 'tensor_tensor', B[3][:, cu], B[3][:, cu], tmp4[1][:, 0:TL], ALU.add)
                                    drain(eng)
                                    op(eng, 'tensor_tensor', B[2][:, cu], B[2][:, cu], A[2][:, cu], ALU.add)
                                    op(eng, 'tensor_tensor', B[3][:, cu], B[3][:, cu], A[3][:, cu], ALU.add)
                                    if pre is not None:
                                        for q in range(4):
                                            op(eng, 'tensor_copy', B[q][:, pre], A[q][:, pre])
                                    drain(eng)
                                    cur = 1 - cur
                                    s_ *= 2
                                Afin = png[cur]
                                op(eng, 'tensor_copy', sc_ar[:, cs], Afin[0][:, pad:])
                                op(eng, 'tensor_copy', sc_ai[:, cs], Afin[1][:, pad:])
                                op(eng, 'tensor_copy', sc_ur[:, cs], Afin[2][:, pad:])
                                op(eng, 'tensor_copy', sc_ui[:, cs], Afin[3][:, pad:], inc=sem_)
                                u_fin[c] = sem_.n
                                drain(eng)

                        scan_chunks('g', s_gp, pngG, tmpG, [0, 1, 2, 3], 0)
                        wait('v', s_dve, front_done)
                        wait('v', s_act, front_act)
                        wait('v', s_mm, front_mm)
                        scan_chunks('v', s_dve, pngV, tmpV, [4, 5, 6, 7], PAD)
                        f2a.close()
                        agrow = f2b("agrow", [1, 512], F32)
                        gath = f2b("gath", [world, 512], F32)
                        selrow = f2b("selrow", [1, 512], F32)
                        Mrow = f2b("Mrow3", [1, D], BF16)
                        x1ring3 = f2b("x1ring3", [128, 2, D], F32)
                        obuf = f2b("obufB", [128, 2, D], F32)
                        zt = f2b("ztB", [128, 512], F32)
                        zf = f2b("zfB", [128, 2, D], F32)
                        wait('y', s_gp, s_gp.n)
                        wait('y', s_dve, s_dve.n)
                        op('y', 'dma_start', out=Mrow[0:1, :], in_=mrow_scr[:, :], inc=s_ld0, k=16)
                        wait('t', s_ld0, s_ld0.n)
                        wait('v', s_gp, front_gp)
                        wait('v', s_act, front_act)
                        for c in range(NCHK):
                            cs = slice(c * CH, (c + 1) * CH)
                            init = 0.0 if c == 0 else r128[0][0:1, 0:1]
                            op('v', 'tensor_tensor_scan', scr128[0:1, :], a128b[0:1, cs], u128b[0:1, cs],
                               init, ALU.mult, ALU.add)
                            drain('v')
                            op('v', 'tensor_copy', r128[0][0:1, :], scr128[0:1, CH - 1:CH])
                            drain('v')
                        wait('v', s_gp, u_fin[3])
                        op('v', 'tensor_copy', Fre[:, :], sc_ur[:, CH - 1:CH])
                        op('v', 'tensor_copy', Fim[:, :], sc_ui[:, CH - 1:CH])
                        drain('v')
                        for c in range(1, NCHK):
                            le = (c + 1) * CH - 1
                            ALr = sc_ar[:, le:le + 1]
                            ALi = sc_ai[:, le:le + 1]
                            hLr = sc_ur[:, le:le + 1]
                            hLi = sc_ui[:, le:le + 1]
                            op('v', 'tensor_tensor', tc4[0][:, :], ALr, Fre[:, :], ALU.mult)
                            op('v', 'tensor_tensor', tc4[1][:, :], ALi, Fim[:, :], ALU.mult)
                            op('v', 'tensor_tensor', tc4[2][:, :], ALr, Fim[:, :], ALU.mult)
                            op('v', 'tensor_tensor', tc4[3][:, :], ALi, Fre[:, :], ALU.mult)
                            drain('v')
                            op('v', 'tensor_tensor', nFre[:, :], tc4[0][:, :], tc4[1][:, :], ALU.subtract)
                            op('v', 'tensor_tensor', nFim[:, :], tc4[2][:, :], tc4[3][:, :], ALU.add)
                            drain('v')
                            op('v', 'tensor_tensor', Fre[:, :], nFre[:, :], hLr, ALU.add)
                            op('v', 'tensor_tensor', Fim[:, :], nFim[:, :], hLi, ALU.add)
                            drain('v')
                        op('v', 'memset', agrow[0:1, :], 0.0, inc=s_dve)
                        fcol_done = s_dve.n
                        wait('t', s_dve, fcol_done)
                        em.wait_res('t', rPJ[0])
                        op('t', 'transpose', pPJ[0:1, 0, 0:128], Fre[:, :], c_identf[:, :], inc=s_mm)
                        wait('v', s_mm, s_mm.n)
                        drain('v')
                        op('v', 'tensor_copy', agrow[0:1, 0:128], pPJ[0:1, 0, 0:128], inc=s_dve)
                        rPJ[0].ready(s_dve)
                        em.wait_res('t', rPJ[1])
                        op('t', 'transpose', pPJ[0:1, 1, 0:128], Fim[:, :], c_identf[:, :], inc=s_mm)
                        wait('v', s_mm, s_mm.n)
                        drain('v')
                        op('v', 'tensor_copy', agrow[0:1, 128:256], pPJ[0:1, 1, 0:128], inc=s_dve)
                        rPJ[1].ready(s_dve)
                        op('v', 'tensor_copy', agrow[0:1, 256:257], r128[0][0:1, 0:1], inc=s_dve)
                        ag_ready = s_dve.n
                        wait('g', s_dve, ag_ready)
                        op('g', 'dma_start', out=cc_in[:, :], in_=agrow[0:1, :], inc=s_wl0, k=16)
                        wait('g', s_wl0, s_wl0.n)
                        op('g', 'collective_compute', "AllGather", ALU.bypass,
                           replica_groups=[list(range(world))],
                           ins=[cc_in.ap().opt()], outs=[cc_out.ap().opt()], inc=s_cc)
                        wait('g', s_cc, s_cc.n)
                        op('g', 'dma_start', out=gath[:, :], in_=cc_out[:, :], inc=s_wl0, k=16)
                        gath_val = s_wl0.n
                        wait('t', s_wl0, gath_val)
                        em.wait_res('t', rPJ[0])
                        op('t', 'matmul', pPJ[0:1, 0, :], c_sel[:, :], gath[:, :], start=True, stop=True,
                           inc=s_mm)
                        wait('v', s_mm, s_mm.n)
                        drain('v')
                        op('v', 'tensor_copy', selrow[0:1, :], pPJ[0:1, 0, :], inc=s_dve)
                        rPJ[0].ready(s_dve)
                        wait('t', s_dve, s_dve.n)
                        em.wait_res('t', rPJ[0])
                        op('t', 'transpose', pPJ[:, 0, 0:1], selrow[0:1, 0:128], c_identf[0:1, 0:1], inc=s_mm)
                        em.wait_res('t', rPJ[1])
                        op('t', 'transpose', pPJ[:, 1, 0:1], selrow[0:1, 128:256], c_identf[0:1, 0:1], inc=s_mm)
                        wait('v', s_mm, s_mm.n)
                        drain('v')
                        op('v', 'tensor_copy', c0re[:, :], pPJ[:, 0, 0:1])
                        op('v', 'tensor_copy', c0im[:, :], pPJ[:, 1, 0:1])
                        op('v', 'tensor_copy', c128c[0:1, :], selrow[0:1, 256:257], inc=s_dve)
                        rPJ[0].ready(s_dve)
                        rPJ[1].ready(s_dve)
                        drain('v')
                        rY3 = [Res(), Res(), Res()]
                        yv = [None] * NTILE
                        ysl = [None] * NTILE
                        ctr3 = {'y': 0}

                        def stP(i, gate):
                            toff = i * 128
                            xs = i % 2
                            em.wait_res('y', rX1[xs])
                            op('y', 'dma_start', out=x1ring3[:, xs, :], in_=x1s[toff:toff + 128, :],
                               inc=l1[xs], k=16)
                            wait('v', l1[xs], l1[xs].n)
                            em.wait_res('v', rZ[xs])
                            drain('v')
                            op('v', 'tensor_scalar', zf[:, xs, :], x1ring3[:, xs, :],
                               bnm[:, i:i + 1], bnr[:, i:i + 1], ALU.subtract, ALU.mult, inc=s_dve)
                            vals = []
                            slots = []
                            wait('t', s_act, gate)
                            for dt_ in range(2):
                                dsl = slice(dt_ * 512, (dt_ + 1) * 512)
                                yp = ctr3['y'] % 3
                                ctr3['y'] += 1
                                em.wait_res('t', rY3[yp])
                                op('t', 'matmul', pY[:, yp, :], hrebf[:, toff:toff + 128],
                                   M2reT[:, dsl], start=True, stop=False)
                                op('t', 'matmul', pY[:, yp, :], himbf[:, toff:toff + 128],
                                   M2imT[:, dsl], start=False, stop=False)
                                op('t', 'matmul', pY[:, yp, :], h128bf[0:1, toff:toff + 128],
                                   Mrow[0:1, dsl], start=False, stop=True, inc=s_mm)
                                vals.append(s_mm.n)
                                slots.append(yp)
                            yv[i] = vals
                            ysl[i] = slots

                        def stQ(i):
                            toff = i * 128
                            xs = i % 2
                            os_ = i % 2
                            for dt_ in range(2):
                                dsl = slice(dt_ * 512, (dt_ + 1) * 512)
                                yp = ysl[i][dt_]
                                wait('v', s_mm, yv[i][dt_])
                                if dt_ == 0:
                                    em.wait_res('v', rO[os_])
                                drain('v')
                                op('v', 'tensor_tensor', zt[:, :], zf[:, xs, dsl], c_Dsk[:, dsl],
                                   ALU.mult)
                                op('v', 'tensor_tensor', obuf[:, os_, dsl], pY[:, yp, :], c_WCb[:, dsl],
                                   ALU.add)
                                drain('v')
                                op('v', 'tensor_tensor', obuf[:, os_, dsl], obuf[:, os_, dsl], zt[:, :],
                                   ALU.add)
                                drain('v')
                                op('v', 'tensor_tensor', obuf[:, os_, dsl], obuf[:, os_, dsl],
                                   x1ring3[:, xs, dsl], ALU.add, inc=s_dve)
                                rY3[yp].ready(s_dve)
                            rX1[xs].ready(s_dve)
                            rZ[xs].ready(s_dve)
                            wait('y', s_dve, s_dve.n)
                            op('y', 'dma_start', out=dst[toff:toff + 128, :], in_=obuf[:, os_, :],
                               inc=so[os_], k=16)
                            store_events.append((so[os_], so[os_].n))
                            rO[os_].ready(so[os_])

                        prev_tile = None
                        for c in range(NCHK):
                            cs = slice(c * CH, (c + 1) * CH)
                            cre = c0re if c == 0 else ccre[(c - 1) % 2]
                            cim = c0im if c == 0 else ccim[(c - 1) % 2]
                            op('v', 'tensor_scalar', ncim[:, :], cim[:, :], -1.0, None, ALU.mult)
                            drain('v')
                            op('v', 'scalar_tensor_tensor', sc_ur[:, cs], sc_ar[:, cs], cre[:, 0:1],
                               sc_ur[:, cs], ALU.mult, ALU.add)
                            op('v', 'scalar_tensor_tensor', sc_ui[:, cs], sc_ar[:, cs], cim[:, 0:1],
                               sc_ui[:, cs], ALU.mult, ALU.add)
                            drain('v')
                            op('v', 'scalar_tensor_tensor', sc_ur[:, cs], sc_ai[:, cs], ncim[:, 0:1],
                               sc_ur[:, cs], ALU.mult, ALU.add)
                            op('v', 'scalar_tensor_tensor', sc_ui[:, cs], sc_ai[:, cs], cre[:, 0:1],
                               sc_ui[:, cs], ALU.mult, ALU.add)
                            drain('v')
                            le = (c + 1) * CH - 1
                            op('v', 'tensor_copy', ccre[c % 2][:, :], sc_ur[:, le:le + 1])
                            op('v', 'tensor_copy', ccim[c % 2][:, :], sc_ui[:, le:le + 1], inc=s_dve)
                            drain('v')
                            casc_v = s_dve.n
                            wait('s', s_dve, casc_v)
                            op('s', 'copy', hrebf[:, cs], sc_ur[:, cs])
                            op('s', 'copy', himbf[:, cs], sc_ui[:, cs], inc=s_act)
                            init = c128c[0:1, 0:1] if c == 0 else r128[1][0:1, 0:1]
                            em.wait_res('v', rSCR)
                            op('v', 'tensor_tensor_scan', scr128[0:1, :], a128b[0:1, cs], u128b[0:1, cs],
                               init, ALU.mult, ALU.add)
                            drain('v')
                            op('v', 'tensor_copy', r128[1][0:1, :], scr128[0:1, CH - 1:CH], inc=s_dve)
                            drain('v')
                            wait('s', s_dve, s_dve.n)
                            op('s', 'copy', h128bf[0:1, cs], scr128[0:1, :], inc=s_act)
                            rSCR.ready(s_act)
                            hbf_c = s_act.n
                            for k in range(4):
                                i = c * 4 + k
                                if prev_tile is not None:
                                    stQ(prev_tile)
                                stP(i, hbf_c)
                                prev_tile = i
                        stQ(prev_tile)
                        em.run_block()
            return store_events

        def so_totals():
            return ((so[0], so[0].n), (so[1], so[1].n))

        if phases == 1:
            emit_ffn(x_in, y_out, "A")
        elif phases == 2:
            ev1 = emit_ffn(x_in, x1s, "A")
            emit_attn(y_out, ev1, entry=so_totals())
        else:
            ev1 = emit_ffn(x_in, x1s, "A")
            ev2 = emit_attn(x2s, ev1, entry=so_totals())
            emit_ffn(x2s, y_out, "C", src_events=ev2, entry=so_totals())

        fin = Emit(nc)
        fin.wait('y', so[0], so[0].n)
        fin.wait('y', so[1], so[1].n)
        fin.run_block()

    return nc


def _prep_inputs(L0_attn_Wa_w, L0_attn_Wa_b, L0_attn_log_gamma, L0_attn_WB_w, L0_attn_WB_b,
                 L0_attn_WC_w, L0_attn_WC_b, L0_attn_D_skip, L0_attn_ln_w, L0_attn_ln_b,
                 L0_ffn_router_w, L0_ffn_sg, L0_ffn_su, L0_ffn_sd, L0_ffn_eg, L0_ffn_eu,
                 L0_ffn_ed):
    f32 = np.float32
    ac = np.ascontiguousarray
    n = np.arange(NST)
    ang = 2.0 * np.pi * np.outer(n, n) / NST
    Cfull = np.cos(ang)
    Sfull = np.sin(ang)
    rep = lambda v: ac(np.broadcast_to(np.asarray(v, f32)[None, :], (128, D)))
    return dict(
        rtW=ac(np.asarray(L0_ffn_router_w, f32).T),
        sgT=ac(np.asarray(L0_ffn_sg, f32).T),
        suT=ac(np.asarray(L0_ffn_su, f32).T),
        sdT=ac(np.asarray(L0_ffn_sd, f32).T),
        egT=ac(np.asarray(L0_ffn_eg, f32).transpose(0, 2, 1)),
        euT=ac(np.asarray(L0_ffn_eu, f32).transpose(0, 2, 1)),
        edT=ac(np.asarray(L0_ffn_ed, f32).transpose(0, 2, 1)),
        WaT=ac(np.asarray(L0_attn_Wa_w, f32).T),
        WBT=ac(np.asarray(L0_attn_WB_w, f32).T),
        WCT=ac(np.asarray(L0_attn_WC_w, f32).T),
        Wab=ac(np.asarray(L0_attn_Wa_b, f32).reshape(2, 128).T),
        WBb=ac(np.asarray(L0_attn_WB_b, f32).reshape(2, 128).T),
        lgam=np.asarray(L0_attn_log_gamma, f32).reshape(1, 1),
        Ctab=ac(Cfull[:, 0:KH].astype(f32)),
        Stab=ac(Sfull[:, 0:KH].astype(f32)),
        c128=ac(Cfull[:, KH].astype(f32).reshape(2, 128).T),
        ident=np.eye(128, dtype=f32),
        lnwrs=ac(np.asarray(L0_attn_ln_w, f32).reshape(DK, 128).T),
        lnbrs=ac(np.asarray(L0_attn_ln_b, f32).reshape(DK, 128).T),
        WCb_rep=rep(L0_attn_WC_b),
        Dsk_rep=rep(L0_attn_D_skip),
    )


def kernel(L0_attn_x, **kw):
    f32 = np.float32
    TLOC = 4096
    xflat = np.ascontiguousarray(np.asarray(L0_attn_x, f32).reshape(-1, D))
    common = _prep_inputs(**kw)
    nc = build_program(TLOC=TLOC, phases=3)
    in_maps = []
    for c in range(8):
        sel = np.zeros((8, 1), f32)
        if c % 2 == 1:
            sel[c - 1, 0] = 1.0
        m = dict(common)
        m["x"] = xflat[c * TLOC:(c + 1) * TLOC]
        m["sel"] = sel
        in_maps.append(m)
    res = run_bass_kernel_spmd(nc, in_maps, core_ids=list(range(8)))
    out = np.concatenate([res.results[c]["y"] for c in range(8)], axis=0)
    return out.reshape(L0_attn_x.shape).astype(f32)


# version 17 (final)
# speedup vs baseline: 1.0224x; 1.0183x over previous
import contextlib
import numpy as np
import concourse.bass as bass
import concourse.mybir as mybir
from concourse.bass_utils import run_bass_kernel_spmd

F32 = mybir.dt.float32
BF16 = mybir.dt.bfloat16
AF = mybir.ActivationFunctionType
ALU = mybir.AluOpType
AX = mybir.AxisListType

D = 1024
DK = 8
NST = 256
KH = 128
H = 2048
HC = 256
NHC = H // HC
HS = HC // 128
NE = 8
LN_EPS = 1e-5


class Sem:
    def __init__(self, h):
        self.h = h
        self.n = 0


class Res:
    def __init__(self):
        self.cons = []

    def ready(self, sem, val=None):
        self.cons = [(sem, sem.n if val is None else val)]

    def also(self, sem, val=None):
        self.cons.append((sem, sem.n if val is None else val))


class Emit:
    def __init__(self, nc):
        self.nc = nc
        self.ops = {k: [] for k in 'tvsgy'}

    def op(self, eng, method, *args, inc=None, k=1, **kw):
        semh = inc.h if inc is not None else None
        if inc is not None:
            inc.n += k

        def f(e, m=method, a=args, kw=kw, s=semh, k=k):
            r = getattr(e, m)(*a, **kw)
            if s is not None:
                r.then_inc(s, k)
        self.ops[eng].append(f)
        return inc.n if inc is not None else None

    def wait(self, eng, sem, val=None):
        v = sem.n if val is None else val
        self.ops[eng].append(lambda e, s=sem.h, v=v: e.wait_ge(s, v))

    def wait_res(self, eng, res):
        for sem, val in res.cons:
            self.wait(eng, sem, val)

    def drain(self, eng):
        self.ops[eng].append(lambda e: e.drain())

    def run_block(self):
        nc = self.nc
        ops = self.ops
        with nc.Block() as blk:
            @blk.tensor
            def _(t):
                for f in ops['t']:
                    f(t)

            @blk.vector
            def _(v):
                for f in ops['v']:
                    f(v)

            @blk.scalar
            def _(s):
                for f in ops['s']:
                    f(s)

            @blk.gpsimd
            def _(g):
                for f in ops['g']:
                    f(g)

            @blk.sync
            def _(y):
                for f in ops['y']:
                    f(y)
        self.ops = {k: [] for k in 'tvsgy'}


def build_program(TLOC=4096, phases=3, world=8):
    TGRP = min(2048, TLOC)
    NGRP = TLOC // TGRP
    NTT = TGRP // 512
    NTM = TGRP // 128
    NTILE = TLOC // 128
    CH = min(512, TLOC)
    NCHK = TLOC // CH
    PAD = CH // 2
    NLEV = CH.bit_length() - 1

    nc = bass.Bass("TRN2", target_bir_lowering=False)

    x_in = nc.dram_tensor("x", [TLOC, D], F32, kind="ExternalInput").ap()
    y_out = nc.dram_tensor("y", [TLOC, D], F32, kind="ExternalOutput").ap()
    d_sel = nc.dram_tensor("sel", [world, 1], F32, kind="ExternalInput").ap()

    d_rtW = nc.dram_tensor("rtW", [D, NE], F32, kind="ExternalInput").ap()
    d_sgT = nc.dram_tensor("sgT", [D, H], F32, kind="ExternalInput").ap()
    d_suT = nc.dram_tensor("suT", [D, H], F32, kind="ExternalInput").ap()
    d_sdT = nc.dram_tensor("sdT", [H, D], F32, kind="ExternalInput").ap()
    d_egT = nc.dram_tensor("egT", [NE, D, H], F32, kind="ExternalInput").ap()
    d_euT = nc.dram_tensor("euT", [NE, D, H], F32, kind="ExternalInput").ap()
    d_edT = nc.dram_tensor("edT", [NE, H, D], F32, kind="ExternalInput").ap()

    d_WaT = nc.dram_tensor("WaT", [D, NST], F32, kind="ExternalInput").ap()
    d_WBT = nc.dram_tensor("WBT", [D, NST], F32, kind="ExternalInput").ap()
    d_WCT = nc.dram_tensor("WCT", [NST, D], F32, kind="ExternalInput").ap()
    d_Wab = nc.dram_tensor("Wab", [128, 2], F32, kind="ExternalInput").ap()
    d_WBb = nc.dram_tensor("WBb", [128, 2], F32, kind="ExternalInput").ap()
    d_lgam = nc.dram_tensor("lgam", [1, 1], F32, kind="ExternalInput").ap()
    d_Ctab = nc.dram_tensor("Ctab", [NST, KH], F32, kind="ExternalInput").ap()
    d_Stab = nc.dram_tensor("Stab", [NST, KH], F32, kind="ExternalInput").ap()
    d_c128 = nc.dram_tensor("c128", [128, 2], F32, kind="ExternalInput").ap()
    d_ident = nc.dram_tensor("ident", [128, 128], F32, kind="ExternalInput").ap()
    d_lnwrs = nc.dram_tensor("lnwrs", [128, DK], F32, kind="ExternalInput").ap()
    d_lnbrs = nc.dram_tensor("lnbrs", [128, DK], F32, kind="ExternalInput").ap()
    d_WCb = nc.dram_tensor("WCb_rep", [128, D], F32, kind="ExternalInput").ap()
    d_Dsk = nc.dram_tensor("Dsk_rep", [128, D], F32, kind="ExternalInput").ap()

    x1s = nc.dram_tensor("x1s", [TLOC, D], F32).ap()
    mrow_scr = nc.dram_tensor("mrow_scr", [1, D], BF16).ap()
    x2s = nc.dram_tensor("x2s", [TLOC, D], F32).ap()
    cc_in = nc.dram_tensor("cc_in", [1, 512], F32)
    cc_out = nc.dram_tensor("cc_out", [world, 512], F32)

    em = Emit(nc)
    stk = contextlib.ExitStack()

    def sb(name, shape, dt):
        return stk.enter_context(nc.sbuf_tensor(name, shape, dt))

    def sem(name):
        return Sem(stk.enter_context(nc.semaphore(name)))

    with stk:
        s_mm = sem("s_mm")
        s_act = sem("s_act")
        s_dve = sem("s_dve")
        s_gp = sem("s_gp")
        s_cc = sem("s_cc")
        s_ld0 = sem("s_ld0")
        s_wl0 = sem("s_wl0")
        lx = [sem(f"lx{j}") for j in range(2)]
        so = [sem(f"so{j}") for j in range(2)]
        wl = [sem(f"wl{j}") for j in range(3)]
        wl2 = [sem(f"wl2{j}") for j in range(2)]
        l1 = [sem(f"l1{j}") for j in range(2)]

        c_identb = sb("c_identb", [128, 128], BF16)
        c_identf = sb("c_identf", [128, 128], F32)
        c_rtW = sb("c_rtW", [128, DK, NE], BF16)
        c_WCT = sb("c_WCT", [128, 2, D], BF16)
        c_C = sb("c_C", [128, 2, KH], BF16)
        c_S = sb("c_S", [128, 2, KH], BF16)
        c_c128 = sb("c_c128", [128, 2], BF16)
        c_Wab = sb("c_Wab", [128, 2], F32)
        c_WBb = sb("c_WBb", [128, 2], BF16)
        c_WCb = sb("c_WCb", [128, D], F32)
        c_Dsk = sb("c_Dsk", [128, D], F32)
        c_gamma = sb("c_gamma", [128, 1], F32)
        c_ones = sb("c_ones", [1, 128], F32)
        c_lgam = sb("c_lgam", [1, 1], F32)
        c_sel = sb("c_sel", [world, 1], F32)
        M2reT = sb("M2reT", [128, D], BF16)
        M2imT = sb("M2imT", [128, D], BF16)
        fac = sb("fac", [128, 1], F32)
        fbre = sb("fbre", [128, 1], F32)
        fbim = sb("fbim", [128, 1], F32)
        fb128 = sb("fb128", [1, 1], F32)
        c_eps = sb("c_eps", [128, 1], F32)
        c_two = sb("c_two", [128, 1], F32)

        op, wait, drain = em.op, em.wait, em.drain

        for dst, src in ((c_identf, d_ident),
                         (c_WCb, d_WCb), (c_Dsk, d_Dsk)):
            op('y', 'dma_start', out=dst[:, :], in_=src[:, :], inc=s_ld0, k=16)
        op('y', 'dma_start', out=c_lgam[:, :], in_=d_lgam[:, :], inc=s_ld0, k=16)
        op('y', 'dma_start', out=c_sel[:, :], in_=d_sel[:, :], inc=s_ld0, k=16)
        op('y', 'dma_start', out=c_Wab[:, :], in_=d_Wab[:, :], inc=s_ld0, k=16)
        op('g', 'dma_start', out=c_identb[:, :], in_=d_ident[:, :], inc=s_wl0, k=16)
        op('g', 'dma_start', out=c_rtW[:, :, :],
           in_=d_rtW.rearrange("(c p) e -> p c e", p=128), inc=s_wl0, k=16)
        op('g', 'dma_start', out=c_WCT[:, :, :],
           in_=d_WCT.rearrange("(g p) d -> p g d", p=128), inc=s_wl0, k=16)
        op('g', 'dma_start', out=c_C[:, :, :],
           in_=d_Ctab.rearrange("(g p) k -> p g k", p=128), inc=s_wl0, k=16)
        op('g', 'dma_start', out=c_S[:, :, :],
           in_=d_Stab.rearrange("(g p) k -> p g k", p=128), inc=s_wl0, k=16)
        op('g', 'dma_start', out=c_c128[:, :], in_=d_c128[:, :], inc=s_wl0, k=16)
        op('g', 'dma_start', out=c_WBb[:, :], in_=d_WBb[:, :], inc=s_wl0, k=16)
        ld_all = s_ld0.n
        wl_all = s_wl0.n

        wait('s', s_ld0, ld_all)
        op('s', 'activation', c_lgam[:, :], c_lgam[:, :], AF.Sigmoid, inc=s_act)
        op('v', 'memset', c_ones[:, :], 1.0)
        op('v', 'memset', fac[:, :], 2.0)
        op('v', 'memset', c_eps[:, :], LN_EPS)
        op('v', 'memset', c_two[:, :], 2.0)
        drain('v')
        op('v', 'memset', fac[0:1, :], 1.0, inc=s_dve)
        drain('v')

        with nc.psum_tensor("ip0", [128, 512], F32) as ip0, \
             nc.psum_tensor("ip1", [1, 512], F32) as ip1, \
             nc.sbuf_tensor("MrowT", [1, D], BF16) as Mrow:
            wait('t', s_act, s_act.n)
            wait('t', s_dve, s_dve.n)
            wait('t', s_wl0, wl_all)
            wait('t', s_ld0, ld_all)
            op('t', 'matmul', ip0[:, 0:1], c_ones[:, :], c_lgam[:, :], start=True, stop=True,
               inc=s_mm)
            wait('v', s_mm, s_mm.n)
            op('v', 'tensor_copy', c_gamma[:, :], ip0[:, 0:1], inc=s_dve)
            wait('t', s_dve, s_dve.n)
            for (tabs, dstb, sc) in ((c_C, M2reT, 1.0 / NST), (c_S, M2imT, -1.0 / NST)):
                for dt_ in range(2):
                    op('t', 'matmul', ip0[:, :], tabs[:, 0, :], c_WCT[:, 0, dt_ * 512:(dt_ + 1) * 512],
                       start=True, stop=False)
                    op('t', 'matmul', ip0[:, :], tabs[:, 1, :], c_WCT[:, 1, dt_ * 512:(dt_ + 1) * 512],
                       start=False, stop=True, inc=s_mm)
                    wait('v', s_mm, s_mm.n)
                    op('v', 'tensor_scalar', dstb[:, dt_ * 512:(dt_ + 1) * 512], ip0[:, :],
                       fac[:, 0:1], sc, ALU.mult, ALU.mult, inc=s_dve)
                    wait('t', s_dve, s_dve.n)
            for dt_ in range(2):
                op('t', 'matmul', ip1[:, :], c_c128[:, 0:1], c_WCT[:, 0, dt_ * 512:(dt_ + 1) * 512],
                   start=True, stop=False)
                op('t', 'matmul', ip1[:, :], c_c128[:, 1:2], c_WCT[:, 1, dt_ * 512:(dt_ + 1) * 512],
                   start=False, stop=True, inc=s_mm)
                wait('v', s_mm, s_mm.n)
                op('v', 'tensor_scalar', Mrow[0:1, dt_ * 512:(dt_ + 1) * 512], ip1[:, :],
                   1.0 / NST, None, ALU.mult, inc=s_dve)
                wait('t', s_dve, s_dve.n)
            wait('y', s_dve, s_dve.n)
            op('y', 'dma_start', out=mrow_scr[:, :], in_=Mrow[0:1, :], inc=s_ld0, k=16)
            wait('y', s_ld0, s_ld0.n)
            em.run_block()


        def emit_ffn(src, dst, tag, src_events=None, entry=()):
            TGRP = 1024
            NGRP = TLOC // TGRP
            NTM = TGRP // 128
            NTT = TGRP // 512
            NSC = 4
            NHS = 8
            units = [(None, 0), (None, 1)] + [(e, h) for e in range(NE) for h in range(2)]
            NU = len(units)
            store_events = [None] * (TLOC // 128)
            with contextlib.ExitStack() as fs:
                def fsb(name, shape, dt):
                    return fs.enter_context(nc.sbuf_tensor(name, shape, dt))
                xn_fm = fsb("xn_fm" + tag, [128, 2, DK, TGRP], BF16)
                actb = [fsb(f"actb{j}{tag}", [128, NHS, TGRP], BF16) for j in range(2)]
                wg = [fsb(f"wg{j}{tag}", [128, DK, 256], BF16) for j in range(3)]
                wu = [fsb(f"wu{j}{tag}", [128, DK, 256], BF16) for j in range(3)]
                wd = [fsb(f"wd{j}{tag}", [128, NHS, D], BF16) for j in range(2)]
                acc = fsb("acc" + tag, [128, NTM, D], F32)
                xring = fsb("xring" + tag, [128, 2, D], F32)
                oring = fsb("oring" + tag, [128, 2, D], F32)
                xnt = fsb("xnt" + tag, [128, 2, D], BF16)
                silb = [fsb(f"silb{j}{tag}", [128, 512], F32) for j in range(2)]
                uevb = [fsb(f"uevb{j}{tag}", [128, 512], BF16) for j in range(2)]
                cwb = fsb("cwb" + tag, [128, 2, NTM * 8], F32)
                bn6 = fsb("bn6" + tag, [128, 12], F32)
                bn2 = fsb("bn2" + tag, [128, 2], F32)
                sdv = fsb("sdv" + tag, [128, 1], F32)
                rstd = fsb("rstd" + tag, [128, 1], F32)
                lg8 = fsb("lg8" + tag, [128, 8], F32)
                m1 = fsb("m1" + tag, [128, 1], F32)
                nm1 = fsb("nm1" + tag, [128, 1], F32)
                m2 = fsb("m2" + tag, [128, 1], F32)
                dm = fsb("dm" + tag, [128, 1], F32)
                eqm = fsb("eqm" + tag, [128, 8], F32)
                lmb = fsb("lmb" + tag, [128, 8], F32)
                ex8 = fsb("ex8" + tag, [128, 8], F32)
                ed1 = fsb("ed1" + tag, [128, 1], F32)
                den = fsb("den" + tag, [128, 1], F32)
                rden = fsb("rden" + tag, [128, 1], F32)
                cwt = fsb("cwt" + tag, [128, 8], F32)

                with nc.psum_tensor("pGU" + tag, [128, 4, 512], F32) as pGU, \
                     nc.psum_tensor("pDD" + tag, [128, 2, 512], F32) as pD, \
                     nc.psum_tensor("pTR" + tag, [128, DK, 128], BF16) as pTR, \
                     nc.psum_tensor("pRT" + tag, [128, 512], F32) as pRT:

                    rGU = [Res(), Res()]
                    rD = [Res(), Res()]
                    rACT = [Res(), Res()]
                    rSIL = [Res(), Res()]
                    rW = [Res(), Res(), Res()]
                    rWD = [Res(), Res()]
                    rX = [Res(), Res()]
                    rO = [Res(), Res()]
                    rXN = [Res(), Res()]
                    rTR = Res()
                    rRT = Res()

                    for eng in 'tvsgy':
                        for bsem, bval in entry:
                            wait(eng, bsem, bval)

                    ctr = {'x': 0, 'o': 0, 'xn': 0, 'gu': 0, 'dsl': 0}
                    side1 = []
                    side5 = []

                    def pop_side(q):
                        if q:
                            q.pop(0)()

                    def flush(q):
                        while q:
                            q.pop(0)()

                    def load_src(gt):
                        xs = ctr['x'] % 2
                        ctr['x'] += 1
                        em.wait_res('y', rX[xs])
                        if src_events is not None:
                            gsem, gval = src_events[gt]
                            wait('y', gsem, gval)
                        op('y', 'dma_start', out=xring[:, xs, :],
                           in_=src[gt * 128:(gt + 1) * 128, :], inc=lx[xs], k=16)
                        return xs, lx[xs], lx[xs].n

                    def stage1_tile(g, i):
                        def f():
                            gt = g * NTM + i
                            xs, lsem, lval = load_src(gt)
                            wait('v', lsem, lval)
                            drain('v')
                            op('v', 'bn_stats', bn6[:, 0:6], xring[:, xs, 0:512])
                            op('v', 'bn_stats', bn6[:, 6:12], xring[:, xs, 512:1024])
                            drain('v')
                            op('v', 'bn_aggr', bn2[:, :], bn6[:, :], inc=s_dve)
                            wait('s', s_dve, s_dve.n)
                            op('s', 'activation', sdv[:, :], bn2[:, 1:2], AF.Sqrt,
                               bias=c_eps[:, 0:1], inc=s_act)
                            wait('v', s_act, s_act.n)
                            op('v', 'reciprocal', rstd[:, :], sdv[:, :])
                            drain('v')
                            ns = ctr['xn'] % 2
                            ctr['xn'] += 1
                            em.wait_res('v', rXN[ns])
                            op('v', 'tensor_scalar', xnt[:, ns, :], xring[:, xs, :],
                               bn2[:, 0:1], rstd[:, 0:1], ALU.subtract, ALU.mult, inc=s_dve)
                            rX[xs].ready(s_dve)
                            xn_ready = s_dve.n
                            wait('t', s_dve, xn_ready)
                            em.wait_res('t', rTR)
                            for dk in range(DK):
                                op('t', 'transpose', pTR[:, dk, :],
                                   xnt[:, ns, dk * 128:(dk + 1) * 128], c_identb[:, :],
                                   inc=(s_mm if dk == DK - 1 else None))
                            wait('v', s_mm, s_mm.n)
                            op('v', 'tensor_copy', xn_fm[:, g % 2, :, i * 128:(i + 1) * 128],
                               pTR[:, :, :], inc=s_dve)
                            rTR.ready(s_dve)
                            rXN[ns].ready(s_dve)
                            xnfm_tile_done = s_dve.n
                            em.wait_res('t', rRT)
                            wait('t', s_dve, xnfm_tile_done)
                            for dk in range(DK):
                                op('t', 'matmul', pRT[:, 0:8],
                                   xn_fm[:, g % 2, dk, i * 128:(i + 1) * 128], c_rtW[:, dk, :],
                                   start=(dk == 0), stop=(dk == DK - 1),
                                   inc=(s_mm if dk == DK - 1 else None))
                            wait('v', s_mm, s_mm.n)
                            drain('v')
                            op('v', 'tensor_copy', lg8[:, :], pRT[:, 0:8], inc=s_dve)
                            rRT.ready(s_dve)
                            drain('v')
                            op('v', 'tensor_reduce', m1[:, :], lg8[:, :], AX.X, ALU.max)
                            drain('v')
                            op('v', 'tensor_scalar', eqm[:, :], lg8[:, :], m1[:, 0:1], None, ALU.is_ge)
                            op('v', 'tensor_scalar', nm1[:, :], m1[:, :], -1.0, None, ALU.mult)
                            drain('v')
                            op('v', 'scalar_tensor_tensor', lmb[:, :], eqm[:, :], -1e30, lg8[:, :],
                               ALU.mult, ALU.add)
                            drain('v')
                            op('v', 'tensor_reduce', m2[:, :], lmb[:, :], AX.X, ALU.max)
                            drain('v')
                            op('v', 'tensor_tensor', dm[:, :], m2[:, :], m1[:, :], ALU.subtract,
                               inc=s_dve)
                            wait('s', s_dve, s_dve.n)
                            op('s', 'activation', ex8[:, :], lg8[:, :], AF.Exp,
                               bias=nm1[:, 0:1], scale=1.0)
                            op('s', 'activation', ed1[:, :], dm[:, :], AF.Exp, inc=s_act)
                            wait('v', s_act, s_act.n)
                            op('v', 'tensor_scalar', den[:, :], ed1[:, :], 1.0, None, ALU.add)
                            drain('v')
                            op('v', 'reciprocal', rden[:, :], den[:, :])
                            op('v', 'tensor_scalar', eqm[:, :], lg8[:, :], m2[:, 0:1], None, ALU.is_ge)
                            drain('v')
                            op('v', 'tensor_tensor', cwt[:, :], ex8[:, :], eqm[:, :], ALU.mult)
                            drain('v')
                            op('v', 'tensor_scalar', cwb[:, g % 2, i * 8:(i + 1) * 8], cwt[:, :],
                               rden[:, 0:1], 0.5, ALU.mult, ALU.mult, inc=s_dve)
                        return f

                    def stage5_tile(g, tm, acc_done):
                        def f():
                            gt = g * NTM + tm
                            xs, lsem, lval = load_src(gt)
                            os_ = ctr['o'] % 2
                            ctr['o'] += 1
                            wait('v', lsem, lval)
                            em.wait_res('v', rO[os_])
                            drain('v')
                            op('v', 'tensor_tensor', oring[:, os_, :], xring[:, xs, :],
                               acc[:, tm, :], ALU.add, inc=s_dve)
                            rX[xs].ready(s_dve)
                            wait('y', s_dve, s_dve.n)
                            op('y', 'dma_start', out=dst[gt * 128:(gt + 1) * 128, :],
                               in_=oring[:, os_, :], inc=so[os_], k=16)
                            store_events[gt] = (so[os_], so[os_].n)
                            rO[os_].ready(so[os_])
                        return f

                    guw_ev = {}
                    wd_ev = {}
                    GSEQ_MAX = NGRP * NU * NSC
                    USEQ_MAX = NGRP * NU

                    def sched_guw(seq):
                        if seq >= GSEQ_MAX:
                            return
                        u = (seq // NSC) % NU
                        sc = seq % NSC
                        slot = seq % 3
                        e, half = units[u]
                        h0 = half * 1024 + sc * 256
                        if e is None:
                            gsrc = d_sgT.rearrange("(c p) h -> p c h", p=128)[:, :, h0:h0 + 256]
                            usrc = d_suT.rearrange("(c p) h -> p c h", p=128)[:, :, h0:h0 + 256]
                        else:
                            gsrc = d_egT[e].rearrange("(c p) h -> p c h", p=128)[:, :, h0:h0 + 256]
                            usrc = d_euT[e].rearrange("(c p) h -> p c h", p=128)[:, :, h0:h0 + 256]
                        em.wait_res('g', rW[slot])
                        op('g', 'dma_start', out=wg[slot][:, :, :], in_=gsrc, inc=wl[slot], k=16)
                        op('g', 'dma_start', out=wu[slot][:, :, :], in_=usrc, inc=wl[slot], k=16)
                        guw_ev[seq] = (slot, wl[slot], wl[slot].n)

                    def sched_wd(useq):
                        if useq >= USEQ_MAX:
                            return
                        u = useq % NU
                        slot = useq % 2
                        e, half = units[u]
                        if e is None:
                            dsrc = d_sdT.rearrange("(c p) d -> p c d", p=128)[:, half * 8:(half + 1) * 8, :]
                        else:
                            dsrc = d_edT[e].rearrange("(c p) d -> p c d", p=128)[:, half * 8:(half + 1) * 8, :]
                        em.wait_res('g', rWD[slot])
                        op('g', 'dma_start', out=wd[slot][:, :, :], in_=dsrc, inc=wl2[slot], k=16)
                        wd_ev[useq] = (slot, wl2[slot], wl2[slot].n)

                    act_done = [0, 0]

                    def emit_down(g, u):
                        e, half = units[u]
                        uslot = u % 2
                        wdslot, wdsem, wdval = wd_ev[g * NU + u]
                        wait('t', wdsem, wdval)
                        wait('t', s_dve, act_done[uslot])

                        for tm in range(NTM):
                            for dt_ in range(2):
                                pp = ctr['dsl'] % 2
                                ctr['dsl'] += 1
                                em.wait_res('t', rD[pp])
                                for hsub in range(NHS):
                                    op('t', 'matmul', pD[:, pp, :],
                                       actb[uslot][:, hsub, tm * 128:(tm + 1) * 128],
                                       wd[wdslot][:, hsub, dt_ * 512:(dt_ + 1) * 512],
                                       start=(hsub == 0), stop=(hsub == NHS - 1),
                                       inc=(s_mm if hsub == NHS - 1 else None))
                                wait('v', s_mm, s_mm.n)
                                aslab = acc[:, tm, dt_ * 512:(dt_ + 1) * 512]
                                if u == 0:
                                    op('v', 'tensor_scalar', aslab, pD[:, pp, :],
                                       0.5, None, ALU.mult, inc=s_dve)
                                elif e is None:
                                    op('v', 'scalar_tensor_tensor', aslab, pD[:, pp, :],
                                       0.5, aslab, ALU.mult, ALU.add, inc=s_dve)
                                else:
                                    col = cwb[:, g % 2, tm * 8 + e: tm * 8 + e + 1]
                                    op('v', 'scalar_tensor_tensor', aslab, pD[:, pp, :],
                                       col, aslab, ALU.mult, ALU.add, inc=s_dve)
                                rD[pp].ready(s_dve)
                                pop_side(side1)
                        rACT[uslot].ready(s_mm, s_mm.n)
                        rWD[wdslot].ready(s_mm, s_mm.n)
                        sched_wd(g * NU + u + 2)

                    for sq in range(3):
                        sched_guw(sq)
                    sched_wd(0)
                    sched_wd(1)
                    for i in range(NTM):
                        side1.append(stage1_tile(0, i))
                    flush(side1)

                    for g in range(NGRP):
                        xn_all = s_dve.n
                        for u in range(NU):
                            e, half = units[u]
                            uslot = u % 2
                            first_mult = True
                            for sc in range(NSC):
                                seq = (g * NU + u) * NSC + sc
                                slot, wsem, wval = guw_ev[seq]
                                wait('t', wsem, wval)
                                for hs in range(2):
                                    hsub = sc * 2 + hs
                                    for tt in range(NTT):
                                        pp = ctr['gu'] % 2
                                        ctr['gu'] += 1
                                        em.wait_res('t', rGU[pp])
                                        if u == 0 and sc == 0 and hs == 0 and tt == 0:
                                            wait('t', s_dve, xn_all)
                                        for dk in range(DK):
                                            op('t', 'matmul', pGU[:, 2 * pp, :],
                                               wg[slot][:, dk, hs * 128:(hs + 1) * 128],
                                               xn_fm[:, g % 2, dk, tt * 512:(tt + 1) * 512],
                                               start=(dk == 0), stop=(dk == DK - 1),
                                               inc=(s_mm if dk == DK - 1 else None))
                                        gval = s_mm.n
                                        for dk in range(DK):
                                            op('t', 'matmul', pGU[:, 2 * pp + 1, :],
                                               wu[slot][:, dk, hs * 128:(hs + 1) * 128],
                                               xn_fm[:, g % 2, dk, tt * 512:(tt + 1) * 512],
                                               start=(dk == 0), stop=(dk == DK - 1),
                                               inc=(s_mm if dk == DK - 1 else None))
                                        uval = s_mm.n
                                        wait('s', s_mm, gval)
                                        em.wait_res('s', rSIL[pp])
                                        op('s', 'activation', silb[pp][:, :], pGU[:, 2 * pp, :],
                                           AF.Silu)
                                        wait('s', s_mm, uval)
                                        op('s', 'activation', uevb[pp][:, :], pGU[:, 2 * pp + 1, :],
                                           AF.Identity, inc=s_act)
                                        rGU[pp].ready(s_act)
                                        wait('v', s_act, s_act.n)
                                        if first_mult:
                                            em.wait_res('v', rACT[uslot])
                                            first_mult = False
                                        op('v', 'tensor_tensor',
                                           actb[uslot][:, hsub, tt * 512:(tt + 1) * 512],
                                           silb[pp][:, :], uevb[pp][:, :], ALU.mult,
                                           inc=s_dve)
                                        rSIL[pp].ready(s_dve)
                                        pop_side(side5)
                                rW[slot].ready(s_mm, s_mm.n)
                                sched_guw(seq + 3)
                            act_done[uslot] = s_dve.n
                            if u == 12 and g + 1 < NGRP:
                                for i in range(NTM):
                                    side1.append(stage1_tile(g + 1, i))
                            if u > 0:
                                if u == 1:
                                    flush(side5)
                                emit_down(g, u - 1)
                        emit_down(g, NU - 1)
                        flush(side1)
                        acc_done = s_dve.n
                        for tm in range(NTM):
                            side5.append(stage5_tile(g, tm, acc_done))
                    flush(side5)
                    em.run_block()
            return store_events

        def emit_attn(dst, x1_events, entry=()):
            store_events = []
            with contextlib.ExitStack() as fs:
                def fsb(name, shape, dt):
                    return fs.enter_context(nc.sbuf_tensor(name, shape, dt))
                sc_ar = fsb("sc_ar", [128, TLOC], F32)
                sc_ai = fsb("sc_ai", [128, TLOC], F32)
                sc_ur = fsb("sc_ur", [128, TLOC], F32)
                sc_ui = fsb("sc_ui", [128, TLOC], F32)
                hrebf = fsb("hrebf", [128, TLOC], BF16)
                himbf = fsb("himbf", [128, TLOC], BF16)
                a128b = fsb("a128b", [1, TLOC], F32)
                u128b = fsb("u128b", [1, TLOC], F32)
                h128bf = fsb("h128bf", [1, TLOC], BF16)
                r128 = [fsb(f"r128{q}", [1, 1], F32) for q in range(2)]
                c0re = fsb("c0re", [128, 1], F32)
                c0im = fsb("c0im", [128, 1], F32)
                c128c = fsb("c128c", [1, 1], F32)

                bnm = fsb("bnm", [128, NTILE], F32)
                bnr = fsb("bnr", [128, NTILE], F32)
                pngG = [[fsb(f"pgg{j}{q}", [128, CH], F32) for q in range(4)] for j in range(2)]
                tmpG = [fsb(f"tmg{q}", [128, CH], F32) for q in range(2)]

                with nc.psum_tensor("pTRB", [128, DK, 128], BF16) as pTR, \
                     nc.psum_tensor("pPJ", [128, 4, 512], F32) as pPJ:

                    rTR = Res()
                    rPJ = [Res(), Res(), Res(), Res()]
                    rFF = [Res(), Res()]
                    rYp = [Res(), Res()]
                    rX1 = [Res(), Res()]
                    rZ = [Res(), Res()]
                    rO = [Res(), Res()]
                    rSCR = Res()

                    for eng in 'tvsgy':
                        for bsem, bval in entry:
                            wait(eng, bsem, bval)

                    chunk_rdy = [None] * NCHK
                    u_fin = [0] * NCHK

                    def scan_chunks(eng, sem_, png, tmp4, chunks, pad):
                        if pad:
                            for j in range(2):
                                op(eng, 'memset', png[j][0][:, 0:pad], 1.0)
                                op(eng, 'memset', png[j][1][:, 0:pad], 0.0)
                                op(eng, 'memset', png[j][2][:, 0:pad], 0.0)
                                op(eng, 'memset', png[j][3][:, 0:pad], 0.0)
                            drain(eng)
                        for c in chunks:
                            cs = slice(c * CH, (c + 1) * CH)
                            rv, rs = chunk_rdy[c]
                            wait(eng, s_dve, rv)
                            wait(eng, s_act, rs)
                            for q, srcb in enumerate((sc_ar, sc_ai, sc_ur, sc_ui)):
                                op(eng, 'tensor_copy', png[0][q][:, pad:], srcb[:, cs])
                            drain(eng)
                            cur = 0
                            s_ = 1
                            for lev in range(NLEV):
                                A = png[cur]
                                B = png[1 - cur]
                                if pad:
                                    cu = slice(pad, pad + CH)
                                    sh = slice(pad - s_, pad - s_ + CH)
                                    TL = CH
                                    pre = None
                                else:
                                    cu = slice(s_, CH)
                                    sh = slice(0, CH - s_)
                                    TL = CH - s_
                                    pre = slice(0, s_)
                                op(eng, 'tensor_tensor', B[0][:, cu], A[0][:, cu], A[0][:, sh], ALU.mult)
                                op(eng, 'tensor_tensor', tmp4[0][:, 0:TL], A[1][:, cu], A[1][:, sh], ALU.mult)
                                op(eng, 'tensor_tensor', B[1][:, cu], A[0][:, cu], A[1][:, sh], ALU.mult)
                                op(eng, 'tensor_tensor', tmp4[1][:, 0:TL], A[1][:, cu], A[0][:, sh], ALU.mult)
                                drain(eng)
                                op(eng, 'tensor_tensor', B[0][:, cu], B[0][:, cu], tmp4[0][:, 0:TL], ALU.subtract)
                                op(eng, 'tensor_tensor', B[1][:, cu], B[1][:, cu], tmp4[1][:, 0:TL], ALU.add)
                                op(eng, 'tensor_tensor', B[2][:, cu], A[0][:, cu], A[2][:, sh], ALU.mult)
                                op(eng, 'tensor_tensor', tmp4[0][:, 0:TL], A[1][:, cu], A[3][:, sh], ALU.mult)
                                op(eng, 'tensor_tensor', B[3][:, cu], A[0][:, cu], A[3][:, sh], ALU.mult)
                                op(eng, 'tensor_tensor', tmp4[1][:, 0:TL], A[1][:, cu], A[2][:, sh], ALU.mult)
                                drain(eng)
                                op(eng, 'tensor_tensor', B[2][:, cu], B[2][:, cu], tmp4[0][:, 0:TL], ALU.subtract)
                                op(eng, 'tensor_tensor', B[3][:, cu], B[3][:, cu], tmp4[1][:, 0:TL], ALU.add)
                                drain(eng)
                                op(eng, 'tensor_tensor', B[2][:, cu], B[2][:, cu], A[2][:, cu], ALU.add)
                                op(eng, 'tensor_tensor', B[3][:, cu], B[3][:, cu], A[3][:, cu], ALU.add)
                                if pre is not None:
                                    for q in range(4):
                                        op(eng, 'tensor_copy', B[q][:, pre], A[q][:, pre])
                                drain(eng)
                                cur = 1 - cur
                                s_ *= 2
                            Afin = png[cur]
                            op(eng, 'tensor_copy', sc_ar[:, cs], Afin[0][:, pad:])
                            op(eng, 'tensor_copy', sc_ai[:, cs], Afin[1][:, pad:])
                            op(eng, 'tensor_copy', sc_ur[:, cs], Afin[2][:, pad:])
                            op(eng, 'tensor_copy', sc_ui[:, cs], Afin[3][:, pad:], inc=sem_)
                            u_fin[c] = sem_.n
                            drain(eng)

                    with contextlib.ExitStack() as f1, \
                         nc.psum_tensor("pFF", [128, 2, 512], F32) as pFF:
                        def f1b(name, shape, dt):
                            return f1.enter_context(nc.sbuf_tensor(name, shape, dt))
                        c_WaT = f1b("c_WaT", [128, DK, NST], BF16)
                        c_WBT = f1b("c_WBT", [128, DK, NST], BF16)
                        c_lnwrs = f1b("c_lnwrs", [128, DK], F32)
                        c_lnbrs = f1b("c_lnbrs", [128, DK], BF16)
                        wbbe = f1b("wbbe", [128, 2], BF16)
                        op('g', 'dma_start', out=c_WaT[:, :, :],
                           in_=d_WaT.rearrange("(c p) n -> p c n", p=128), inc=s_ld0, k=16)
                        op('g', 'dma_start', out=c_WBT[:, :, :],
                           in_=d_WBT.rearrange("(c p) n -> p c n", p=128), inc=s_ld0, k=16)
                        op('y', 'dma_start', out=c_lnwrs[:, :], in_=d_lnwrs[:, :], inc=s_ld0, k=16)
                        op('g', 'dma_start', out=c_lnbrs[:, :], in_=d_lnbrs[:, :], inc=s_ld0, k=16)
                        waW_val = s_ld0.n
                        wait('t', s_ld0, waW_val)
                        for wsrc, ng0 in ((c_WaT, 0), (c_WBT, 2)):
                            for ng in range(2):
                                for dk in range(DK):
                                    op('t', 'matmul', pPJ[:, ng0 + ng, 0:1],
                                       wsrc[:, dk, ng * 128:(ng + 1) * 128], c_lnbrs[:, dk:dk + 1],
                                       start=(dk == 0), stop=(dk == DK - 1),
                                       inc=(s_mm if dk == DK - 1 else None))
                        fold_mm = s_mm.n
                        wait('v', s_mm, fold_mm)
                        drain('v')
                        op('v', 'tensor_tensor', c_Wab[:, 0:1], c_Wab[:, 0:1], pPJ[:, 0, 0:1], ALU.add)
                        op('v', 'tensor_tensor', c_Wab[:, 1:2], c_Wab[:, 1:2], pPJ[:, 1, 0:1], ALU.add)
                        op('v', 'tensor_tensor', wbbe[:, 0:1], c_WBb[:, 0:1], pPJ[:, 2, 0:1], ALU.add)
                        op('v', 'tensor_tensor', wbbe[:, 1:2], c_WBb[:, 1:2], pPJ[:, 3, 0:1], ALU.add)
                        drain('v')
                        for dk in range(DK):
                            op('v', 'tensor_scalar', c_WaT[:, dk, :], c_WaT[:, dk, :],
                               c_lnwrs[:, dk:dk + 1], None, ALU.mult)
                            op('v', 'tensor_scalar', c_WBT[:, dk, :], c_WBT[:, dk, :],
                               c_lnwrs[:, dk:dk + 1], None, ALU.mult,
                               inc=(s_dve if dk == DK - 1 else None))
                        fold_done = s_dve.n
                        drain('v')
                        wait('t', s_dve, fold_done)
                        op('t', 'matmul', pFF[:, 0, 0:1], c_C[:, 0, :], wbbe[:, 0:1], start=True, stop=False)
                        op('t', 'matmul', pFF[:, 0, 0:1], c_C[:, 1, :], wbbe[:, 1:2], start=False, stop=True)
                        op('t', 'matmul', pFF[:, 1, 0:1], c_S[:, 0, :], wbbe[:, 0:1], start=True, stop=False)
                        op('t', 'matmul', pFF[:, 1, 0:1], c_S[:, 1, :], wbbe[:, 1:2], start=False, stop=True)
                        op('t', 'matmul', pPJ[0:1, 0, 0:1], c_c128[:, 0:1], wbbe[:, 0:1], start=True, stop=False)
                        op('t', 'matmul', pPJ[0:1, 0, 0:1], c_c128[:, 1:2], wbbe[:, 1:2], start=False, stop=True,
                           inc=s_mm)
                        wait('v', s_mm, s_mm.n)
                        drain('v')
                        op('v', 'tensor_copy', fbre[:, :], pFF[:, 0, 0:1])
                        op('v', 'tensor_scalar', fbim[:, :], pFF[:, 1, 0:1], -1.0, None, ALU.mult)
                        op('v', 'tensor_copy', fb128[:, :], pPJ[0:1, 0, 0:1], inc=s_dve)
                        fold_done = s_dve.n
                        wait('s', s_dve, fold_done)
                        wait('t', s_dve, fold_done)
                        x1ring = f1b("x1ringF", [128, 2, D], F32)
                        zfr = f1b("zfrF", [128, 2, D], F32)
                        xntr = f1b("xntrF", [128, 2, D], BF16)
                        xfa = [f1b(f"xfa{j}", [128, DK, 128], BF16) for j in range(2)]
                        abf = f1b("abf", [128, 2, 128], BF16)
                        ubf = f1b("ubf", [128, 2, 128], BF16)
                        tanb = f1b("tanb", [128, 2, 128], BF16)
                        g1b = f1b("g1b", [128, 128], F32)
                        g2b = f1b("g2b", [128, 128], F32)
                        ggb = g1b
                        bn6 = f1b("bn6B", [128, 12], F32)
                        bn6b = bn6
                        bn2r = f1b("bn2rB", [128, 2, 2], F32)
                        bn2br = f1b("bn2brB", [128, 2, 2], F32)
                        sdq = f1b("sdqB", [128, 4, 1], F32)
                        sdvr = sdq[:, 0:2, :]
                        sdv2r = sdq[:, 2:4, :]
                        rstd2 = f1b("rstd2B", [128, 1], F32)
                        row3 = f1b("row3", [1, 384], F32)
                        ab128 = row3[0:1, 0:128]
                        gg128 = row3[0:1, 128:256]
                        raw128 = row3[0:1, 256:384]
                        ggv = [0, 0, 0]

                        rZF = [Res(), Res()]
                        rXN4 = [Res() for _ in range(4)]
                        rFA = [Res(), Res()]
                        rB2 = [Res(), Res()]
                        rB2b = [Res(), Res()]
                        rSD = [Res(), Res()]
                        rSD2 = [Res(), Res()]
                        rTAN = [Res(), Res()]

                        sq1v = [0] * NTILE
                        sq2v = [0] * NTILE
                        xnt_done = [0] * NTILE
                        aval = [None] * NTILE
                        uval = [None] * NTILE

                        def A1(i):
                            xs = i % 2
                            em.wait_res('y', rX1[xs])
                            gsem, gval = x1_events[i]
                            wait('y', gsem, gval)
                            op('y', 'dma_start', out=x1ring[:, xs, :],
                               in_=x1s[i * 128:(i + 1) * 128, :], inc=l1[xs], k=16)
                            wait('v', l1[xs], l1[xs].n)
                            drain('v')
                            op('v', 'bn_stats', bn6[:, 0:6], x1ring[:, xs, 0:512])
                            op('v', 'bn_stats', bn6[:, 6:12], x1ring[:, xs, 512:1024])
                            drain('v')
                            em.wait_res('v', rB2[xs])
                            op('v', 'bn_aggr', bn2r[:, xs, :], bn6[:, :], inc=s_dve)
                            wait('s', s_dve, s_dve.n)
                            em.wait_res('s', rSD[xs])
                            op('s', 'activation', sdvr[:, xs, :], bn2r[:, xs, 1:2], AF.Sqrt,
                               bias=c_eps[:, 0:1], inc=s_act)
                            sq1v[i] = s_act.n

                        def A2(i):
                            xs = i % 2
                            zs = i % 2
                            wait('v', s_act, sq1v[i])
                            op('v', 'reciprocal', bnr[:, i:i + 1], sdvr[:, xs, :])
                            drain('v')
                            op('v', 'tensor_copy', bnm[:, i:i + 1], bn2r[:, xs, 0:1], inc=s_dve)
                            rSD[xs].ready(s_dve)
                            rB2[xs].ready(s_dve)
                            em.wait_res('v', rZF[zs])
                            drain('v')
                            op('v', 'tensor_scalar', zfr[:, zs, :], x1ring[:, xs, :],
                               bnm[:, i:i + 1], bnr[:, i:i + 1], ALU.subtract, ALU.mult, inc=s_dve)
                            rX1[xs].ready(s_dve)
                            drain('v')
                            op('v', 'bn_stats', bn6b[:, 0:6], zfr[:, zs, 0:512])
                            op('v', 'bn_stats', bn6b[:, 6:12], zfr[:, zs, 512:1024])
                            drain('v')
                            em.wait_res('v', rB2b[xs])
                            op('v', 'bn_aggr', bn2br[:, xs, :], bn6b[:, :], inc=s_dve)
                            wait('s', s_dve, s_dve.n)
                            em.wait_res('s', rSD2[xs])
                            op('s', 'activation', sdv2r[:, xs, :], bn2br[:, xs, 1:2], AF.Sqrt,
                               bias=c_eps[:, 0:1], inc=s_act)
                            sq2v[i] = s_act.n

                        def A3(i):
                            xs = i % 2
                            zs = i % 2
                            ns = i % 2
                            wait('v', s_act, sq2v[i])
                            op('v', 'reciprocal', rstd2[:, :], sdv2r[:, xs, :], inc=s_dve)
                            rSD2[xs].ready(s_dve)
                            drain('v')
                            em.wait_res('v', rXN4[ns])
                            op('v', 'tensor_scalar', xntr[:, ns, :], zfr[:, zs, :],
                               bn2br[:, xs, 0:1], rstd2[:, 0:1], ALU.subtract, ALU.mult, inc=s_dve)
                            rB2b[xs].ready(s_dve)
                            rZF[zs].ready(s_dve)
                            xnt_done[i] = s_dve.n

                        def B1b1(i):
                            ns = i % 2
                            fa = i % 2
                            wait('t', s_dve, xnt_done[i])
                            em.wait_res('t', rTR)
                            for dk in range(DK):
                                op('t', 'transpose', pTR[:, dk, :],
                                   xntr[:, ns, dk * 128:(dk + 1) * 128], c_identb[:, :],
                                   inc=(s_mm if dk == DK - 1 else None))
                            wait('v', s_mm, s_mm.n)
                            em.wait_res('v', rFA[fa])
                            op('v', 'tensor_copy', xfa[fa][:, :, :], pTR[:, :, :], inc=s_dve)
                            rTR.ready(s_dve)
                            rXN4[ns].ready(s_dve)
                            fcv = s_dve.n
                            wait('t', s_dve, fcv)
                            av = []
                            for ng in range(2):
                                em.wait_res('t', rPJ[ng])
                                for dk in range(DK):
                                    op('t', 'matmul', pPJ[:, ng, 0:128],
                                       c_WaT[:, dk, ng * 128:(ng + 1) * 128], xfa[fa][:, dk, :],
                                       start=(dk == 0), stop=(dk == DK - 1),
                                       inc=(s_mm if dk == DK - 1 else None))
                                av.append(s_mm.n)
                            uv = []
                            for ng in range(2):
                                em.wait_res('t', rPJ[2 + ng])
                                for dk in range(DK):
                                    op('t', 'matmul', pPJ[:, 2 + ng, 0:128],
                                       c_WBT[:, dk, ng * 128:(ng + 1) * 128], xfa[fa][:, dk, :],
                                       start=(dk == 0), stop=(dk == DK - 1),
                                       inc=(s_mm if dk == DK - 1 else None))
                                uv.append(s_mm.n)
                            aval[i] = av
                            uval[i] = uv
                            rFA[fa].ready(s_mm, s_mm.n)

                        def B1b2(i):
                            toff = i * 128
                            for ng in range(2):
                                wait('s', s_mm, aval[i][ng])
                                em.wait_res('s', rTAN[ng])
                                op('s', 'activation', tanb[:, ng, :], pPJ[:, ng, 0:128], AF.Tanh,
                                   bias=c_Wab[:, ng:ng + 1], scale=1.0, inc=s_act)
                            tanv = s_act.n
                            wait('v', s_act, tanv)
                            op('v', 'tensor_scalar', abf[:, 0, :], tanb[:, 0, :],
                               c_gamma[:, 0:1], None, ALU.mult)
                            op('v', 'tensor_scalar', abf[:, 1, :], tanb[:, 1, :],
                               c_gamma[:, 0:1], None, ALU.mult, inc=s_dve)
                            rTAN[0].ready(s_dve)
                            rTAN[1].ready(s_dve)
                            rPJ[1].ready(s_dve)
                            abf_done = s_dve.n
                            wait('v', s_mm, uval[i][1])
                            drain('v')
                            op('v', 'tensor_copy', ubf[:, 0, :], pPJ[:, 2, 0:128])
                            op('v', 'tensor_copy', ubf[:, 1, :], pPJ[:, 3, 0:128], inc=s_dve)
                            rPJ[3].ready(s_dve)
                            ubf_done = s_dve.n
                            em.wait_res('t', rFF[0])
                            em.wait_res('t', rFF[1])
                            wait('t', s_dve, abf_done)
                            op('t', 'matmul', pFF[:, 0, 0:128], c_C[:, 0, :], abf[:, 0, :], start=True, stop=False)
                            op('t', 'matmul', pFF[:, 0, 0:128], c_C[:, 1, :], abf[:, 1, :], start=False, stop=True)
                            op('t', 'matmul', pFF[:, 1, 0:128], c_S[:, 0, :], abf[:, 0, :], start=True, stop=False)
                            op('t', 'matmul', pFF[:, 1, 0:128], c_S[:, 1, :], abf[:, 1, :], start=False, stop=True)
                            op('t', 'matmul', pPJ[0:1, 0, 0:128], c_c128[:, 0:1], abf[:, 0, :], start=True, stop=False)
                            op('t', 'matmul', pPJ[0:1, 0, 0:128], c_c128[:, 1:2], abf[:, 1, :], start=False, stop=True,
                               inc=s_mm)
                            ffta = s_mm.n
                            wait('s', s_mm, ffta)
                            if ggv[0]:
                                wait('s', s_dve, ggv[0])
                            op('s', 'activation', g1b[:, :], pFF[:, 0, 0:128], AF.Square)
                            op('s', 'activation', g2b[:, :], pFF[:, 1, 0:128], AF.Square, inc=s_act)
                            wait('v', s_act, s_act.n)
                            op('v', 'tensor_tensor', g1b[:, :], g1b[:, :], g2b[:, :], ALU.add, inc=s_dve)
                            wait('s', s_dve, s_dve.n)
                            op('s', 'activation', g2b[:, :], g1b[:, :], AF.Sqrt, inc=s_act)
                            wait('v', s_act, s_act.n)
                            op('v', 'tensor_scalar', g2b[:, :], g2b[:, :], 1e-8, None, ALU.max, inc=s_dve)
                            ggv[0] = s_dve.n
                            wait('s', s_dve, s_dve.n)
                            if ggv[2]:
                                wait('s', s_dve, ggv[2])
                            op('s', 'activation', ggb[:, :], g2b[:, :], AF.Sigmoid, bias=c_two[:, 0:1], scale=-2.0,
                               inc=s_act)
                            ggend = s_act.n
                            wait('v', s_act, ggend)
                            op('v', 'tensor_tensor', sc_ar[:, toff:toff + 128], pFF[:, 0, 0:128],
                               ggb[:, :], ALU.mult)
                            op('v', 'scalar_tensor_tensor', sc_ai[:, toff:toff + 128], pFF[:, 1, 0:128],
                               -1.0, ggb[:, :], ALU.mult, ALU.mult, inc=s_dve)
                            ggv[2] = s_dve.n
                            rFF[0].ready(s_dve)
                            rFF[1].ready(s_dve)
                            wait('s', s_mm, ffta)
                            if ggv[1]:
                                wait('s', s_dve, ggv[1])
                            op('s', 'activation', ab128, pPJ[0:1, 0, 0:128], AF.Abs)
                            op('s', 'activation', raw128, pPJ[0:1, 0, 0:128], AF.Identity, inc=s_act)
                            wait('v', s_act, s_act.n)
                            op('v', 'tensor_scalar', ab128, ab128, 1e-8, None, ALU.max, inc=s_dve)
                            wait('s', s_dve, s_dve.n)
                            op('s', 'activation', gg128, ab128, AF.Sigmoid, bias=c_two[0:1, 0:1], scale=-2.0,
                               inc=s_act)
                            wait('v', s_act, s_act.n)
                            op('v', 'tensor_tensor', a128b[0:1, toff:toff + 128], raw128,
                               gg128, ALU.mult, inc=s_dve)
                            ggv[1] = s_dve.n
                            rPJ[0].ready(s_dve)
                            em.wait_res('t', rFF[0])
                            em.wait_res('t', rFF[1])
                            wait('t', s_dve, ubf_done)
                            op('t', 'matmul', pFF[:, 0, 0:128], c_C[:, 0, :], ubf[:, 0, :], start=True, stop=False)
                            op('t', 'matmul', pFF[:, 0, 0:128], c_C[:, 1, :], ubf[:, 1, :], start=False, stop=True)
                            op('t', 'matmul', pFF[:, 1, 0:128], c_S[:, 0, :], ubf[:, 0, :], start=True, stop=False)
                            op('t', 'matmul', pFF[:, 1, 0:128], c_S[:, 1, :], ubf[:, 1, :], start=False, stop=True)
                            op('t', 'matmul', pPJ[0:1, 2, 0:128], c_c128[:, 0:1], ubf[:, 0, :], start=True, stop=False)
                            op('t', 'matmul', pPJ[0:1, 2, 0:128], c_c128[:, 1:2], ubf[:, 1, :], start=False, stop=True,
                               inc=s_mm)
                            fftu = s_mm.n
                            wait('s', s_mm, fftu)
                            op('s', 'activation', sc_ur[:, toff:toff + 128], pFF[:, 0, 0:128],
                               AF.Identity, bias=fbre[:, 0:1], scale=1.0)
                            op('s', 'activation', sc_ui[:, toff:toff + 128], pFF[:, 1, 0:128],
                               AF.Identity, bias=fbim[:, 0:1], scale=-1.0)
                            op('s', 'activation', u128b[0:1, toff:toff + 128], pPJ[0:1, 2, 0:128],
                               AF.Identity, bias=fb128[0:1, 0:1], scale=1.0, inc=s_act)
                            rFF[0].ready(s_act)
                            rFF[1].ready(s_act)
                            rPJ[2].ready(s_act)

                        for r in range(NTILE + 4):
                            if r < NTILE:
                                A1(r)
                            if 1 <= r < NTILE + 1:
                                A2(r - 1)
                            if 2 <= r < NTILE + 2:
                                A3(r - 2)
                            if 4 <= r:
                                B1b2(r - 4)
                                if (r - 4) % 4 == 3:
                                    cc_ = (r - 4) // 4
                                    chunk_rdy[cc_] = (s_dve.n, s_act.n)
                                    if cc_ < 4:
                                        scan_chunks('g', s_gp, pngG, tmpG, [cc_], 0)
                            if 3 <= r < NTILE + 3:
                                B1b1(r - 3)
                        front_done = s_dve.n
                        front_act = s_act.n
                        front_gp = s_gp.n
                        front_mm = s_mm.n
                        em.run_block()

                    with contextlib.ExitStack() as f2, \
                         nc.psum_tensor("pYB", [128, 3, 512], F32) as pY:
                        def f2b(name, shape, dt):
                            return f2.enter_context(nc.sbuf_tensor(name, shape, dt))
                        scr128 = f2b("scr128", [1, CH], F32)
                        Fre = f2b("Fre", [128, 1], F32)
                        Fim = f2b("Fim", [128, 1], F32)
                        nFre = f2b("nFre", [128, 1], F32)
                        nFim = f2b("nFim", [128, 1], F32)
                        tc4 = [f2b(f"tc{q}", [128, 1], F32) for q in range(4)]
                        ccre = [f2b(f"ccre{q}", [128, 1], F32) for q in range(2)]
                        ccim = [f2b(f"ccim{q}", [128, 1], F32) for q in range(2)]
                        ncim = f2b("ncim", [128, 1], F32)
                        f2a = f2.enter_context(contextlib.ExitStack())
                        def f2ab(name, shape, dt):
                            return f2a.enter_context(nc.sbuf_tensor(name, shape, dt))
                        pngV = [[f2ab(f"pgv{j}{q}", [128, PAD + CH], F32) for q in range(4)] for j in range(2)]
                        tmpV = [f2ab(f"tmv{q}", [128, CH], F32) for q in range(2)]

                        wait('v', s_dve, front_done)
                        wait('v', s_act, front_act)
                        wait('v', s_mm, front_mm)
                        scan_chunks('v', s_dve, pngV, tmpV, [4, 5, 6, 7], PAD)
                        f2a.close()
                        agrow = f2b("agrow", [1, 512], F32)
                        gath = f2b("gath", [world, 512], F32)
                        selrow = f2b("selrow", [1, 512], F32)
                        Mrow = f2b("Mrow3", [1, D], BF16)
                        x1ring3 = f2b("x1ring3", [128, 2, D], F32)
                        obuf = f2b("obufB", [128, 2, D], F32)
                        zt = f2b("ztB", [128, 512], F32)
                        zf = f2b("zfB", [128, 2, D], F32)
                        wait('y', s_gp, s_gp.n)
                        wait('y', s_dve, s_dve.n)
                        op('y', 'dma_start', out=Mrow[0:1, :], in_=mrow_scr[:, :], inc=s_ld0, k=16)
                        wait('t', s_ld0, s_ld0.n)
                        wait('v', s_gp, front_gp)
                        wait('v', s_act, front_act)
                        for c in range(NCHK):
                            cs = slice(c * CH, (c + 1) * CH)
                            init = 0.0 if c == 0 else r128[0][0:1, 0:1]
                            op('v', 'tensor_tensor_scan', scr128[0:1, :], a128b[0:1, cs], u128b[0:1, cs],
                               init, ALU.mult, ALU.add)
                            drain('v')
                            op('v', 'tensor_copy', r128[0][0:1, :], scr128[0:1, CH - 1:CH])
                            drain('v')
                        wait('v', s_gp, u_fin[3])
                        op('v', 'tensor_copy', Fre[:, :], sc_ur[:, CH - 1:CH])
                        op('v', 'tensor_copy', Fim[:, :], sc_ui[:, CH - 1:CH])
                        drain('v')
                        for c in range(1, NCHK):
                            le = (c + 1) * CH - 1
                            ALr = sc_ar[:, le:le + 1]
                            ALi = sc_ai[:, le:le + 1]
                            hLr = sc_ur[:, le:le + 1]
                            hLi = sc_ui[:, le:le + 1]
                            op('v', 'tensor_tensor', tc4[0][:, :], ALr, Fre[:, :], ALU.mult)
                            op('v', 'tensor_tensor', tc4[1][:, :], ALi, Fim[:, :], ALU.mult)
                            op('v', 'tensor_tensor', tc4[2][:, :], ALr, Fim[:, :], ALU.mult)
                            op('v', 'tensor_tensor', tc4[3][:, :], ALi, Fre[:, :], ALU.mult)
                            drain('v')
                            op('v', 'tensor_tensor', nFre[:, :], tc4[0][:, :], tc4[1][:, :], ALU.subtract)
                            op('v', 'tensor_tensor', nFim[:, :], tc4[2][:, :], tc4[3][:, :], ALU.add)
                            drain('v')
                            op('v', 'tensor_tensor', Fre[:, :], nFre[:, :], hLr, ALU.add)
                            op('v', 'tensor_tensor', Fim[:, :], nFim[:, :], hLi, ALU.add)
                            drain('v')
                        op('v', 'memset', agrow[0:1, :], 0.0, inc=s_dve)
                        fcol_done = s_dve.n
                        wait('t', s_dve, fcol_done)
                        em.wait_res('t', rPJ[0])
                        op('t', 'transpose', pPJ[0:1, 0, 0:128], Fre[:, :], c_identf[:, :], inc=s_mm)
                        wait('v', s_mm, s_mm.n)
                        drain('v')
                        op('v', 'tensor_copy', agrow[0:1, 0:128], pPJ[0:1, 0, 0:128], inc=s_dve)
                        rPJ[0].ready(s_dve)
                        em.wait_res('t', rPJ[1])
                        op('t', 'transpose', pPJ[0:1, 1, 0:128], Fim[:, :], c_identf[:, :], inc=s_mm)
                        wait('v', s_mm, s_mm.n)
                        drain('v')
                        op('v', 'tensor_copy', agrow[0:1, 128:256], pPJ[0:1, 1, 0:128], inc=s_dve)
                        rPJ[1].ready(s_dve)
                        op('v', 'tensor_copy', agrow[0:1, 256:257], r128[0][0:1, 0:1], inc=s_dve)
                        ag_ready = s_dve.n
                        wait('g', s_dve, ag_ready)
                        op('g', 'dma_start', out=cc_in[:, :], in_=agrow[0:1, :], inc=s_wl0, k=16)
                        wait('g', s_wl0, s_wl0.n)
                        op('g', 'collective_compute', "AllGather", ALU.bypass,
                           replica_groups=[list(range(world))],
                           ins=[cc_in.ap().opt()], outs=[cc_out.ap().opt()], inc=s_cc)
                        wait('g', s_cc, s_cc.n)
                        op('g', 'dma_start', out=gath[:, :], in_=cc_out[:, :], inc=s_wl0, k=16)
                        gath_val = s_wl0.n
                        wait('t', s_wl0, gath_val)
                        em.wait_res('t', rPJ[0])
                        op('t', 'matmul', pPJ[0:1, 0, :], c_sel[:, :], gath[:, :], start=True, stop=True,
                           inc=s_mm)
                        wait('v', s_mm, s_mm.n)
                        drain('v')
                        op('v', 'tensor_copy', selrow[0:1, :], pPJ[0:1, 0, :], inc=s_dve)
                        rPJ[0].ready(s_dve)
                        wait('t', s_dve, s_dve.n)
                        em.wait_res('t', rPJ[0])
                        op('t', 'transpose', pPJ[:, 0, 0:1], selrow[0:1, 0:128], c_identf[0:1, 0:1], inc=s_mm)
                        em.wait_res('t', rPJ[1])
                        op('t', 'transpose', pPJ[:, 1, 0:1], selrow[0:1, 128:256], c_identf[0:1, 0:1], inc=s_mm)
                        wait('v', s_mm, s_mm.n)
                        drain('v')
                        op('v', 'tensor_copy', c0re[:, :], pPJ[:, 0, 0:1])
                        op('v', 'tensor_copy', c0im[:, :], pPJ[:, 1, 0:1])
                        op('v', 'tensor_copy', c128c[0:1, :], selrow[0:1, 256:257], inc=s_dve)
                        rPJ[0].ready(s_dve)
                        rPJ[1].ready(s_dve)
                        drain('v')
                        rY3 = [Res(), Res(), Res()]
                        yv = [None] * NTILE
                        ysl = [None] * NTILE
                        ctr3 = {'y': 0}

                        def stP(i, gate):
                            toff = i * 128
                            xs = i % 2
                            em.wait_res('y', rX1[xs])
                            op('y', 'dma_start', out=x1ring3[:, xs, :], in_=x1s[toff:toff + 128, :],
                               inc=l1[xs], k=16)
                            wait('v', l1[xs], l1[xs].n)
                            em.wait_res('v', rZ[xs])
                            drain('v')
                            op('v', 'tensor_scalar', zf[:, xs, :], x1ring3[:, xs, :],
                               bnm[:, i:i + 1], bnr[:, i:i + 1], ALU.subtract, ALU.mult, inc=s_dve)
                            vals = []
                            slots = []
                            wait('t', s_act, gate)
                            for dt_ in range(2):
                                dsl = slice(dt_ * 512, (dt_ + 1) * 512)
                                yp = ctr3['y'] % 3
                                ctr3['y'] += 1
                                em.wait_res('t', rY3[yp])
                                op('t', 'matmul', pY[:, yp, :], hrebf[:, toff:toff + 128],
                                   M2reT[:, dsl], start=True, stop=False)
                                op('t', 'matmul', pY[:, yp, :], himbf[:, toff:toff + 128],
                                   M2imT[:, dsl], start=False, stop=False)
                                op('t', 'matmul', pY[:, yp, :], h128bf[0:1, toff:toff + 128],
                                   Mrow[0:1, dsl], start=False, stop=True, inc=s_mm)
                                vals.append(s_mm.n)
                                slots.append(yp)
                            yv[i] = vals
                            ysl[i] = slots

                        def stQ(i):
                            toff = i * 128
                            xs = i % 2
                            os_ = i % 2
                            for dt_ in range(2):
                                dsl = slice(dt_ * 512, (dt_ + 1) * 512)
                                yp = ysl[i][dt_]
                                wait('v', s_mm, yv[i][dt_])
                                if dt_ == 0:
                                    em.wait_res('v', rO[os_])
                                drain('v')
                                op('v', 'tensor_tensor', zt[:, :], zf[:, xs, dsl], c_Dsk[:, dsl],
                                   ALU.mult)
                                op('v', 'tensor_tensor', obuf[:, os_, dsl], pY[:, yp, :], c_WCb[:, dsl],
                                   ALU.add)
                                drain('v')
                                op('v', 'tensor_tensor', obuf[:, os_, dsl], obuf[:, os_, dsl], zt[:, :],
                                   ALU.add)
                                drain('v')
                                op('v', 'tensor_tensor', obuf[:, os_, dsl], obuf[:, os_, dsl],
                                   x1ring3[:, xs, dsl], ALU.add, inc=s_dve)
                                rY3[yp].ready(s_dve)
                            rX1[xs].ready(s_dve)
                            rZ[xs].ready(s_dve)
                            wait('y', s_dve, s_dve.n)
                            op('y', 'dma_start', out=dst[toff:toff + 128, :], in_=obuf[:, os_, :],
                               inc=so[os_], k=16)
                            store_events.append((so[os_], so[os_].n))
                            rO[os_].ready(so[os_])

                        prev_tile = None
                        for c in range(NCHK):
                            cs = slice(c * CH, (c + 1) * CH)
                            cre = c0re if c == 0 else ccre[(c - 1) % 2]
                            cim = c0im if c == 0 else ccim[(c - 1) % 2]
                            op('v', 'tensor_scalar', ncim[:, :], cim[:, :], -1.0, None, ALU.mult)
                            drain('v')
                            op('v', 'scalar_tensor_tensor', sc_ur[:, cs], sc_ar[:, cs], cre[:, 0:1],
                               sc_ur[:, cs], ALU.mult, ALU.add)
                            op('v', 'scalar_tensor_tensor', sc_ui[:, cs], sc_ar[:, cs], cim[:, 0:1],
                               sc_ui[:, cs], ALU.mult, ALU.add)
                            drain('v')
                            op('v', 'scalar_tensor_tensor', sc_ur[:, cs], sc_ai[:, cs], ncim[:, 0:1],
                               sc_ur[:, cs], ALU.mult, ALU.add)
                            op('v', 'scalar_tensor_tensor', sc_ui[:, cs], sc_ai[:, cs], cre[:, 0:1],
                               sc_ui[:, cs], ALU.mult, ALU.add)
                            drain('v')
                            le = (c + 1) * CH - 1
                            op('v', 'tensor_copy', ccre[c % 2][:, :], sc_ur[:, le:le + 1])
                            op('v', 'tensor_copy', ccim[c % 2][:, :], sc_ui[:, le:le + 1], inc=s_dve)
                            drain('v')
                            casc_v = s_dve.n
                            wait('s', s_dve, casc_v)
                            op('s', 'copy', hrebf[:, cs], sc_ur[:, cs])
                            op('s', 'copy', himbf[:, cs], sc_ui[:, cs], inc=s_act)
                            init = c128c[0:1, 0:1] if c == 0 else r128[1][0:1, 0:1]
                            em.wait_res('v', rSCR)
                            op('v', 'tensor_tensor_scan', scr128[0:1, :], a128b[0:1, cs], u128b[0:1, cs],
                               init, ALU.mult, ALU.add)
                            drain('v')
                            op('v', 'tensor_copy', r128[1][0:1, :], scr128[0:1, CH - 1:CH], inc=s_dve)
                            drain('v')
                            wait('s', s_dve, s_dve.n)
                            op('s', 'copy', h128bf[0:1, cs], scr128[0:1, :], inc=s_act)
                            rSCR.ready(s_act)
                            hbf_c = s_act.n
                            for k in range(4):
                                i = c * 4 + k
                                if prev_tile is not None:
                                    stQ(prev_tile)
                                stP(i, hbf_c)
                                prev_tile = i
                        stQ(prev_tile)
                        em.run_block()
            return store_events

        def so_totals():
            return ((so[0], so[0].n), (so[1], so[1].n))

        if phases == 1:
            emit_ffn(x_in, y_out, "A")
        elif phases == 2:
            ev1 = emit_ffn(x_in, x1s, "A")
            emit_attn(y_out, ev1, entry=so_totals())
        else:
            ev1 = emit_ffn(x_in, x1s, "A")
            ev2 = emit_attn(x2s, ev1, entry=so_totals())
            emit_ffn(x2s, y_out, "C", src_events=ev2, entry=so_totals())

        fin = Emit(nc)
        fin.wait('y', so[0], so[0].n)
        fin.wait('y', so[1], so[1].n)
        fin.run_block()

    return nc


def _prep_inputs(L0_attn_Wa_w, L0_attn_Wa_b, L0_attn_log_gamma, L0_attn_WB_w, L0_attn_WB_b,
                 L0_attn_WC_w, L0_attn_WC_b, L0_attn_D_skip, L0_attn_ln_w, L0_attn_ln_b,
                 L0_ffn_router_w, L0_ffn_sg, L0_ffn_su, L0_ffn_sd, L0_ffn_eg, L0_ffn_eu,
                 L0_ffn_ed):
    f32 = np.float32
    ac = np.ascontiguousarray
    n = np.arange(NST)
    ang = 2.0 * np.pi * np.outer(n, n) / NST
    Cfull = np.cos(ang)
    Sfull = np.sin(ang)
    rep = lambda v: ac(np.broadcast_to(np.asarray(v, f32)[None, :], (128, D)))
    return dict(
        rtW=ac(np.asarray(L0_ffn_router_w, f32).T),
        sgT=ac(np.asarray(L0_ffn_sg, f32).T),
        suT=ac(np.asarray(L0_ffn_su, f32).T),
        sdT=ac(np.asarray(L0_ffn_sd, f32).T),
        egT=ac(np.asarray(L0_ffn_eg, f32).transpose(0, 2, 1)),
        euT=ac(np.asarray(L0_ffn_eu, f32).transpose(0, 2, 1)),
        edT=ac(np.asarray(L0_ffn_ed, f32).transpose(0, 2, 1)),
        WaT=ac(np.asarray(L0_attn_Wa_w, f32).T),
        WBT=ac(np.asarray(L0_attn_WB_w, f32).T),
        WCT=ac(np.asarray(L0_attn_WC_w, f32).T),
        Wab=ac(np.asarray(L0_attn_Wa_b, f32).reshape(2, 128).T),
        WBb=ac(np.asarray(L0_attn_WB_b, f32).reshape(2, 128).T),
        lgam=np.asarray(L0_attn_log_gamma, f32).reshape(1, 1),
        Ctab=ac(Cfull[:, 0:KH].astype(f32)),
        Stab=ac(Sfull[:, 0:KH].astype(f32)),
        c128=ac(Cfull[:, KH].astype(f32).reshape(2, 128).T),
        ident=np.eye(128, dtype=f32),
        lnwrs=ac(np.asarray(L0_attn_ln_w, f32).reshape(DK, 128).T),
        lnbrs=ac(np.asarray(L0_attn_ln_b, f32).reshape(DK, 128).T),
        WCb_rep=rep(L0_attn_WC_b),
        Dsk_rep=rep(L0_attn_D_skip),
    )


def kernel(L0_attn_x, **kw):
    f32 = np.float32
    TLOC = 4096
    xflat = np.ascontiguousarray(np.asarray(L0_attn_x, f32).reshape(-1, D))
    common = _prep_inputs(**kw)
    nc = build_program(TLOC=TLOC, phases=3)
    in_maps = []
    for c in range(8):
        sel = np.zeros((8, 1), f32)
        if c % 2 == 1:
            sel[c - 1, 0] = 1.0
        m = dict(common)
        m["x"] = xflat[c * TLOC:(c + 1) * TLOC]
        m["sel"] = sel
        in_maps.append(m)
    res = run_bass_kernel_spmd(nc, in_maps, core_ids=list(range(8)))
    out = np.concatenate([res.results[c]["y"] for c in range(8)], axis=0)
    return out.reshape(L0_attn_x.shape).astype(f32)


# version 18
# speedup vs baseline: 1.0248x; 1.0024x over previous
import contextlib
import numpy as np
import concourse.bass as bass
import concourse.mybir as mybir
from concourse.bass_utils import run_bass_kernel_spmd

F32 = mybir.dt.float32
BF16 = mybir.dt.bfloat16
AF = mybir.ActivationFunctionType
ALU = mybir.AluOpType
AX = mybir.AxisListType

D = 1024
DK = 8
NST = 256
KH = 128
H = 2048
HC = 256
NHC = H // HC
HS = HC // 128
NE = 8
LN_EPS = 1e-5


class Sem:
    def __init__(self, h):
        self.h = h
        self.n = 0


class Res:
    def __init__(self):
        self.cons = []

    def ready(self, sem, val=None):
        self.cons = [(sem, sem.n if val is None else val)]

    def also(self, sem, val=None):
        self.cons.append((sem, sem.n if val is None else val))


class Emit:
    def __init__(self, nc):
        self.nc = nc
        self.ops = {k: [] for k in 'tvsgy'}

    def op(self, eng, method, *args, inc=None, k=1, **kw):
        semh = inc.h if inc is not None else None
        if inc is not None:
            inc.n += k

        def f(e, m=method, a=args, kw=kw, s=semh, k=k):
            r = getattr(e, m)(*a, **kw)
            if s is not None:
                r.then_inc(s, k)
        self.ops[eng].append(f)
        return inc.n if inc is not None else None

    def wait(self, eng, sem, val=None):
        v = sem.n if val is None else val
        self.ops[eng].append(lambda e, s=sem.h, v=v: e.wait_ge(s, v))

    def wait_res(self, eng, res):
        for sem, val in res.cons:
            self.wait(eng, sem, val)

    def drain(self, eng):
        self.ops[eng].append(lambda e: e.drain())

    def run_block(self):
        nc = self.nc
        ops = self.ops
        with nc.Block() as blk:
            @blk.tensor
            def _(t):
                for f in ops['t']:
                    f(t)

            @blk.vector
            def _(v):
                for f in ops['v']:
                    f(v)

            @blk.scalar
            def _(s):
                for f in ops['s']:
                    f(s)

            @blk.gpsimd
            def _(g):
                for f in ops['g']:
                    f(g)

            @blk.sync
            def _(y):
                for f in ops['y']:
                    f(y)
        self.ops = {k: [] for k in 'tvsgy'}


def build_program(TLOC=4096, phases=3, world=8):
    TGRP = min(2048, TLOC)
    NGRP = TLOC // TGRP
    NTT = TGRP // 512
    NTM = TGRP // 128
    NTILE = TLOC // 128
    CH = min(512, TLOC)
    NCHK = TLOC // CH
    PAD = CH // 2
    NLEV = CH.bit_length() - 1

    nc = bass.Bass("TRN2", target_bir_lowering=False)

    x_in = nc.dram_tensor("x", [TLOC, D], F32, kind="ExternalInput").ap()
    y_out = nc.dram_tensor("y", [TLOC, D], F32, kind="ExternalOutput").ap()
    d_sel = nc.dram_tensor("sel", [world, 1], F32, kind="ExternalInput").ap()

    d_rtW = nc.dram_tensor("rtW", [D, NE], F32, kind="ExternalInput").ap()
    d_sgT = nc.dram_tensor("sgT", [D, H], F32, kind="ExternalInput").ap()
    d_suT = nc.dram_tensor("suT", [D, H], F32, kind="ExternalInput").ap()
    d_sdT = nc.dram_tensor("sdT", [H, D], F32, kind="ExternalInput").ap()
    d_egT = nc.dram_tensor("egT", [NE, D, H], F32, kind="ExternalInput").ap()
    d_euT = nc.dram_tensor("euT", [NE, D, H], F32, kind="ExternalInput").ap()
    d_edT = nc.dram_tensor("edT", [NE, H, D], F32, kind="ExternalInput").ap()

    d_WaT = nc.dram_tensor("WaT", [D, NST], F32, kind="ExternalInput").ap()
    d_WBT = nc.dram_tensor("WBT", [D, NST], F32, kind="ExternalInput").ap()
    d_WCT = nc.dram_tensor("WCT", [NST, D], F32, kind="ExternalInput").ap()
    d_Wab = nc.dram_tensor("Wab", [128, 2], F32, kind="ExternalInput").ap()
    d_WBb = nc.dram_tensor("WBb", [128, 2], F32, kind="ExternalInput").ap()
    d_lgam = nc.dram_tensor("lgam", [1, 1], F32, kind="ExternalInput").ap()
    d_Ctab = nc.dram_tensor("Ctab", [NST, KH], F32, kind="ExternalInput").ap()
    d_Stab = nc.dram_tensor("Stab", [NST, KH], F32, kind="ExternalInput").ap()
    d_c128 = nc.dram_tensor("c128", [128, 2], F32, kind="ExternalInput").ap()
    d_ident = nc.dram_tensor("ident", [128, 128], F32, kind="ExternalInput").ap()
    d_lnwrs = nc.dram_tensor("lnwrs", [128, DK], F32, kind="ExternalInput").ap()
    d_lnbrs = nc.dram_tensor("lnbrs", [128, DK], F32, kind="ExternalInput").ap()
    d_WCb = nc.dram_tensor("WCb_rep", [128, D], F32, kind="ExternalInput").ap()
    d_Dsk = nc.dram_tensor("Dsk_rep", [128, D], F32, kind="ExternalInput").ap()

    x1s = nc.dram_tensor("x1s", [TLOC, D], F32).ap()
    mrow_scr = nc.dram_tensor("mrow_scr", [1, D], BF16).ap()
    x2s = nc.dram_tensor("x2s", [TLOC, D], F32).ap()
    cc_in = nc.dram_tensor("cc_in", [1, 512], F32)
    cc_out = nc.dram_tensor("cc_out", [world, 512], F32)

    em = Emit(nc)
    stk = contextlib.ExitStack()

    def sb(name, shape, dt):
        return stk.enter_context(nc.sbuf_tensor(name, shape, dt))

    def sem(name):
        return Sem(stk.enter_context(nc.semaphore(name)))

    with stk:
        s_mm = sem("s_mm")
        s_act = sem("s_act")
        s_dve = sem("s_dve")
        s_gp = sem("s_gp")
        s_cc = sem("s_cc")
        s_ld0 = sem("s_ld0")
        s_wl0 = sem("s_wl0")
        lx = [sem(f"lx{j}") for j in range(2)]
        so = [sem(f"so{j}") for j in range(2)]
        wl = [sem(f"wl{j}") for j in range(3)]
        wl2 = [sem(f"wl2{j}") for j in range(2)]
        l1 = [sem(f"l1{j}") for j in range(2)]

        c_identb = sb("c_identb", [128, 128], BF16)
        c_identf = sb("c_identf", [128, 128], F32)
        c_rtW = sb("c_rtW", [128, DK, NE], BF16)
        c_WCT = sb("c_WCT", [128, 2, D], BF16)
        c_C = sb("c_C", [128, 2, KH], BF16)
        c_S = sb("c_S", [128, 2, KH], BF16)
        c_c128 = sb("c_c128", [128, 2], BF16)
        c_Wab = sb("c_Wab", [128, 2], F32)
        c_WBb = sb("c_WBb", [128, 2], BF16)
        c_WCb = sb("c_WCb", [128, D], F32)
        c_Dsk = sb("c_Dsk", [128, D], F32)
        c_gamma = sb("c_gamma", [128, 1], F32)
        c_ones = sb("c_ones", [1, 128], F32)
        c_lgam = sb("c_lgam", [1, 1], F32)
        c_sel = sb("c_sel", [world, 1], F32)
        M2reT = sb("M2reT", [128, D], BF16)
        M2imT = sb("M2imT", [128, D], BF16)
        fac = sb("fac", [128, 1], F32)
        fbre = sb("fbre", [128, 1], F32)
        fbim = sb("fbim", [128, 1], F32)
        fb128 = sb("fb128", [1, 1], F32)
        c_eps = sb("c_eps", [128, 1], F32)
        c_two = sb("c_two", [128, 1], F32)

        op, wait, drain = em.op, em.wait, em.drain

        for dst, src in ((c_identf, d_ident),
                         (c_WCb, d_WCb), (c_Dsk, d_Dsk)):
            op('y', 'dma_start', out=dst[:, :], in_=src[:, :], inc=s_ld0, k=16)
        op('y', 'dma_start', out=c_lgam[:, :], in_=d_lgam[:, :], inc=s_ld0, k=16)
        op('y', 'dma_start', out=c_sel[:, :], in_=d_sel[:, :], inc=s_ld0, k=16)
        op('y', 'dma_start', out=c_Wab[:, :], in_=d_Wab[:, :], inc=s_ld0, k=16)
        op('g', 'dma_start', out=c_identb[:, :], in_=d_ident[:, :], inc=s_wl0, k=16)
        op('g', 'dma_start', out=c_rtW[:, :, :],
           in_=d_rtW.rearrange("(c p) e -> p c e", p=128), inc=s_wl0, k=16)
        op('g', 'dma_start', out=c_WCT[:, :, :],
           in_=d_WCT.rearrange("(g p) d -> p g d", p=128), inc=s_wl0, k=16)
        op('g', 'dma_start', out=c_C[:, :, :],
           in_=d_Ctab.rearrange("(g p) k -> p g k", p=128), inc=s_wl0, k=16)
        op('g', 'dma_start', out=c_S[:, :, :],
           in_=d_Stab.rearrange("(g p) k -> p g k", p=128), inc=s_wl0, k=16)
        op('g', 'dma_start', out=c_c128[:, :], in_=d_c128[:, :], inc=s_wl0, k=16)
        op('g', 'dma_start', out=c_WBb[:, :], in_=d_WBb[:, :], inc=s_wl0, k=16)
        ld_all = s_ld0.n
        wl_all = s_wl0.n

        wait('s', s_ld0, ld_all)
        op('s', 'activation', c_lgam[:, :], c_lgam[:, :], AF.Sigmoid, inc=s_act)
        op('v', 'memset', c_ones[:, :], 1.0)
        op('v', 'memset', fac[:, :], 2.0)
        op('v', 'memset', c_eps[:, :], LN_EPS)
        op('v', 'memset', c_two[:, :], 2.0)
        drain('v')
        op('v', 'memset', fac[0:1, :], 1.0, inc=s_dve)
        drain('v')

        with nc.psum_tensor("ip0", [128, 512], F32) as ip0, \
             nc.psum_tensor("ip1", [1, 512], F32) as ip1, \
             nc.sbuf_tensor("MrowT", [1, D], BF16) as Mrow:
            wait('t', s_act, s_act.n)
            wait('t', s_dve, s_dve.n)
            wait('t', s_wl0, wl_all)
            wait('t', s_ld0, ld_all)
            op('t', 'matmul', ip0[:, 0:1], c_ones[:, :], c_lgam[:, :], start=True, stop=True,
               inc=s_mm)
            wait('v', s_mm, s_mm.n)
            op('v', 'tensor_copy', c_gamma[:, :], ip0[:, 0:1], inc=s_dve)
            wait('t', s_dve, s_dve.n)
            for (tabs, dstb, sc) in ((c_C, M2reT, 1.0 / NST), (c_S, M2imT, -1.0 / NST)):
                for dt_ in range(2):
                    op('t', 'matmul', ip0[:, :], tabs[:, 0, :], c_WCT[:, 0, dt_ * 512:(dt_ + 1) * 512],
                       start=True, stop=False)
                    op('t', 'matmul', ip0[:, :], tabs[:, 1, :], c_WCT[:, 1, dt_ * 512:(dt_ + 1) * 512],
                       start=False, stop=True, inc=s_mm)
                    wait('v', s_mm, s_mm.n)
                    op('v', 'tensor_scalar', dstb[:, dt_ * 512:(dt_ + 1) * 512], ip0[:, :],
                       fac[:, 0:1], sc, ALU.mult, ALU.mult, inc=s_dve)
                    wait('t', s_dve, s_dve.n)
            for dt_ in range(2):
                op('t', 'matmul', ip1[:, :], c_c128[:, 0:1], c_WCT[:, 0, dt_ * 512:(dt_ + 1) * 512],
                   start=True, stop=False)
                op('t', 'matmul', ip1[:, :], c_c128[:, 1:2], c_WCT[:, 1, dt_ * 512:(dt_ + 1) * 512],
                   start=False, stop=True, inc=s_mm)
                wait('v', s_mm, s_mm.n)
                op('v', 'tensor_scalar', Mrow[0:1, dt_ * 512:(dt_ + 1) * 512], ip1[:, :],
                   1.0 / NST, None, ALU.mult, inc=s_dve)
                wait('t', s_dve, s_dve.n)
            wait('y', s_dve, s_dve.n)
            op('y', 'dma_start', out=mrow_scr[:, :], in_=Mrow[0:1, :], inc=s_ld0, k=16)
            wait('y', s_ld0, s_ld0.n)
            em.run_block()


        def emit_ffn(src, dst, tag, src_events=None, entry=()):
            TGRP = 1024
            NGRP = TLOC // TGRP
            NTM = TGRP // 128
            NTT = TGRP // 512
            NSC = 4
            NHS = 8
            units = [(None, 0), (None, 1)] + [(e, h) for e in range(NE) for h in range(2)]
            NU = len(units)
            store_events = [None] * (TLOC // 128)
            with contextlib.ExitStack() as fs:
                def fsb(name, shape, dt):
                    return fs.enter_context(nc.sbuf_tensor(name, shape, dt))
                xn_fm = fsb("xn_fm" + tag, [128, 2, DK, TGRP], BF16)
                actb = [fsb(f"actb{j}{tag}", [128, NHS, TGRP], BF16) for j in range(2)]
                wg = [fsb(f"wg{j}{tag}", [128, DK, 256], BF16) for j in range(3)]
                wu = [fsb(f"wu{j}{tag}", [128, DK, 256], BF16) for j in range(3)]
                wd = [fsb(f"wd{j}{tag}", [128, NHS, D], BF16) for j in range(2)]
                acc = fsb("acc" + tag, [128, NTM, D], F32)
                xring = fsb("xring" + tag, [128, 2, D], F32)
                oring = fsb("oring" + tag, [128, 2, D], F32)
                xnt = fsb("xnt" + tag, [128, 2, D], BF16)
                silb = [fsb(f"silb{j}{tag}", [128, 512], F32) for j in range(2)]
                uevb = [fsb(f"uevb{j}{tag}", [128, 512], BF16) for j in range(2)]
                cwb = fsb("cwb" + tag, [128, 2, NTM * 8], F32)
                bn6 = fsb("bn6" + tag, [128, 12], F32)
                bn2 = fsb("bn2" + tag, [128, 2], F32)
                sdv = fsb("sdv" + tag, [128, 1], F32)
                rstd = fsb("rstd" + tag, [128, 1], F32)
                lg8 = fsb("lg8" + tag, [128, 8], F32)
                m1 = fsb("m1" + tag, [128, 1], F32)
                nm1 = fsb("nm1" + tag, [128, 1], F32)
                m2 = fsb("m2" + tag, [128, 1], F32)
                dm = fsb("dm" + tag, [128, 1], F32)
                eqm = fsb("eqm" + tag, [128, 8], F32)
                lmb = fsb("lmb" + tag, [128, 8], F32)
                ex8 = fsb("ex8" + tag, [128, 8], F32)
                ed1 = fsb("ed1" + tag, [128, 1], F32)
                den = fsb("den" + tag, [128, 1], F32)
                rden = fsb("rden" + tag, [128, 1], F32)
                cwt = fsb("cwt" + tag, [128, 8], F32)

                with nc.psum_tensor("pGU" + tag, [128, 4, 512], F32) as pGU, \
                     nc.psum_tensor("pDD" + tag, [128, 2, 512], F32) as pD, \
                     nc.psum_tensor("pTR" + tag, [128, DK, 128], BF16) as pTR, \
                     nc.psum_tensor("pRT" + tag, [128, 512], F32) as pRT:

                    rGU = [Res(), Res()]
                    rD = [Res(), Res()]
                    rACT = [Res(), Res()]
                    rSIL = [Res(), Res()]
                    rW = [Res(), Res(), Res()]
                    rWD = [Res(), Res()]
                    rX = [Res(), Res()]
                    rO = [Res(), Res()]
                    rXN = [Res(), Res()]
                    rTR = Res()
                    rRT = Res()

                    for eng in 'tvsgy':
                        for bsem, bval in entry:
                            wait(eng, bsem, bval)

                    ctr = {'x': 0, 'o': 0, 'xn': 0, 'gu': 0, 'dsl': 0}
                    side1 = []
                    side5 = []

                    def pop_side(q):
                        if q:
                            q.pop(0)()

                    def flush(q):
                        while q:
                            q.pop(0)()

                    def load_src(gt):
                        xs = ctr['x'] % 2
                        ctr['x'] += 1
                        em.wait_res('y', rX[xs])
                        if src_events is not None:
                            gsem, gval = src_events[gt]
                            wait('y', gsem, gval)
                        op('y', 'dma_start', out=xring[:, xs, :],
                           in_=src[gt * 128:(gt + 1) * 128, :], inc=lx[xs], k=16)
                        return xs, lx[xs], lx[xs].n

                    def stage1_tile(g, i):
                        def f():
                            gt = g * NTM + i
                            xs, lsem, lval = load_src(gt)
                            wait('v', lsem, lval)
                            drain('v')
                            op('v', 'bn_stats', bn6[:, 0:6], xring[:, xs, 0:512])
                            op('v', 'bn_stats', bn6[:, 6:12], xring[:, xs, 512:1024])
                            drain('v')
                            op('v', 'bn_aggr', bn2[:, :], bn6[:, :], inc=s_dve)
                            wait('s', s_dve, s_dve.n)
                            op('s', 'activation', sdv[:, :], bn2[:, 1:2], AF.Sqrt,
                               bias=c_eps[:, 0:1], inc=s_act)
                            wait('v', s_act, s_act.n)
                            op('v', 'reciprocal', rstd[:, :], sdv[:, :])
                            drain('v')
                            ns = ctr['xn'] % 2
                            ctr['xn'] += 1
                            em.wait_res('v', rXN[ns])
                            op('v', 'tensor_scalar', xnt[:, ns, :], xring[:, xs, :],
                               bn2[:, 0:1], rstd[:, 0:1], ALU.subtract, ALU.mult, inc=s_dve)
                            rX[xs].ready(s_dve)
                            xn_ready = s_dve.n
                            wait('t', s_dve, xn_ready)
                            em.wait_res('t', rTR)
                            for dk in range(DK):
                                op('t', 'transpose', pTR[:, dk, :],
                                   xnt[:, ns, dk * 128:(dk + 1) * 128], c_identb[:, :],
                                   inc=(s_mm if dk == DK - 1 else None))
                            wait('v', s_mm, s_mm.n)
                            op('v', 'tensor_copy', xn_fm[:, g % 2, :, i * 128:(i + 1) * 128],
                               pTR[:, :, :], inc=s_dve)
                            rTR.ready(s_dve)
                            rXN[ns].ready(s_dve)
                            xnfm_tile_done = s_dve.n
                            em.wait_res('t', rRT)
                            wait('t', s_dve, xnfm_tile_done)
                            for dk in range(DK):
                                op('t', 'matmul', pRT[:, 0:8],
                                   xn_fm[:, g % 2, dk, i * 128:(i + 1) * 128], c_rtW[:, dk, :],
                                   start=(dk == 0), stop=(dk == DK - 1),
                                   inc=(s_mm if dk == DK - 1 else None))
                            wait('v', s_mm, s_mm.n)
                            drain('v')
                            op('v', 'tensor_copy', lg8[:, :], pRT[:, 0:8], inc=s_dve)
                            rRT.ready(s_dve)
                            drain('v')
                            op('v', 'tensor_reduce', m1[:, :], lg8[:, :], AX.X, ALU.max)
                            drain('v')
                            op('v', 'tensor_scalar', eqm[:, :], lg8[:, :], m1[:, 0:1], None, ALU.is_ge)
                            op('v', 'tensor_scalar', nm1[:, :], m1[:, :], -1.0, None, ALU.mult)
                            drain('v')
                            op('v', 'scalar_tensor_tensor', lmb[:, :], eqm[:, :], -1e30, lg8[:, :],
                               ALU.mult, ALU.add)
                            drain('v')
                            op('v', 'tensor_reduce', m2[:, :], lmb[:, :], AX.X, ALU.max)
                            drain('v')
                            op('v', 'tensor_tensor', dm[:, :], m2[:, :], m1[:, :], ALU.subtract,
                               inc=s_dve)
                            wait('s', s_dve, s_dve.n)
                            op('s', 'activation', ex8[:, :], lg8[:, :], AF.Exp,
                               bias=nm1[:, 0:1], scale=1.0)
                            op('s', 'activation', ed1[:, :], dm[:, :], AF.Exp, inc=s_act)
                            wait('v', s_act, s_act.n)
                            op('v', 'tensor_scalar', den[:, :], ed1[:, :], 1.0, None, ALU.add)
                            drain('v')
                            op('v', 'reciprocal', rden[:, :], den[:, :])
                            op('v', 'tensor_scalar', eqm[:, :], lg8[:, :], m2[:, 0:1], None, ALU.is_ge)
                            drain('v')
                            op('v', 'tensor_tensor', cwt[:, :], ex8[:, :], eqm[:, :], ALU.mult)
                            drain('v')
                            op('v', 'tensor_scalar', cwb[:, g % 2, i * 8:(i + 1) * 8], cwt[:, :],
                               rden[:, 0:1], 0.5, ALU.mult, ALU.mult, inc=s_dve)
                        return f

                    def stage5_tile(g, tm, acc_done):
                        def f():
                            gt = g * NTM + tm
                            xs, lsem, lval = load_src(gt)
                            os_ = ctr['o'] % 2
                            ctr['o'] += 1
                            wait('v', lsem, lval)
                            em.wait_res('v', rO[os_])
                            drain('v')
                            op('v', 'tensor_tensor', oring[:, os_, :], xring[:, xs, :],
                               acc[:, tm, :], ALU.add, inc=s_dve)
                            rX[xs].ready(s_dve)
                            wait('y', s_dve, s_dve.n)
                            op('y', 'dma_start', out=dst[gt * 128:(gt + 1) * 128, :],
                               in_=oring[:, os_, :], inc=so[os_], k=16)
                            store_events[gt] = (so[os_], so[os_].n)
                            rO[os_].ready(so[os_])
                        return f

                    guw_ev = {}
                    wd_ev = {}
                    GSEQ_MAX = NGRP * NU * NSC
                    USEQ_MAX = NGRP * NU

                    def sched_guw(seq):
                        if seq >= GSEQ_MAX:
                            return
                        u = (seq // NSC) % NU
                        sc = seq % NSC
                        slot = seq % 3
                        e, half = units[u]
                        h0 = half * 1024 + sc * 256
                        if e is None:
                            gsrc = d_sgT.rearrange("(c p) h -> p c h", p=128)[:, :, h0:h0 + 256]
                            usrc = d_suT.rearrange("(c p) h -> p c h", p=128)[:, :, h0:h0 + 256]
                        else:
                            gsrc = d_egT[e].rearrange("(c p) h -> p c h", p=128)[:, :, h0:h0 + 256]
                            usrc = d_euT[e].rearrange("(c p) h -> p c h", p=128)[:, :, h0:h0 + 256]
                        em.wait_res('g', rW[slot])
                        op('g', 'dma_start', out=wg[slot][:, :, :], in_=gsrc, inc=wl[slot], k=16)
                        op('g', 'dma_start', out=wu[slot][:, :, :], in_=usrc, inc=wl[slot], k=16)
                        guw_ev[seq] = (slot, wl[slot], wl[slot].n)

                    def sched_wd(useq):
                        if useq >= USEQ_MAX:
                            return
                        u = useq % NU
                        slot = useq % 2
                        e, half = units[u]
                        if e is None:
                            dsrc = d_sdT.rearrange("(c p) d -> p c d", p=128)[:, half * 8:(half + 1) * 8, :]
                        else:
                            dsrc = d_edT[e].rearrange("(c p) d -> p c d", p=128)[:, half * 8:(half + 1) * 8, :]
                        em.wait_res('g', rWD[slot])
                        op('g', 'dma_start', out=wd[slot][:, :, :], in_=dsrc, inc=wl2[slot], k=16)
                        wd_ev[useq] = (slot, wl2[slot], wl2[slot].n)

                    act_done = [0, 0]

                    def emit_down(g, u):
                        e, half = units[u]
                        uslot = u % 2
                        wdslot, wdsem, wdval = wd_ev[g * NU + u]
                        wait('t', wdsem, wdval)
                        wait('t', s_dve, act_done[uslot])

                        for tm in range(NTM):
                            for dt_ in range(2):
                                pp = ctr['dsl'] % 2
                                ctr['dsl'] += 1
                                em.wait_res('t', rD[pp])
                                for hsub in range(NHS):
                                    op('t', 'matmul', pD[:, pp, :],
                                       actb[uslot][:, hsub, tm * 128:(tm + 1) * 128],
                                       wd[wdslot][:, hsub, dt_ * 512:(dt_ + 1) * 512],
                                       start=(hsub == 0), stop=(hsub == NHS - 1),
                                       inc=(s_mm if hsub == NHS - 1 else None))
                                wait('v', s_mm, s_mm.n)
                                aslab = acc[:, tm, dt_ * 512:(dt_ + 1) * 512]
                                if u == 0:
                                    op('v', 'tensor_scalar', aslab, pD[:, pp, :],
                                       0.5, None, ALU.mult, inc=s_dve)
                                elif e is None:
                                    op('v', 'scalar_tensor_tensor', aslab, pD[:, pp, :],
                                       0.5, aslab, ALU.mult, ALU.add, inc=s_dve)
                                else:
                                    col = cwb[:, g % 2, tm * 8 + e: tm * 8 + e + 1]
                                    op('v', 'scalar_tensor_tensor', aslab, pD[:, pp, :],
                                       col, aslab, ALU.mult, ALU.add, inc=s_dve)
                                rD[pp].ready(s_dve)
                                pop_side(side1)
                        rACT[uslot].ready(s_mm, s_mm.n)
                        rWD[wdslot].ready(s_mm, s_mm.n)
                        sched_wd(g * NU + u + 2)

                    for sq in range(3):
                        sched_guw(sq)
                    sched_wd(0)
                    sched_wd(1)
                    for i in range(NTM):
                        side1.append(stage1_tile(0, i))
                    flush(side1)

                    for g in range(NGRP):
                        xn_all = s_dve.n
                        for u in range(NU):
                            e, half = units[u]
                            uslot = u % 2
                            first_mult = True
                            for sc in range(NSC):
                                seq = (g * NU + u) * NSC + sc
                                slot, wsem, wval = guw_ev[seq]
                                wait('t', wsem, wval)
                                for hs in range(2):
                                    hsub = sc * 2 + hs
                                    for tt in range(NTT):
                                        pp = ctr['gu'] % 2
                                        ctr['gu'] += 1
                                        em.wait_res('t', rGU[pp])
                                        if u == 0 and sc == 0 and hs == 0 and tt == 0:
                                            wait('t', s_dve, xn_all)
                                        for dk in range(DK):
                                            op('t', 'matmul', pGU[:, 2 * pp, :],
                                               wg[slot][:, dk, hs * 128:(hs + 1) * 128],
                                               xn_fm[:, g % 2, dk, tt * 512:(tt + 1) * 512],
                                               start=(dk == 0), stop=(dk == DK - 1),
                                               inc=(s_mm if dk == DK - 1 else None))
                                        gval = s_mm.n
                                        for dk in range(DK):
                                            op('t', 'matmul', pGU[:, 2 * pp + 1, :],
                                               wu[slot][:, dk, hs * 128:(hs + 1) * 128],
                                               xn_fm[:, g % 2, dk, tt * 512:(tt + 1) * 512],
                                               start=(dk == 0), stop=(dk == DK - 1),
                                               inc=(s_mm if dk == DK - 1 else None))
                                        uval = s_mm.n
                                        wait('s', s_mm, gval)
                                        em.wait_res('s', rSIL[pp])
                                        op('s', 'activation', silb[pp][:, :], pGU[:, 2 * pp, :],
                                           AF.Silu)
                                        wait('s', s_mm, uval)
                                        op('s', 'activation', uevb[pp][:, :], pGU[:, 2 * pp + 1, :],
                                           AF.Identity, inc=s_act)
                                        rGU[pp].ready(s_act)
                                        wait('v', s_act, s_act.n)
                                        if first_mult:
                                            em.wait_res('v', rACT[uslot])
                                            first_mult = False
                                        op('v', 'tensor_tensor',
                                           actb[uslot][:, hsub, tt * 512:(tt + 1) * 512],
                                           silb[pp][:, :], uevb[pp][:, :], ALU.mult,
                                           inc=s_dve)
                                        rSIL[pp].ready(s_dve)
                                        pop_side(side5)
                                rW[slot].ready(s_mm, s_mm.n)
                                sched_guw(seq + 3)
                            act_done[uslot] = s_dve.n
                            if u == 12 and g + 1 < NGRP:
                                for i in range(NTM):
                                    side1.append(stage1_tile(g + 1, i))
                            if u > 0:
                                if u == 1:
                                    flush(side5)
                                emit_down(g, u - 1)
                        emit_down(g, NU - 1)
                        flush(side1)
                        acc_done = s_dve.n
                        for tm in range(NTM):
                            side5.append(stage5_tile(g, tm, acc_done))
                    flush(side5)
                    em.run_block()
            return store_events

        def emit_attn(dst, x1_events, entry=()):
            store_events = []
            with contextlib.ExitStack() as fs:
                def fsb(name, shape, dt):
                    return fs.enter_context(nc.sbuf_tensor(name, shape, dt))
                sc_ar = fsb("sc_ar", [128, TLOC], F32)
                sc_ai = fsb("sc_ai", [128, TLOC], F32)
                sc_ur = fsb("sc_ur", [128, TLOC], F32)
                sc_ui = fsb("sc_ui", [128, TLOC], F32)
                hrebf = fsb("hrebf", [128, TLOC], BF16)
                himbf = fsb("himbf", [128, TLOC], BF16)
                a128b = fsb("a128b", [1, TLOC], F32)
                u128b = fsb("u128b", [1, TLOC], F32)
                h128bf = fsb("h128bf", [1, TLOC], BF16)
                r128 = [fsb(f"r128{q}", [1, 1], F32) for q in range(2)]
                c0re = fsb("c0re", [128, 1], F32)
                c0im = fsb("c0im", [128, 1], F32)
                c128c = fsb("c128c", [1, 1], F32)

                bnm = fsb("bnm", [128, NTILE], F32)
                bnr = fsb("bnr", [128, NTILE], F32)
                pngG = [[fsb(f"pgg{j}{q}", [128, CH], F32) for q in range(4)] for j in range(2)]
                tmpG = [fsb(f"tmg{q}", [128, CH], F32) for q in range(2)]

                with nc.psum_tensor("pTRB", [128, DK, 128], BF16) as pTR, \
                     nc.psum_tensor("pPJ", [128, 4, 512], F32) as pPJ:

                    rTR = Res()
                    rPJ = [Res(), Res(), Res(), Res()]
                    rFF = [Res(), Res()]
                    rYp = [Res(), Res()]
                    rX1 = [Res(), Res()]
                    rZ = [Res(), Res()]
                    rO = [Res(), Res()]
                    rSCR = Res()

                    for eng in 'tvsgy':
                        for bsem, bval in entry:
                            wait(eng, bsem, bval)

                    chunk_rdy = [None] * NCHK
                    u_fin = [0] * NCHK

                    def scan_chunks(eng, sem_, png, tmp4, chunks, pad):
                        if pad:
                            for j in range(2):
                                op(eng, 'memset', png[j][0][:, 0:pad], 1.0)
                                op(eng, 'memset', png[j][1][:, 0:pad], 0.0)
                                op(eng, 'memset', png[j][2][:, 0:pad], 0.0)
                                op(eng, 'memset', png[j][3][:, 0:pad], 0.0)
                            drain(eng)
                        for c in chunks:
                            cs = slice(c * CH, (c + 1) * CH)
                            rv, rs = chunk_rdy[c]
                            wait(eng, s_dve, rv)
                            wait(eng, s_act, rs)
                            for q, srcb in enumerate((sc_ar, sc_ai, sc_ur, sc_ui)):
                                op(eng, 'tensor_copy', png[0][q][:, pad:], srcb[:, cs])
                            drain(eng)
                            cur = 0
                            s_ = 1
                            for lev in range(NLEV):
                                A = png[cur]
                                B = png[1 - cur]
                                if pad:
                                    cu = slice(pad, pad + CH)
                                    sh = slice(pad - s_, pad - s_ + CH)
                                    TL = CH
                                    pre = None
                                else:
                                    cu = slice(s_, CH)
                                    sh = slice(0, CH - s_)
                                    TL = CH - s_
                                    pre = slice(0, s_)
                                op(eng, 'tensor_tensor', B[0][:, cu], A[0][:, cu], A[0][:, sh], ALU.mult)
                                op(eng, 'tensor_tensor', tmp4[0][:, 0:TL], A[1][:, cu], A[1][:, sh], ALU.mult)
                                op(eng, 'tensor_tensor', B[1][:, cu], A[0][:, cu], A[1][:, sh], ALU.mult)
                                op(eng, 'tensor_tensor', tmp4[1][:, 0:TL], A[1][:, cu], A[0][:, sh], ALU.mult)
                                drain(eng)
                                op(eng, 'tensor_tensor', B[0][:, cu], B[0][:, cu], tmp4[0][:, 0:TL], ALU.subtract)
                                op(eng, 'tensor_tensor', B[1][:, cu], B[1][:, cu], tmp4[1][:, 0:TL], ALU.add)
                                op(eng, 'tensor_tensor', B[2][:, cu], A[0][:, cu], A[2][:, sh], ALU.mult)
                                op(eng, 'tensor_tensor', tmp4[0][:, 0:TL], A[1][:, cu], A[3][:, sh], ALU.mult)
                                op(eng, 'tensor_tensor', B[3][:, cu], A[0][:, cu], A[3][:, sh], ALU.mult)
                                op(eng, 'tensor_tensor', tmp4[1][:, 0:TL], A[1][:, cu], A[2][:, sh], ALU.mult)
                                drain(eng)
                                op(eng, 'tensor_tensor', B[2][:, cu], B[2][:, cu], tmp4[0][:, 0:TL], ALU.subtract)
                                op(eng, 'tensor_tensor', B[3][:, cu], B[3][:, cu], tmp4[1][:, 0:TL], ALU.add)
                                drain(eng)
                                op(eng, 'tensor_tensor', B[2][:, cu], B[2][:, cu], A[2][:, cu], ALU.add)
                                op(eng, 'tensor_tensor', B[3][:, cu], B[3][:, cu], A[3][:, cu], ALU.add)
                                if pre is not None:
                                    for q in range(4):
                                        op(eng, 'tensor_copy', B[q][:, pre], A[q][:, pre])
                                drain(eng)
                                cur = 1 - cur
                                s_ *= 2
                            Afin = png[cur]
                            op(eng, 'tensor_copy', sc_ar[:, cs], Afin[0][:, pad:])
                            op(eng, 'tensor_copy', sc_ai[:, cs], Afin[1][:, pad:])
                            op(eng, 'tensor_copy', sc_ur[:, cs], Afin[2][:, pad:])
                            op(eng, 'tensor_copy', sc_ui[:, cs], Afin[3][:, pad:], inc=sem_)
                            u_fin[c] = sem_.n
                            drain(eng)

                    with contextlib.ExitStack() as f1, \
                         nc.psum_tensor("pFF", [128, 2, 512], F32) as pFF:
                        def f1b(name, shape, dt):
                            return f1.enter_context(nc.sbuf_tensor(name, shape, dt))
                        c_WaT = f1b("c_WaT", [128, DK, NST], BF16)
                        c_WBT = f1b("c_WBT", [128, DK, NST], BF16)
                        c_lnwrs = f1b("c_lnwrs", [128, DK], F32)
                        c_lnbrs = f1b("c_lnbrs", [128, DK], BF16)
                        wbbe = f1b("wbbe", [128, 2], BF16)
                        op('g', 'dma_start', out=c_WaT[:, :, :],
                           in_=d_WaT.rearrange("(c p) n -> p c n", p=128), inc=s_ld0, k=16)
                        op('g', 'dma_start', out=c_WBT[:, :, :],
                           in_=d_WBT.rearrange("(c p) n -> p c n", p=128), inc=s_ld0, k=16)
                        op('y', 'dma_start', out=c_lnwrs[:, :], in_=d_lnwrs[:, :], inc=s_ld0, k=16)
                        op('g', 'dma_start', out=c_lnbrs[:, :], in_=d_lnbrs[:, :], inc=s_ld0, k=16)
                        waW_val = s_ld0.n
                        wait('t', s_ld0, waW_val)
                        for wsrc, ng0 in ((c_WaT, 0), (c_WBT, 2)):
                            for ng in range(2):
                                for dk in range(DK):
                                    op('t', 'matmul', pPJ[:, ng0 + ng, 0:1],
                                       wsrc[:, dk, ng * 128:(ng + 1) * 128], c_lnbrs[:, dk:dk + 1],
                                       start=(dk == 0), stop=(dk == DK - 1),
                                       inc=(s_mm if dk == DK - 1 else None))
                        fold_mm = s_mm.n
                        wait('v', s_mm, fold_mm)
                        drain('v')
                        op('v', 'tensor_tensor', c_Wab[:, 0:1], c_Wab[:, 0:1], pPJ[:, 0, 0:1], ALU.add)
                        op('v', 'tensor_tensor', c_Wab[:, 1:2], c_Wab[:, 1:2], pPJ[:, 1, 0:1], ALU.add)
                        op('v', 'tensor_tensor', wbbe[:, 0:1], c_WBb[:, 0:1], pPJ[:, 2, 0:1], ALU.add)
                        op('v', 'tensor_tensor', wbbe[:, 1:2], c_WBb[:, 1:2], pPJ[:, 3, 0:1], ALU.add)
                        drain('v')
                        for dk in range(DK):
                            op('v', 'tensor_scalar', c_WaT[:, dk, :], c_WaT[:, dk, :],
                               c_lnwrs[:, dk:dk + 1], None, ALU.mult)
                            op('v', 'tensor_scalar', c_WBT[:, dk, :], c_WBT[:, dk, :],
                               c_lnwrs[:, dk:dk + 1], None, ALU.mult,
                               inc=(s_dve if dk == DK - 1 else None))
                        fold_done = s_dve.n
                        drain('v')
                        wait('t', s_dve, fold_done)
                        op('t', 'matmul', pFF[:, 0, 0:1], c_C[:, 0, :], wbbe[:, 0:1], start=True, stop=False)
                        op('t', 'matmul', pFF[:, 0, 0:1], c_C[:, 1, :], wbbe[:, 1:2], start=False, stop=True)
                        op('t', 'matmul', pFF[:, 1, 0:1], c_S[:, 0, :], wbbe[:, 0:1], start=True, stop=False)
                        op('t', 'matmul', pFF[:, 1, 0:1], c_S[:, 1, :], wbbe[:, 1:2], start=False, stop=True)
                        op('t', 'matmul', pPJ[0:1, 0, 0:1], c_c128[:, 0:1], wbbe[:, 0:1], start=True, stop=False)
                        op('t', 'matmul', pPJ[0:1, 0, 0:1], c_c128[:, 1:2], wbbe[:, 1:2], start=False, stop=True,
                           inc=s_mm)
                        wait('v', s_mm, s_mm.n)
                        drain('v')
                        op('v', 'tensor_copy', fbre[:, :], pFF[:, 0, 0:1])
                        op('v', 'tensor_scalar', fbim[:, :], pFF[:, 1, 0:1], -1.0, None, ALU.mult)
                        op('v', 'tensor_copy', fb128[:, :], pPJ[0:1, 0, 0:1], inc=s_dve)
                        fold_done = s_dve.n
                        wait('s', s_dve, fold_done)
                        wait('t', s_dve, fold_done)
                        x1ring = f1b("x1ringF", [128, 2, D], F32)
                        zfr = f1b("zfrF", [128, 2, D], F32)
                        xntr = f1b("xntrF", [128, 2, D], BF16)
                        xfa = [f1b(f"xfa{j}", [128, DK, 128], BF16) for j in range(2)]
                        abf = f1b("abf", [128, 2, 128], BF16)
                        ubf = f1b("ubf", [128, 2, 128], BF16)
                        tanb = f1b("tanb", [128, 2, 128], BF16)
                        g1b = f1b("g1b", [128, 128], F32)
                        g2b = f1b("g2b", [128, 128], F32)
                        ggb = g1b
                        bn6 = f1b("bn6B", [128, 12], F32)
                        bn6b = bn6
                        bn2r = f1b("bn2rB", [128, 2, 2], F32)
                        bn2br = f1b("bn2brB", [128, 2, 2], F32)
                        sdq = f1b("sdqB", [128, 4, 1], F32)
                        sdvr = sdq[:, 0:2, :]
                        sdv2r = sdq[:, 2:4, :]
                        rstd2 = f1b("rstd2B", [128, 1], F32)
                        row3 = f1b("row3", [1, 384], F32)
                        ab128 = row3[0:1, 0:128]
                        gg128 = row3[0:1, 128:256]
                        raw128 = row3[0:1, 256:384]
                        ggv = [0, 0, 0]

                        rZF = [Res(), Res()]
                        rXN4 = [Res() for _ in range(4)]
                        rFA = [Res(), Res()]
                        rB2 = [Res(), Res()]
                        rB2b = [Res(), Res()]
                        rSD = [Res(), Res()]
                        rSD2 = [Res(), Res()]
                        rTAN = [Res(), Res()]

                        sq1v = [0] * NTILE
                        sq2v = [0] * NTILE
                        xnt_done = [0] * NTILE
                        aval = [None] * NTILE
                        uval = [None] * NTILE

                        def A1(i):
                            xs = i % 2
                            em.wait_res('y', rX1[xs])
                            gsem, gval = x1_events[i]
                            wait('y', gsem, gval)
                            op('y', 'dma_start', out=x1ring[:, xs, :],
                               in_=x1s[i * 128:(i + 1) * 128, :], inc=l1[xs], k=16)
                            wait('v', l1[xs], l1[xs].n)
                            drain('v')
                            op('v', 'bn_stats', bn6[:, 0:6], x1ring[:, xs, 0:512])
                            op('v', 'bn_stats', bn6[:, 6:12], x1ring[:, xs, 512:1024])
                            drain('v')
                            em.wait_res('v', rB2[xs])
                            op('v', 'bn_aggr', bn2r[:, xs, :], bn6[:, :], inc=s_dve)
                            wait('s', s_dve, s_dve.n)
                            em.wait_res('s', rSD[xs])
                            op('s', 'activation', sdvr[:, xs, :], bn2r[:, xs, 1:2], AF.Sqrt,
                               bias=c_eps[:, 0:1], inc=s_act)
                            sq1v[i] = s_act.n

                        def A2(i):
                            xs = i % 2
                            zs = i % 2
                            wait('v', s_act, sq1v[i])
                            op('v', 'reciprocal', bnr[:, i:i + 1], sdvr[:, xs, :])
                            drain('v')
                            op('v', 'tensor_copy', bnm[:, i:i + 1], bn2r[:, xs, 0:1], inc=s_dve)
                            rSD[xs].ready(s_dve)
                            rB2[xs].ready(s_dve)
                            em.wait_res('v', rZF[zs])
                            drain('v')
                            op('v', 'tensor_scalar', zfr[:, zs, :], x1ring[:, xs, :],
                               bnm[:, i:i + 1], bnr[:, i:i + 1], ALU.subtract, ALU.mult, inc=s_dve)
                            rX1[xs].ready(s_dve)
                            drain('v')
                            op('v', 'bn_stats', bn6b[:, 0:6], zfr[:, zs, 0:512])
                            op('v', 'bn_stats', bn6b[:, 6:12], zfr[:, zs, 512:1024])
                            drain('v')
                            em.wait_res('v', rB2b[xs])
                            op('v', 'bn_aggr', bn2br[:, xs, :], bn6b[:, :], inc=s_dve)
                            wait('s', s_dve, s_dve.n)
                            em.wait_res('s', rSD2[xs])
                            op('s', 'activation', sdv2r[:, xs, :], bn2br[:, xs, 1:2], AF.Sqrt,
                               bias=c_eps[:, 0:1], inc=s_act)
                            sq2v[i] = s_act.n

                        def A3(i):
                            xs = i % 2
                            zs = i % 2
                            ns = i % 2
                            wait('v', s_act, sq2v[i])
                            op('v', 'reciprocal', rstd2[:, :], sdv2r[:, xs, :], inc=s_dve)
                            rSD2[xs].ready(s_dve)
                            drain('v')
                            em.wait_res('v', rXN4[ns])
                            op('v', 'tensor_scalar', xntr[:, ns, :], zfr[:, zs, :],
                               bn2br[:, xs, 0:1], rstd2[:, 0:1], ALU.subtract, ALU.mult, inc=s_dve)
                            rB2b[xs].ready(s_dve)
                            rZF[zs].ready(s_dve)
                            xnt_done[i] = s_dve.n

                        def B1b1(i):
                            ns = i % 2
                            fa = i % 2
                            wait('t', s_dve, xnt_done[i])
                            em.wait_res('t', rTR)
                            for dk in range(DK):
                                op('t', 'transpose', pTR[:, dk, :],
                                   xntr[:, ns, dk * 128:(dk + 1) * 128], c_identb[:, :],
                                   inc=(s_mm if dk == DK - 1 else None))
                            wait('v', s_mm, s_mm.n)
                            em.wait_res('v', rFA[fa])
                            op('v', 'tensor_copy', xfa[fa][:, :, :], pTR[:, :, :], inc=s_dve)
                            rTR.ready(s_dve)
                            rXN4[ns].ready(s_dve)
                            fcv = s_dve.n
                            wait('t', s_dve, fcv)
                            av = []
                            for ng in range(2):
                                em.wait_res('t', rPJ[ng])
                                for dk in range(DK):
                                    op('t', 'matmul', pPJ[:, ng, 0:128],
                                       c_WaT[:, dk, ng * 128:(ng + 1) * 128], xfa[fa][:, dk, :],
                                       start=(dk == 0), stop=(dk == DK - 1),
                                       inc=(s_mm if dk == DK - 1 else None))
                                av.append(s_mm.n)
                            uv = []
                            for ng in range(2):
                                em.wait_res('t', rPJ[2 + ng])
                                for dk in range(DK):
                                    op('t', 'matmul', pPJ[:, 2 + ng, 0:128],
                                       c_WBT[:, dk, ng * 128:(ng + 1) * 128], xfa[fa][:, dk, :],
                                       start=(dk == 0), stop=(dk == DK - 1),
                                       inc=(s_mm if dk == DK - 1 else None))
                                uv.append(s_mm.n)
                            aval[i] = av
                            uval[i] = uv
                            rFA[fa].ready(s_mm, s_mm.n)

                        def B1b2(i):
                            toff = i * 128
                            for ng in range(2):
                                wait('s', s_mm, aval[i][ng])
                                em.wait_res('s', rTAN[ng])
                                op('s', 'activation', tanb[:, ng, :], pPJ[:, ng, 0:128], AF.Tanh,
                                   bias=c_Wab[:, ng:ng + 1], scale=1.0, inc=s_act)
                            tanv = s_act.n
                            wait('v', s_act, tanv)
                            op('v', 'tensor_scalar', abf[:, 0, :], tanb[:, 0, :],
                               c_gamma[:, 0:1], None, ALU.mult)
                            op('v', 'tensor_scalar', abf[:, 1, :], tanb[:, 1, :],
                               c_gamma[:, 0:1], None, ALU.mult, inc=s_dve)
                            rTAN[0].ready(s_dve)
                            rTAN[1].ready(s_dve)
                            rPJ[1].ready(s_dve)
                            abf_done = s_dve.n
                            wait('v', s_mm, uval[i][1])
                            drain('v')
                            op('v', 'tensor_copy', ubf[:, 0, :], pPJ[:, 2, 0:128])
                            op('v', 'tensor_copy', ubf[:, 1, :], pPJ[:, 3, 0:128], inc=s_dve)
                            rPJ[3].ready(s_dve)
                            ubf_done = s_dve.n
                            em.wait_res('t', rFF[0])
                            em.wait_res('t', rFF[1])
                            wait('t', s_dve, abf_done)
                            op('t', 'matmul', pFF[:, 0, 0:128], c_C[:, 0, :], abf[:, 0, :], start=True, stop=False)
                            op('t', 'matmul', pFF[:, 0, 0:128], c_C[:, 1, :], abf[:, 1, :], start=False, stop=True)
                            op('t', 'matmul', pFF[:, 1, 0:128], c_S[:, 0, :], abf[:, 0, :], start=True, stop=False)
                            op('t', 'matmul', pFF[:, 1, 0:128], c_S[:, 1, :], abf[:, 1, :], start=False, stop=True)
                            op('t', 'matmul', pPJ[0:1, 0, 0:128], c_c128[:, 0:1], abf[:, 0, :], start=True, stop=False)
                            op('t', 'matmul', pPJ[0:1, 0, 0:128], c_c128[:, 1:2], abf[:, 1, :], start=False, stop=True,
                               inc=s_mm)
                            ffta = s_mm.n
                            wait('s', s_mm, ffta)
                            if ggv[0]:
                                wait('s', s_dve, ggv[0])
                            op('s', 'activation', g1b[:, :], pFF[:, 0, 0:128], AF.Square)
                            op('s', 'activation', g2b[:, :], pFF[:, 1, 0:128], AF.Square, inc=s_act)
                            wait('v', s_act, s_act.n)
                            op('v', 'tensor_tensor', g1b[:, :], g1b[:, :], g2b[:, :], ALU.add, inc=s_dve)
                            wait('s', s_dve, s_dve.n)
                            op('s', 'activation', g2b[:, :], g1b[:, :], AF.Sqrt, inc=s_act)
                            wait('v', s_act, s_act.n)
                            op('v', 'tensor_scalar', g2b[:, :], g2b[:, :], 1e-8, None, ALU.max, inc=s_dve)
                            ggv[0] = s_dve.n
                            wait('s', s_dve, s_dve.n)
                            if ggv[2]:
                                wait('s', s_dve, ggv[2])
                            op('s', 'activation', ggb[:, :], g2b[:, :], AF.Sigmoid, bias=c_two[:, 0:1], scale=-2.0,
                               inc=s_act)
                            ggend = s_act.n
                            wait('v', s_act, ggend)
                            op('v', 'tensor_tensor', sc_ar[:, toff:toff + 128], pFF[:, 0, 0:128],
                               ggb[:, :], ALU.mult)
                            op('v', 'scalar_tensor_tensor', sc_ai[:, toff:toff + 128], pFF[:, 1, 0:128],
                               -1.0, ggb[:, :], ALU.mult, ALU.mult, inc=s_dve)
                            ggv[2] = s_dve.n
                            rFF[0].ready(s_dve)
                            rFF[1].ready(s_dve)
                            wait('s', s_mm, ffta)
                            if ggv[1]:
                                wait('s', s_dve, ggv[1])
                            op('s', 'activation', ab128, pPJ[0:1, 0, 0:128], AF.Abs)
                            op('s', 'activation', raw128, pPJ[0:1, 0, 0:128], AF.Identity, inc=s_act)
                            wait('v', s_act, s_act.n)
                            op('v', 'tensor_scalar', ab128, ab128, 1e-8, None, ALU.max, inc=s_dve)
                            wait('s', s_dve, s_dve.n)
                            op('s', 'activation', gg128, ab128, AF.Sigmoid, bias=c_two[0:1, 0:1], scale=-2.0,
                               inc=s_act)
                            wait('v', s_act, s_act.n)
                            op('v', 'tensor_tensor', a128b[0:1, toff:toff + 128], raw128,
                               gg128, ALU.mult, inc=s_dve)
                            ggv[1] = s_dve.n
                            rPJ[0].ready(s_dve)
                            em.wait_res('t', rFF[0])
                            em.wait_res('t', rFF[1])
                            wait('t', s_dve, ubf_done)
                            op('t', 'matmul', pFF[:, 0, 0:128], c_C[:, 0, :], ubf[:, 0, :], start=True, stop=False)
                            op('t', 'matmul', pFF[:, 0, 0:128], c_C[:, 1, :], ubf[:, 1, :], start=False, stop=True)
                            op('t', 'matmul', pFF[:, 1, 0:128], c_S[:, 0, :], ubf[:, 0, :], start=True, stop=False)
                            op('t', 'matmul', pFF[:, 1, 0:128], c_S[:, 1, :], ubf[:, 1, :], start=False, stop=True)
                            op('t', 'matmul', pPJ[0:1, 2, 0:128], c_c128[:, 0:1], ubf[:, 0, :], start=True, stop=False)
                            op('t', 'matmul', pPJ[0:1, 2, 0:128], c_c128[:, 1:2], ubf[:, 1, :], start=False, stop=True,
                               inc=s_mm)
                            fftu = s_mm.n
                            wait('s', s_mm, fftu)
                            op('s', 'activation', sc_ur[:, toff:toff + 128], pFF[:, 0, 0:128],
                               AF.Identity, bias=fbre[:, 0:1], scale=1.0)
                            op('s', 'activation', sc_ui[:, toff:toff + 128], pFF[:, 1, 0:128],
                               AF.Identity, bias=fbim[:, 0:1], scale=-1.0)
                            op('s', 'activation', u128b[0:1, toff:toff + 128], pPJ[0:1, 2, 0:128],
                               AF.Identity, bias=fb128[0:1, 0:1], scale=1.0, inc=s_act)
                            rFF[0].ready(s_act)
                            rFF[1].ready(s_act)
                            rPJ[2].ready(s_act)

                        for r in range(NTILE + 4):
                            if r < NTILE:
                                A1(r)
                            if 1 <= r < NTILE + 1:
                                A2(r - 1)
                            if 2 <= r < NTILE + 2:
                                A3(r - 2)
                            if 4 <= r:
                                B1b2(r - 4)
                                if (r - 4) % 4 == 3:
                                    cc_ = (r - 4) // 4
                                    chunk_rdy[cc_] = (s_dve.n, s_act.n)
                                    if cc_ < 4:
                                        scan_chunks('g', s_gp, pngG, tmpG, [cc_], 0)
                            if 3 <= r < NTILE + 3:
                                B1b1(r - 3)
                        front_done = s_dve.n
                        front_act = s_act.n
                        front_gp = s_gp.n
                        front_mm = s_mm.n

                    with contextlib.ExitStack() as f2, \
                         nc.psum_tensor("pYB", [128, 3, 512], F32) as pY:
                        def f2b(name, shape, dt):
                            return f2.enter_context(nc.sbuf_tensor(name, shape, dt))
                        scr128 = f2b("scr128", [1, CH], F32)
                        Fre = f2b("Fre", [128, 1], F32)
                        Fim = f2b("Fim", [128, 1], F32)
                        nFre = f2b("nFre", [128, 1], F32)
                        nFim = f2b("nFim", [128, 1], F32)
                        tc4 = [f2b(f"tc{q}", [128, 1], F32) for q in range(4)]
                        ccre = [f2b(f"ccre{q}", [128, 1], F32) for q in range(2)]
                        ccim = [f2b(f"ccim{q}", [128, 1], F32) for q in range(2)]
                        ncim = f2b("ncim", [128, 1], F32)
                        f2a = f2.enter_context(contextlib.ExitStack())
                        def f2ab(name, shape, dt):
                            return f2a.enter_context(nc.sbuf_tensor(name, shape, dt))
                        pngV = [[f2ab(f"pgv{j}{q}", [128, PAD + CH], F32) for q in range(4)] for j in range(2)]
                        tmpV = [f2ab(f"tmv{q}", [128, CH], F32) for q in range(2)]

                        wait('v', s_dve, front_done)
                        wait('v', s_act, front_act)
                        wait('v', s_mm, front_mm)
                        scan_chunks('v', s_dve, pngV, tmpV, [4, 5, 6, 7], PAD)
                        f2a.close()
                        agrow = f2b("agrow", [1, 512], F32)
                        gath = f2b("gath", [world, 512], F32)
                        selrow = f2b("selrow", [1, 512], F32)
                        Mrow = f2b("Mrow3", [1, D], BF16)
                        x1ring3 = f2b("x1ring3", [128, 2, D], F32)
                        obuf = f2b("obufB", [128, 2, D], F32)
                        zt = f2b("ztB", [128, 512], F32)
                        zf = f2b("zfB", [128, 2, D], F32)
                        wait('y', s_gp, s_gp.n)
                        wait('y', s_dve, s_dve.n)
                        op('y', 'dma_start', out=Mrow[0:1, :], in_=mrow_scr[:, :], inc=s_ld0, k=16)
                        wait('t', s_ld0, s_ld0.n)
                        wait('v', s_gp, front_gp)
                        wait('v', s_act, front_act)
                        for c in range(NCHK):
                            cs = slice(c * CH, (c + 1) * CH)
                            init = 0.0 if c == 0 else r128[0][0:1, 0:1]
                            op('v', 'tensor_tensor_scan', scr128[0:1, :], a128b[0:1, cs], u128b[0:1, cs],
                               init, ALU.mult, ALU.add)
                            drain('v')
                            op('v', 'tensor_copy', r128[0][0:1, :], scr128[0:1, CH - 1:CH])
                            drain('v')
                        wait('v', s_gp, u_fin[3])
                        op('v', 'tensor_copy', Fre[:, :], sc_ur[:, CH - 1:CH])
                        op('v', 'tensor_copy', Fim[:, :], sc_ui[:, CH - 1:CH])
                        drain('v')
                        for c in range(1, NCHK):
                            le = (c + 1) * CH - 1
                            ALr = sc_ar[:, le:le + 1]
                            ALi = sc_ai[:, le:le + 1]
                            hLr = sc_ur[:, le:le + 1]
                            hLi = sc_ui[:, le:le + 1]
                            op('v', 'tensor_tensor', tc4[0][:, :], ALr, Fre[:, :], ALU.mult)
                            op('v', 'tensor_tensor', tc4[1][:, :], ALi, Fim[:, :], ALU.mult)
                            op('v', 'tensor_tensor', tc4[2][:, :], ALr, Fim[:, :], ALU.mult)
                            op('v', 'tensor_tensor', tc4[3][:, :], ALi, Fre[:, :], ALU.mult)
                            drain('v')
                            op('v', 'tensor_tensor', nFre[:, :], tc4[0][:, :], tc4[1][:, :], ALU.subtract)
                            op('v', 'tensor_tensor', nFim[:, :], tc4[2][:, :], tc4[3][:, :], ALU.add)
                            drain('v')
                            op('v', 'tensor_tensor', Fre[:, :], nFre[:, :], hLr, ALU.add)
                            op('v', 'tensor_tensor', Fim[:, :], nFim[:, :], hLi, ALU.add)
                            drain('v')
                        op('v', 'memset', agrow[0:1, :], 0.0, inc=s_dve)
                        fcol_done = s_dve.n
                        wait('t', s_dve, fcol_done)
                        em.wait_res('t', rPJ[0])
                        op('t', 'transpose', pPJ[0:1, 0, 0:128], Fre[:, :], c_identf[:, :], inc=s_mm)
                        wait('v', s_mm, s_mm.n)
                        drain('v')
                        op('v', 'tensor_copy', agrow[0:1, 0:128], pPJ[0:1, 0, 0:128], inc=s_dve)
                        rPJ[0].ready(s_dve)
                        em.wait_res('t', rPJ[1])
                        op('t', 'transpose', pPJ[0:1, 1, 0:128], Fim[:, :], c_identf[:, :], inc=s_mm)
                        wait('v', s_mm, s_mm.n)
                        drain('v')
                        op('v', 'tensor_copy', agrow[0:1, 128:256], pPJ[0:1, 1, 0:128], inc=s_dve)
                        rPJ[1].ready(s_dve)
                        op('v', 'tensor_copy', agrow[0:1, 256:257], r128[0][0:1, 0:1], inc=s_dve)
                        ag_ready = s_dve.n
                        wait('g', s_dve, ag_ready)
                        op('g', 'dma_start', out=cc_in[:, :], in_=agrow[0:1, :], inc=s_wl0, k=16)
                        wait('g', s_wl0, s_wl0.n)
                        op('g', 'collective_compute', "AllGather", ALU.bypass,
                           replica_groups=[list(range(world))],
                           ins=[cc_in.ap().opt()], outs=[cc_out.ap().opt()], inc=s_cc)
                        wait('g', s_cc, s_cc.n)
                        op('g', 'dma_start', out=gath[:, :], in_=cc_out[:, :], inc=s_wl0, k=16)
                        gath_val = s_wl0.n
                        wait('t', s_wl0, gath_val)
                        em.wait_res('t', rPJ[0])
                        op('t', 'matmul', pPJ[0:1, 0, :], c_sel[:, :], gath[:, :], start=True, stop=True,
                           inc=s_mm)
                        wait('v', s_mm, s_mm.n)
                        drain('v')
                        op('v', 'tensor_copy', selrow[0:1, :], pPJ[0:1, 0, :], inc=s_dve)
                        rPJ[0].ready(s_dve)
                        wait('t', s_dve, s_dve.n)
                        em.wait_res('t', rPJ[0])
                        op('t', 'transpose', pPJ[:, 0, 0:1], selrow[0:1, 0:128], c_identf[0:1, 0:1], inc=s_mm)
                        em.wait_res('t', rPJ[1])
                        op('t', 'transpose', pPJ[:, 1, 0:1], selrow[0:1, 128:256], c_identf[0:1, 0:1], inc=s_mm)
                        wait('v', s_mm, s_mm.n)
                        drain('v')
                        op('v', 'tensor_copy', c0re[:, :], pPJ[:, 0, 0:1])
                        op('v', 'tensor_copy', c0im[:, :], pPJ[:, 1, 0:1])
                        op('v', 'tensor_copy', c128c[0:1, :], selrow[0:1, 256:257], inc=s_dve)
                        rPJ[0].ready(s_dve)
                        rPJ[1].ready(s_dve)
                        drain('v')
                        rY3 = [Res(), Res(), Res()]
                        yv = [None] * NTILE
                        ysl = [None] * NTILE
                        ctr3 = {'y': 0}

                        def stP(i, gate):
                            toff = i * 128
                            xs = i % 2
                            em.wait_res('y', rX1[xs])
                            op('y', 'dma_start', out=x1ring3[:, xs, :], in_=x1s[toff:toff + 128, :],
                               inc=l1[xs], k=16)
                            wait('v', l1[xs], l1[xs].n)
                            em.wait_res('v', rZ[xs])
                            drain('v')
                            op('v', 'tensor_scalar', zf[:, xs, :], x1ring3[:, xs, :],
                               bnm[:, i:i + 1], bnr[:, i:i + 1], ALU.subtract, ALU.mult, inc=s_dve)
                            vals = []
                            slots = []
                            wait('t', s_act, gate)
                            for dt_ in range(2):
                                dsl = slice(dt_ * 512, (dt_ + 1) * 512)
                                yp = ctr3['y'] % 3
                                ctr3['y'] += 1
                                em.wait_res('t', rY3[yp])
                                op('t', 'matmul', pY[:, yp, :], hrebf[:, toff:toff + 128],
                                   M2reT[:, dsl], start=True, stop=False)
                                op('t', 'matmul', pY[:, yp, :], himbf[:, toff:toff + 128],
                                   M2imT[:, dsl], start=False, stop=False)
                                op('t', 'matmul', pY[:, yp, :], h128bf[0:1, toff:toff + 128],
                                   Mrow[0:1, dsl], start=False, stop=True, inc=s_mm)
                                vals.append(s_mm.n)
                                slots.append(yp)
                            yv[i] = vals
                            ysl[i] = slots

                        def stQ(i):
                            toff = i * 128
                            xs = i % 2
                            os_ = i % 2
                            for dt_ in range(2):
                                dsl = slice(dt_ * 512, (dt_ + 1) * 512)
                                yp = ysl[i][dt_]
                                wait('v', s_mm, yv[i][dt_])
                                if dt_ == 0:
                                    em.wait_res('v', rO[os_])
                                drain('v')
                                op('v', 'tensor_tensor', zt[:, :], zf[:, xs, dsl], c_Dsk[:, dsl],
                                   ALU.mult)
                                op('v', 'tensor_tensor', obuf[:, os_, dsl], pY[:, yp, :], c_WCb[:, dsl],
                                   ALU.add)
                                drain('v')
                                op('v', 'tensor_tensor', obuf[:, os_, dsl], obuf[:, os_, dsl], zt[:, :],
                                   ALU.add)
                                drain('v')
                                op('v', 'tensor_tensor', obuf[:, os_, dsl], obuf[:, os_, dsl],
                                   x1ring3[:, xs, dsl], ALU.add, inc=s_dve)
                                rY3[yp].ready(s_dve)
                            rX1[xs].ready(s_dve)
                            rZ[xs].ready(s_dve)
                            wait('y', s_dve, s_dve.n)
                            op('y', 'dma_start', out=dst[toff:toff + 128, :], in_=obuf[:, os_, :],
                               inc=so[os_], k=16)
                            store_events.append((so[os_], so[os_].n))
                            rO[os_].ready(so[os_])

                        prev_tile = None
                        for c in range(NCHK):
                            cs = slice(c * CH, (c + 1) * CH)
                            cre = c0re if c == 0 else ccre[(c - 1) % 2]
                            cim = c0im if c == 0 else ccim[(c - 1) % 2]
                            op('v', 'tensor_scalar', ncim[:, :], cim[:, :], -1.0, None, ALU.mult)
                            drain('v')
                            op('v', 'scalar_tensor_tensor', sc_ur[:, cs], sc_ar[:, cs], cre[:, 0:1],
                               sc_ur[:, cs], ALU.mult, ALU.add)
                            op('v', 'scalar_tensor_tensor', sc_ui[:, cs], sc_ar[:, cs], cim[:, 0:1],
                               sc_ui[:, cs], ALU.mult, ALU.add)
                            drain('v')
                            op('v', 'scalar_tensor_tensor', sc_ur[:, cs], sc_ai[:, cs], ncim[:, 0:1],
                               sc_ur[:, cs], ALU.mult, ALU.add)
                            op('v', 'scalar_tensor_tensor', sc_ui[:, cs], sc_ai[:, cs], cre[:, 0:1],
                               sc_ui[:, cs], ALU.mult, ALU.add)
                            drain('v')
                            le = (c + 1) * CH - 1
                            op('v', 'tensor_copy', ccre[c % 2][:, :], sc_ur[:, le:le + 1])
                            op('v', 'tensor_copy', ccim[c % 2][:, :], sc_ui[:, le:le + 1], inc=s_dve)
                            drain('v')
                            casc_v = s_dve.n
                            wait('s', s_dve, casc_v)
                            op('s', 'copy', hrebf[:, cs], sc_ur[:, cs])
                            op('s', 'copy', himbf[:, cs], sc_ui[:, cs], inc=s_act)
                            init = c128c[0:1, 0:1] if c == 0 else r128[1][0:1, 0:1]
                            em.wait_res('v', rSCR)
                            op('v', 'tensor_tensor_scan', scr128[0:1, :], a128b[0:1, cs], u128b[0:1, cs],
                               init, ALU.mult, ALU.add)
                            drain('v')
                            op('v', 'tensor_copy', r128[1][0:1, :], scr128[0:1, CH - 1:CH], inc=s_dve)
                            drain('v')
                            wait('s', s_dve, s_dve.n)
                            op('s', 'copy', h128bf[0:1, cs], scr128[0:1, :], inc=s_act)
                            rSCR.ready(s_act)
                            hbf_c = s_act.n
                            for k in range(4):
                                i = c * 4 + k
                                if prev_tile is not None:
                                    stQ(prev_tile)
                                stP(i, hbf_c)
                                prev_tile = i
                        stQ(prev_tile)
                        em.run_block()
            return store_events

        def so_totals():
            return ((so[0], so[0].n), (so[1], so[1].n))

        if phases == 1:
            emit_ffn(x_in, y_out, "A")
        elif phases == 2:
            ev1 = emit_ffn(x_in, x1s, "A")
            emit_attn(y_out, ev1, entry=so_totals())
        else:
            ev1 = emit_ffn(x_in, x1s, "A")
            ev2 = emit_attn(x2s, ev1, entry=so_totals())
            emit_ffn(x2s, y_out, "C", src_events=ev2, entry=so_totals())

        fin = Emit(nc)
        fin.wait('y', so[0], so[0].n)
        fin.wait('y', so[1], so[1].n)
        fin.run_block()

    return nc


def _prep_inputs(L0_attn_Wa_w, L0_attn_Wa_b, L0_attn_log_gamma, L0_attn_WB_w, L0_attn_WB_b,
                 L0_attn_WC_w, L0_attn_WC_b, L0_attn_D_skip, L0_attn_ln_w, L0_attn_ln_b,
                 L0_ffn_router_w, L0_ffn_sg, L0_ffn_su, L0_ffn_sd, L0_ffn_eg, L0_ffn_eu,
                 L0_ffn_ed):
    f32 = np.float32
    ac = np.ascontiguousarray
    n = np.arange(NST)
    ang = 2.0 * np.pi * np.outer(n, n) / NST
    Cfull = np.cos(ang)
    Sfull = np.sin(ang)
    rep = lambda v: ac(np.broadcast_to(np.asarray(v, f32)[None, :], (128, D)))
    return dict(
        rtW=ac(np.asarray(L0_ffn_router_w, f32).T),
        sgT=ac(np.asarray(L0_ffn_sg, f32).T),
        suT=ac(np.asarray(L0_ffn_su, f32).T),
        sdT=ac(np.asarray(L0_ffn_sd, f32).T),
        egT=ac(np.asarray(L0_ffn_eg, f32).transpose(0, 2, 1)),
        euT=ac(np.asarray(L0_ffn_eu, f32).transpose(0, 2, 1)),
        edT=ac(np.asarray(L0_ffn_ed, f32).transpose(0, 2, 1)),
        WaT=ac(np.asarray(L0_attn_Wa_w, f32).T),
        WBT=ac(np.asarray(L0_attn_WB_w, f32).T),
        WCT=ac(np.asarray(L0_attn_WC_w, f32).T),
        Wab=ac(np.asarray(L0_attn_Wa_b, f32).reshape(2, 128).T),
        WBb=ac(np.asarray(L0_attn_WB_b, f32).reshape(2, 128).T),
        lgam=np.asarray(L0_attn_log_gamma, f32).reshape(1, 1),
        Ctab=ac(Cfull[:, 0:KH].astype(f32)),
        Stab=ac(Sfull[:, 0:KH].astype(f32)),
        c128=ac(Cfull[:, KH].astype(f32).reshape(2, 128).T),
        ident=np.eye(128, dtype=f32),
        lnwrs=ac(np.asarray(L0_attn_ln_w, f32).reshape(DK, 128).T),
        lnbrs=ac(np.asarray(L0_attn_ln_b, f32).reshape(DK, 128).T),
        WCb_rep=rep(L0_attn_WC_b),
        Dsk_rep=rep(L0_attn_D_skip),
    )


def kernel(L0_attn_x, **kw):
    f32 = np.float32
    TLOC = 4096
    xflat = np.ascontiguousarray(np.asarray(L0_attn_x, f32).reshape(-1, D))
    common = _prep_inputs(**kw)
    nc = build_program(TLOC=TLOC, phases=3)
    in_maps = []
    for c in range(8):
        sel = np.zeros((8, 1), f32)
        if c % 2 == 1:
            sel[c - 1, 0] = 1.0
        m = dict(common)
        m["x"] = xflat[c * TLOC:(c + 1) * TLOC]
        m["sel"] = sel
        in_maps.append(m)
    res = run_bass_kernel_spmd(nc, in_maps, core_ids=list(range(8)))
    out = np.concatenate([res.results[c]["y"] for c in range(8)], axis=0)
    return out.reshape(L0_attn_x.shape).astype(f32)
